# version 01
# baseline (speedup 1.0000x reference)
import numpy as np
import ml_dtypes
from contextlib import ExitStack

import concourse.bass as bass
import concourse.mybir as mybir
from concourse.bass_utils import run_bass_kernel_spmd

F32 = mybir.dt.float32
BF16 = mybir.dt.bfloat16
AF = mybir.ActivationFunctionType
ALU = mybir.AluOpType
AX = mybir.AxisListType.X

L = 4096
DM = 1024
DH = 1024
NT = 8
NG = 8
GS = 8
TC = 512
NTC = L // TC
SEG = TC + 2
CAT = GS * SEG
IH = 1536
NKI = IH // 128
RND = 12582912.0
QMAX = 32767.0

STAGE = 6
DBG = ()

ENGINES = ("sync", "vector", "scalar", "tensor", "gpsimd")
PAIRS = [[0, 1], [2, 3], [4, 5], [6, 7]]
ALL8 = [list(range(8))]


class Sched:
    def __init__(self, nc, ctx):
        self.nc = nc
        self.ops = {e: [] for e in ENGINES}
        self.psem = {e: ctx.enter_context(nc.semaphore(f"sp_{e}")) for e in ENGINES}
        self.pcnt = {e: 0 for e in ENGINES}
        self.dsems = {}
        self.ctx = ctx

    def _dsem(self, name):
        name = name.replace("L0", "").replace("L1", "")
        if name not in self.dsems:
            h = self.ctx.enter_context(self.nc.semaphore(f"sd_{name}"))
            self.dsems[name] = [h, 0]
        return self.dsems[name]

    def emit(self, eng, fn, waits=(), inc=False, dma=None, dmainc=16):
        incs = []
        if inc:
            self.pcnt[eng] += 1
            incs.append((self.psem[eng], 1))
        if dma is not None:
            s = self._dsem(dma)
            s[1] += dmainc
            incs.append((s[0], dmainc))
        ws = tuple((h, v) for (h, v) in waits if v > 0)
        self.ops[eng].append((fn, ws, tuple(incs)))

    def mark(self, eng):
        return (self.psem[eng], self.pcnt[eng])

    def gap(self, eng, n=6):
        for _ in range(n):
            self.emit(eng, lambda e: e.engine_nop())

    def dmark(self, name):
        s = self._dsem(name)
        return (s[0], s[1])

    def finalize(self, block):
        def mk(name):
            def body(eng):
                for fn, waits, incs in self.ops[name]:
                    for (h, v) in waits:
                        eng.wait_ge(h, v)
                    ins = fn(eng)
                    for (h, v) in incs:
                        ins = ins.then_inc(h, v)
            return body
        block.sync(mk("sync"))
        block.vector(mk("vector"))
        block.scalar(mk("scalar"))
        block.tensor(mk("tensor"))
        block.gpsimd(mk("gpsimd"))


def seg3(t, c0, c1):
    return t.ap().rearrange("p (s c) -> p s c", c=SEG)[:, :, c0:c1]


def build_program(dbg=()):
    nc = bass.Bass("TRN2", target_bir_lowering=False, debug=False)

    d_x = nc.dram_tensor("x_tok", [L, DM], F32, kind="ExternalInput").ap()
    d_inw = [nc.dram_tensor(f"inw{i}", [2 * DH, DM], F32, kind="ExternalInput").ap() for i in range(2)]
    d_convw = [nc.dram_tensor(f"convw{i}", [128, NT * 4], F32, kind="ExternalInput").ap() for i in range(2)]
    d_convb = [nc.dram_tensor(f"convb{i}", [128, NT], F32, kind="ExternalInput").ap() for i in range(2)]
    d_xpw = [nc.dram_tensor(f"xpw{i}", [192, DH], F32, kind="ExternalInput").ap() for i in range(2)]
    d_dtw = [nc.dram_tensor(f"dtw{i}", [128, NT * 64], F32, kind="ExternalInput").ap() for i in range(2)]
    d_dtb = [nc.dram_tensor(f"dtb{i}", [128, NT], F32, kind="ExternalInput").ap() for i in range(2)]
    d_alog = [nc.dram_tensor(f"alog{i}", [128, NT * 64], F32, kind="ExternalInput").ap() for i in range(2)]
    d_dskip = [nc.dram_tensor(f"dskip{i}", [128, NT], F32, kind="ExternalInput").ap() for i in range(2)]
    d_outw = [nc.dram_tensor(f"outw{i}", [DM, DH], F32, kind="ExternalInput").ap() for i in range(2)]
    d_wg = nc.dram_tensor("wg", [IH, DM], F32, kind="ExternalInput").ap()
    d_wu = nc.dram_tensor("wu", [IH, DM], F32, kind="ExternalInput").ap()
    d_wd = nc.dram_tensor("wd", [DM, IH], F32, kind="ExternalInput").ap()
    d_eyeb = nc.dram_tensor("eye_b_in", [128, 128], BF16, kind="ExternalInput").ap()
    d_eyef = nc.dram_tensor("eye_f_in", [128, 128], F32, kind="ExternalInput").ap()
    d_out = nc.dram_tensor("out", [L, DM], F32, kind="ExternalOutput").ap()

    ds = lambda n, shp, dt=F32: nc.dram_tensor(n, shp, dt).ap()
    t_xn_tok = ds("s_xn_tok", [L, DM])
    t_xn = ds("s_xn", [DM, L], BF16)
    t_xq = ds("s_xq", [DM, L], BF16)
    t_uraw = ds("s_uraw", [DH, L])
    t_uc = ds("s_uc", [DH, L], BF16)
    t_z = ds("s_z", [DH, L], BF16)
    t_y = ds("s_y", [DH, L])
    t_m2p = ds("s_m2p", [DM, L])
    t_l2in = ds("s_l2in", [DM, L], BF16)
    t_ccsi = nc.dram_tensor("cc_sm_in", [4 * 192, L], F32).ap()
    t_ccso = nc.dram_tensor("cc_sm_out", [4 * 192, L], F32, addr_space="Shared").ap()
    t_ccbi = nc.dram_tensor("cc_big_in", [4 * DM, L], F32).ap()
    t_ccbo = nc.dram_tensor("cc_big_out", [4 * DM, L], F32, addr_space="Shared").ap()
    t_ccx1 = ds("cc_x_in", [1, 4])
    t_ccx2 = nc.dram_tensor("cc_x_out", [1, 4], F32, addr_space="Shared").ap()
    t_grw = ds("s_gate", [IH, L])
    t_upw = ds("s_up", [IH, L])
    t_g2w = ds("s_gate2", [IH, L])
    t_ow = ds("s_o", [IH, L])
    t_Bc = ds("s_Bc", [64, L], BF16)
    t_Cc = ds("s_Cc", [64, L], BF16)
    t_smp = ds("s_smp", [192, L])
    t_smf = ds("s_smf", [192, L])
    t_bgp = ds("s_bgp", [DM, L])
    t_bgf = ds("s_bgf", [DM, L])

    dbg_specs = {"xn": (t_xn, [DM, L], BF16), "uraw": (t_uraw, [DH, L], F32),
                 "uc": (t_uc, [DH, L], BF16), "z": (t_z, [DH, L], BF16),
                 "xdbl": (t_ccso, [192, L], F32), "y": (t_y, [DH, L], F32),
                 "m1": (t_ccbo, [4 * DM, L], F32), "xq": (t_xq, [DM, L], BF16),
                 "gate": (t_grw, [IH, L], F32), "up": (t_upw, [IH, L], F32),
                 "g2": (t_g2w, [IH, L], F32), "o": (t_ow, [IH, L], F32),
                 "m2p": (t_m2p, [DM, L], F32), "xntok": (t_xn_tok, [L, DM], F32)}
    DBG2 = {}
    if "sp" in dbg:
        DBG2["sp"] = nc.dram_tensor("dbg2_sp", [128, NT * TC], F32, kind="ExternalOutput").ap()
        DBG2["q"] = nc.dram_tensor("dbg2_q", [128, NT * TC], BF16, kind="ExternalOutput").ap()
        DBG2["an"] = nc.dram_tensor("dbg2_an", [128, NT * 64], F32, kind="ExternalOutput").ap()
        DBG2["bb"] = nc.dram_tensor("dbg2_bb", [128, CAT], BF16, kind="ExternalOutput").ap()
        DBG2["cb"] = nc.dram_tensor("dbg2_cb", [128, CAT], BF16, kind="ExternalOutput").ap()
    if "ys" in dbg:
        DBG2["ys"] = nc.dram_tensor("dbg2_ys", [128, TC], F32, kind="ExternalOutput").ap()
        DBG2["uc0"] = nc.dram_tensor("dbg2_uc0", [128, TC], BF16, kind="ExternalOutput").ap()
    if "pp" in dbg:
        DBG2["pp"] = nc.dram_tensor("dbg2_pp", [128, TC], F32, kind="ExternalOutput").ap()
    if "hc" in dbg:
        DBG2["hc"] = nc.dram_tensor("dbg2_hc", [128, CAT], BF16, kind="ExternalOutput").ap()
        DBG2["vc"] = nc.dram_tensor("dbg2_vc", [128, CAT], BF16, kind="ExternalOutput").ap()
        DBG2["bc2"] = nc.dram_tensor("dbg2_bc2", [128, CAT], BF16, kind="ExternalOutput").ap()
        DBG2["da"] = nc.dram_tensor("dbg2_da", [128, CAT], F32, kind="ExternalOutput").ap()
    dbg_outs = {}
    for tag in dbg:
        if tag in ("sp", "hc", "ys", "pp"):
            continue
        src, shp, dt = dbg_specs[tag]
        dbg_outs[tag] = (nc.dram_tensor("dbg_" + tag, shp, dt, kind="ExternalOutput").ap(), src, shp, dt)

    with ExitStack() as ctx:
        sbt = lambda n, shp, dt=F32: ctx.enter_context(nc.sbuf_tensor(n, shp, dt))
        pst = lambda n, shp, dt=F32: ctx.enter_context(nc.psum_tensor(n, shp, dt))
        S = Sched(nc, ctx)

        eye_b = sbt("eye_b", [128, 128], BF16)
        eye_f = sbt("eye_f", [128, 128], F32)
        convw = sbt("convw", [128, NT * 4])
        convb = sbt("convb", [128, NT])
        dtb = sbt("dtb", [128, NT])
        dskip = sbt("dskip", [128, NT])
        Aneg = sbt("Aneg", [128, NT * 64])
        carry = sbt("carry", [128, NT * 64], BF16)
        qsc = sbt("qsc", [128, 8])
        sc11 = sbt("sc11", [1, 16])
        cst = sbt("cstc", [128, 4])

        aux = sbt("aux", [128, 1024])
        auxb = sbt("auxb", [128, 1024], BF16)
        acc1 = sbt("acc1", [128, 4])
        st1 = sbt("st1", [128, 4])
        stmaxf = sbt("stmaxf", [128, 128])

        p_mm = [pst(f"p_mm{i}", [128, 512]) for i in range(8)]
        p_tr = [p_mm[6].ap().bitcast(BF16)[:, 0:128], p_mm[7].ap().bitcast(BF16)[:, 0:128]]

        def load_const(dst, src, name, waits=()):
            S.emit("sync", lambda e, d=dst, s=src: e.dma_start(out=d[:, :], in_=s),
                   dma=name, waits=waits)

        def bcast_scalar(dst_col, src_row, name, waits=()):
            S.emit("sync", lambda e, d=dst_col, s=src_row: e.dma_start(
                out=d, in_=s.unsqueeze(1).broadcast_to([1, 128, 1])), dma=name, waits=waits)

        def prep_lhsT(w_dram, rows, name, out_base=0, ktiles=8):
            nm = rows // 128
            for m in range(nm):
                pw = m % 2
                w = [S.mark("vector")] if m == 0 else [prep_marks[-1]]
                S.emit("sync", lambda e, m=m, pw=pw: e.dma_start(
                    out=wld[pw][:, 0:ktiles * 128], in_=w_dram[m * 128:(m + 1) * 128, :]),
                    dma="wp", waits=w)
                S.emit("vector", lambda e, pw=pw: e.tensor_copy(
                    wldb[:, 0:ktiles * 128], wld[pw][:, 0:ktiles * 128]),
                    waits=[S.dmark("wp")], inc=True)
                cast_mark = S.mark("vector")
                for k in range(ktiles):
                    pt = p_tr[k % 2]
                    wtp = [cast_mark]
                    if m * ktiles + k >= 2:
                        wtp.append(tr_evac_marks[-2])
                    S.emit("tensor", lambda e, k=k, pt=pt: e.transpose(
                        pt[:, :], wldb[:, k * 128:(k + 1) * 128], eye_b[:, :]),
                        waits=wtp, inc=True)
                    col = out_base + (k * nm + m) * 128
                    S.emit("vector", lambda e, col=col, pt=pt: e.tensor_copy(
                        wT[:, col:col + 128], pt[:, :]), waits=[S.mark("tensor")], inc=True)
                    tr_evac_marks.append(S.mark("vector"))
                prep_marks.append(S.mark("vector"))
            return lambda k, m: out_base + (k * nm + m) * 128

        tr_evac_marks = []
        prep_marks = []

        def mm_phase(rhs_dram, nk, nm, wt_of, evac, tag, extra_w=(), rhs_ktiles=None):
            kt = rhs_ktiles if rhs_ktiles is not None else nk
            rhs_marks = {}
            grp_marks = {}
            bank_evac = {}
            for n in range(NTC):
                p = n % 2
                w = list(extra_w) if n == 0 else []
                if n >= 2:
                    w.append(grp_marks[(n - 2, nm - 1)])
                for k in range(kt):
                    S.emit("sync", lambda e, n=n, k=k, p=p: e.dma_start(
                        out=rhsb[p][:, k * TC:(k + 1) * TC],
                        in_=rhs_dram[k * 128:(k + 1) * 128, n * TC:(n + 1) * TC]),
                        dma=f"rhs{p}", waits=w if k == 0 else [])
                rhs_marks[n] = S.dmark(f"rhs{p}")
                for m in range(nm):
                    bank = m % 8
                    wm = [rhs_marks[n]]
                    if (n, m) != (0, 0) and (n * nm + m) >= (8 if nm >= 8 else nm):
                        prev = bank_evac.get(bank)
                        if prev is not None:
                            wm.append(prev)
                    for k in range(nk):
                        S.emit("tensor", lambda e, m=m, k=k, bank=bank, p=p: e.matmul(
                            p_mm[bank][:, :], wT[:, wt_of(k, m):wt_of(k, m) + 128],
                            rhsb[p][:, (k % kt) * TC:((k % kt) + 1) * TC],
                            start=(k == 0), stop=(k == nk - 1)),
                            waits=wm if k == 0 else [], inc=(k == nk - 1))
                    grp_marks[(n, m)] = S.mark("tensor")
                    evac(m, n, bank, grp_marks[(n, m)])
                    bank_evac[bank] = S.mark("vector")

        dyn = {}

        def _conds(e, queue):
            key = f"conds_{queue}"
            if key not in dyn:
                b = e.compute_val(e.partition_id() // 2)
                dyn[key] = [e.compute_val(b == p) for p in range(4)]
            return dyn[key]

        def slot_copy(queue, name, dst_of_p, src_of_p, waits=()):
            for p in range(4):
                S.emit(queue, lambda e, p=p, queue=queue: e.dma_start(
                    out=dst_of_p(p), in_=src_of_p(p), cond=_conds(e, queue)[p]),
                    dma=name, waits=waits if p == 0 else [])

        load_const(eye_b, d_eyeb, "cst")
        load_const(eye_f, d_eyef, "cst")
        cst_done = S.dmark("cst")
        S.emit("vector", lambda e: e.memset(acc1[:, :], 0.0), inc=True)
        S.emit("vector", lambda e: e.memset(cst[:, 0:1], 1e-5), inc=True)
        S.emit("vector", lambda e: e.memset(cst[:, 1:2], 1.0 / DM), inc=True)
        S.emit("vector", lambda e: e.memset(cst[:, 2:3], 1.0), inc=True)

        with ExitStack() as c0:
            sb0 = lambda n, shp, dt=F32: c0.enter_context(nc.sbuf_tensor(n, shp, dt))
            xt = [sb0(f"xt{p}", [128, DM]) for p in range(2)]
            xc = sb0("xc", [128, DM])
            sq = aux
            xnb = sb0("xnb", [128, DM], BF16)
            mstat = sb0("mstat", [128, 4])
            stmax = sb0("stmax", [128, 32])
            ntok = L // 128
            tile_done = {}

            def fm_pass(tt, out_dram, name):
                for cblk in range(8):
                    pt = p_tr[cblk % 2]
                    wtp = [S.mark("vector")]
                    if tt == 0 and cblk == 0:
                        wtp.append(cst_done)
                    if cblk >= 2:
                        wtp.append(tr_evac_marks[-2])
                    S.emit("tensor", lambda e, cblk=cblk, pt=pt: e.transpose(
                        pt[:, :], xnb[:, cblk * 128:(cblk + 1) * 128], eye_b[:, :]),
                        waits=wtp, inc=True)
                    wev = [S.mark("tensor")]
                    if cblk == 0:
                        wev.append(S.dmark(name))
                    S.emit("vector", lambda e, cblk=cblk, pt=pt: e.tensor_copy(
                        auxb[:, cblk * 128:(cblk + 1) * 128], pt[:, :]),
                        waits=wev, inc=True)
                    tr_evac_marks.append(S.mark("vector"))
                evm = S.mark("vector")
                for cblk in range(8):
                    S.emit("sync", lambda e, tt=tt, cblk=cblk: e.dma_start(
                        out=out_dram[cblk * 128:(cblk + 1) * 128, tt * 128:(tt + 1) * 128],
                        in_=auxb[:, cblk * 128:(cblk + 1) * 128]),
                        dma=name, waits=[evm] if cblk == 0 else [])

            for tt in range(ntok):
                p = tt % 2
                w = [tile_done[tt - 2]] if tt >= 2 else []
                S.emit("sync", lambda e, tt=tt, p=p: e.dma_start(
                    out=xt[p][:, :], in_=d_x[tt * 128:(tt + 1) * 128, :]), dma=f"xld{p}", waits=w)
                wldx = S.dmark(f"xld{p}")
                S.emit("vector", lambda e, p=p: e.reduce_sum(mstat[:, 0:1], xt[p][:, :], AX),
                       waits=[wldx], inc=True)
                S.emit("scalar", lambda e: e.activation(
                    mstat[:, 1:2], mstat[:, 0:1], AF.Copy, scale=cst[:, 1:2]),
                    waits=[S.mark("vector")], inc=True)
                S.emit("vector", lambda e, p=p: e.tensor_scalar(
                    xc[:, :], xt[p][:, :], mstat[:, 1:2], None, ALU.subtract),
                    waits=[S.mark("scalar")], inc=True)
                S.emit("scalar", lambda e: e.activation(sq[:, :], xc[:, :], AF.Square),
                       waits=[S.mark("vector")], inc=True)
                S.emit("vector", lambda e: e.reduce_sum(mstat[:, 2:3], sq[:, :], AX),
                       waits=[S.mark("scalar")], inc=True)
                S.emit("scalar", lambda e: e.activation(
                    mstat[:, 3:4], mstat[:, 2:3], AF.Sqrt, bias=cst[:, 0:1], scale=cst[:, 1:2]),
                    waits=[S.mark("vector")], inc=True)
                S.emit("vector", lambda e: e.reciprocal(mstat[:, 0:1], mstat[:, 3:4]),
                       waits=[S.mark("scalar")], inc=True)
                S.gap("vector")
                S.emit("vector", lambda e: e.tensor_scalar(
                    xc[:, :], xc[:, :], mstat[:, 0:1], None, ALU.mult), inc=True)
                xn_mark = S.mark("vector")
                S.emit("vector", lambda e, tt=tt: e.reduce_max(
                    stmax[:, tt:tt + 1], xc[:, :], AX, apply_absolute_value=True))
                S.emit("sync", lambda e, tt=tt: e.dma_start(
                    out=t_xn_tok[tt * 128:(tt + 1) * 128, :], in_=xc[:, :]),
                    dma="xntok", waits=[xn_mark])
                S.emit("vector", lambda e: e.tensor_copy(xnb[:, :], xc[:, :]),
                       waits=[S.mark("tensor")], inc=True)
                fm_pass(tt, t_xn, "xnfm")
                tile_done[tt] = S.mark("vector")

            S.emit("vector", lambda e: e.reduce_max(acc1[:, 0:1], stmax[:, :], AX),
                   waits=[S.mark("vector")], inc=True)
            S.gap("vector")
            S.emit("tensor", lambda e: e.transpose(p_mm[0][0:1, 0:128],
                                                   acc1[:, 0:1], eye_f[:, :]),
                   waits=[S.mark("vector")], inc=True)
            S.emit("vector", lambda e: e.reduce_max(sc11[0:1, 0:1], p_mm[0][0:1, 0:128], AX),
                   waits=[S.mark("tensor")], inc=True)
            S.emit("sync", lambda e: e.dma_start(out=t_ccx1[0:1, 0:1], in_=sc11[0:1, 0:1]),
                   dma="ccs", waits=[S.mark("vector")])
            S.emit("gpsimd", lambda e: e.collective_compute(
                "AllReduce", ALU.max, replica_groups=ALL8,
                ins=[t_ccx1.opt()], outs=[t_ccx2.opt()]),
                waits=[S.dmark("ccs")], dma="ccse", dmainc=1)
            S.emit("sync", lambda e: e.dma_start(out=sc11[0:1, 1:2], in_=t_ccx2[0:1, 0:1]),
                   dma="ccs", waits=[S.dmark("ccse")])
            S.emit("vector", lambda e: e.tensor_scalar(
                sc11[0:1, 2:3], sc11[0:1, 1:2], 1.0 / QMAX, 1e-12, ALU.mult, ALU.max),
                waits=[S.dmark("ccs")], inc=True)
            S.gap("vector")
            S.emit("vector", lambda e: e.reciprocal(sc11[0:1, 3:4], sc11[0:1, 2:3]), inc=True)
            S.gap("vector")
            bcast_scalar(qsc[:, 1:2], sc11[0:1, 2:3], "qb", waits=[S.mark("vector")])
            bcast_scalar(qsc[:, 0:1], sc11[0:1, 3:4], "qb")
            for tt in range(ntok):
                p = tt % 2
                w = [S.dmark("qb"), S.dmark("xntok")] if tt == 0 else []
                if tt >= 2:
                    w.append(tile_done[ntok + tt - 2])
                S.emit("sync", lambda e, tt=tt, p=p: e.dma_start(
                    out=xt[p][:, :], in_=t_xn_tok[tt * 128:(tt + 1) * 128, :]),
                    dma=f"xld{p}", waits=w)
                S.emit("vector", lambda e, p=p: e.tensor_scalar(
                    xc[:, :], xt[p][:, :], qsc[:, 0:1], RND, ALU.mult, ALU.add),
                    waits=[S.dmark(f"xld{p}")], inc=True)
                S.emit("vector", lambda e: e.tensor_scalar(
                    xc[:, :], xc[:, :], RND, QMAX, ALU.subtract, ALU.min))
                S.emit("vector", lambda e: e.tensor_scalar(
                    xc[:, :], xc[:, :], -QMAX - 1.0, qsc[:, 1:2], ALU.max, ALU.mult))
                S.emit("vector", lambda e: e.tensor_copy(xnb[:, :], xc[:, :]),
                       waits=[S.mark("tensor")], inc=True)
                fm_pass(tt, t_xq, "xqfm")
                tile_done[ntok + tt] = S.mark("vector")

        dA = [sbt(f"dA{p}", [128, CAT]) for p in range(2)]
        bcat = sbt("bcat", [128, CAT], BF16)
        hcat = sbt("hcat", [128, CAT], BF16)
        vcat = [sbt(f"vcat{p}", [128, CAT], BF16) for p in range(2)]
        Bbc = [sbt(f"Bbc{p}", [128, CAT], BF16) for p in range(2)]
        Cbc = [sbt(f"Cbc{p}", [128, CAT], BF16) for p in range(2)]
        sp_all = sbt("sp_all", [128, NT * TC])
        q_all = sbt("q_all", [128, NT * TC], BF16)
        dtraw = sbt("dtraw", [64, L], BF16)
        escr = sbt("escr", [128, TC])
        dtwT = sbt("dtwT", [64, NT * 128], BF16)

        wld = [sbt(f"wld{p}", [128, 1024]) for p in range(2)]
        wldb = sbt("wldb", [128, 2048], BF16)
        wT = sbt("wT", [128, 12288], BF16)
        rhsb = [sbt(f"rhsb{p}", [128, NT * TC], BF16) for p in range(2)]
        stg = [sbt(f"stg{p}", [128, TC]) for p in range(2)]
        stgb = [sbt(f"stgb{p}", [128, TC], BF16) for p in range(2)]
        for p in range(2):
            S.emit("vector", lambda e, p=p: e.memset(dA[p][:, :], 0.0), inc=True)
        S.emit("vector", lambda e: e.memset(bcat[:, :], 0.0), inc=True)
        ucc = rhsb
        ystg = stg
        S.emit("vector", lambda e: e.memset(stg[0][:, :], 0.0), inc=True)
        zf_mark = S.mark("vector")
        for rg in range(4 * DM // 128):
            S.emit("sync", lambda e, rg=rg: e.dma_start(
                out=t_ccbi[rg * 128:(rg + 1) * 128, :],
                in_=stg[0][:, :].unsqueeze(1).broadcast_to([128, L // TC, TC])),
                dma="zf", waits=[zf_mark] if rg == 0 else [])
        for rg in range(6):
            S.emit("sync", lambda e, rg=rg: e.dma_start(
                out=t_ccsi[rg * 128:(rg + 1) * 128, :],
                in_=stg[0][:, :].unsqueeze(1).broadcast_to([128, L // TC, TC])),
                dma="zf")
        zf_done = S.dmark("zf")
        gt_alias = q_all
        oq_alias = sp_all.ap().bitcast(BF16)

        def scan_phase(i, tag, dtraw_mark, dtwT_mark, bc_done):
            k_idx = [0]
            q_marks = {}
            scan_marks = {}
            v_marks = {}
            pe_marks = {}
            btt_marks = {}
            evac_marks = {}
            dma_y_name = f"ydma{tag}"

            rhs_free = S.mark("tensor")
            for tc in range(NTC):
                ptc = tc % 2
                w = [S.dmark(f"uc{tag}")]
                if tc < 2:
                    w.append(rhs_free)
                if tc >= 2:
                    w.append(evac_marks[(tc - 2, NT - 1)])
                for dt_i in range(NT):
                    S.emit("sync", lambda e, tc=tc, dt_i=dt_i, ptc=ptc: e.dma_start(
                        out=ucc[ptc][:, dt_i * TC:(dt_i + 1) * TC],
                        in_=t_uc[dt_i * 128:(dt_i + 1) * 128, tc * TC:(tc + 1) * TC]),
                        dma=f"ucc{ptc}", waits=w if dt_i == 0 else [])
                ucc_mark = S.dmark(f"ucc{ptc}")
                prev_q_mark = q_marks.get(tc - 1)
                for dt_i in range(NT):
                    bank = dt_i
                    wm = [dtraw_mark, dtwT_mark]
                    if tc > 0:
                        wm.append(evac_marks[(tc - 1, dt_i)])
                    S.emit("tensor", lambda e, tc=tc, dt_i=dt_i, bank=bank: e.matmul(
                        p_mm[bank][:, :], dtwT[0:64, dt_i * 128:(dt_i + 1) * 128],
                        dtraw[0:64, tc * TC:(tc + 1) * TC], start=True, stop=True),
                        waits=wm, inc=True)
                    wsc = [S.mark("tensor")]
                    if dt_i == 0 and prev_q_mark is not None:
                        wsc.append(prev_q_mark)
                    S.emit("scalar", lambda e, dt_i=dt_i, bank=bank: e.activation(
                        escr[:, :], p_mm[bank][:, :], AF.Exp, bias=dtb[:, dt_i:dt_i + 1]),
                        waits=wsc, inc=True)
                    S.emit("scalar", lambda e, dt_i=dt_i: e.activation(
                        sp_all[:, dt_i * TC:(dt_i + 1) * TC], escr[:, :], AF.Ln,
                        bias=cst[:, 2:3]), inc=True)
                    S.emit("vector", lambda e, dt_i=dt_i, ptc=ptc: e.tensor_tensor(
                        q_all[:, dt_i * TC:(dt_i + 1) * TC],
                        sp_all[:, dt_i * TC:(dt_i + 1) * TC],
                        ucc[ptc][:, dt_i * TC:(dt_i + 1) * TC], ALU.mult),
                        waits=[S.mark("scalar"), ucc_mark], inc=True)
                q_mark = S.mark("vector")
                q_marks[tc] = q_mark
                sp_mark = S.mark("scalar")
                if tc == 0 and i == 0 and "sp" in DBG2:
                    S.emit("sync", lambda e: e.dma_start(out=DBG2["sp"], in_=sp_all[:, :]),
                           dma="dbgsp", waits=[sp_mark, q_mark])
                    S.emit("sync", lambda e: e.dma_start(out=DBG2["q"], in_=q_all[:, :]),
                           dma="dbgsp", waits=[sp_mark, q_mark])
                    S.emit("sync", lambda e: e.dma_start(out=DBG2["an"], in_=Aneg[:, :]),
                           dma="dbgsp", waits=[sp_mark, q_mark])

                import os as _os2
                NGl = int(_os2.environ.get("DBGNG", str(NG)))
                for g in range(NGl):
                    pg = (tc * NG + g) % 2
                    gi = tc * NG + g
                    wB = [q_mark, bc_done]
                    if gi >= 2:
                        wB.append(btt_marks.get(gi - 2, S.mark("vector")))
                    for seg in range(GS):
                        srow = g * GS + seg
                        S.emit("sync", lambda e, srow=srow, seg=seg, pg=pg, tc=tc: e.dma_start(
                            out=Bbc[pg][:, seg * SEG + 2:seg * SEG + 2 + TC],
                            in_=t_Bc[srow:srow + 1, tc * TC:(tc + 1) * TC]
                            .unsqueeze(1).broadcast_to([1, 128, TC])),
                            dma=f"bb{pg}", waits=wB if seg == 0 else [])
                    wC = [bc_done]
                    if gi >= 2:
                        wC.append(v_marks.get((gi - 2, NT - 1), S.mark("vector")))
                    for seg in range(GS):
                        srow = g * GS + seg
                        S.emit("gpsimd", lambda e, srow=srow, seg=seg, pg=pg, tc=tc: e.dma_start(
                            out=Cbc[pg][:, seg * SEG + 2:seg * SEG + 2 + TC],
                            in_=t_Cc[srow:srow + 1, tc * TC:(tc + 1) * TC]
                            .unsqueeze(1).broadcast_to([1, 128, TC])),
                            dma=f"cb{pg}", waits=wC if seg == 0 else [])
                    bbc_mark = S.dmark(f"bb{pg}")
                    cbc_mark = S.dmark(f"cb{pg}")
                    if tc == 0 and g == 0 and i == 0 and "sp" in DBG2:
                        S.emit("sync", lambda e: e.dma_start(out=DBG2["bb"], in_=Bbc[0][:, :]),
                               dma="dbgsp", waits=[bbc_mark, cbc_mark])
                        S.emit("sync", lambda e: e.dma_start(out=DBG2["cb"], in_=Cbc[0][:, :]),
                               dma="dbgsp", waits=[bbc_mark, cbc_mark])

                    for dt_i in range(NT):
                        k = k_idx[0]
                        pk = k % 2
                        import os as _os
                        _ser = bool(_os.environ.get("DBGSER"))
                        wA = [sp_mark]
                        if _ser and k >= 1:
                            wA.append(scan_marks[k - 1])
                        if k >= 2:
                            wA.append(scan_marks[k - 2])
                        for seg in range(GS):
                            s_idx = g * GS + seg
                            S.emit("scalar", lambda e, dt_i=dt_i, seg=seg, s_idx=s_idx, pk=pk:
                                   e.activation(
                                       dA[pk][:, seg * SEG + 2:seg * SEG + 2 + TC],
                                       sp_all[:, dt_i * TC:(dt_i + 1) * TC], AF.Exp,
                                       scale=Aneg[:, dt_i * 64 + s_idx:dt_i * 64 + s_idx + 1]),
                                   waits=wA if seg == 0 else [], inc=(seg == GS - 1))
                        dA_mark = S.mark("scalar")
                        wD = [bbc_mark, dA_mark]
                        S.emit("vector", lambda e, dt_i=dt_i, g=g: e.tensor_copy(
                            seg3(bcat, 1, 2),
                            carry[:, dt_i * 64 + g * GS:dt_i * 64 + (g + 1) * GS].unsqueeze(2)),
                            waits=wD)
                        S.emit("vector", lambda e, dt_i=dt_i, pg=pg: e.tensor_tensor(
                            seg3(bcat, 2, SEG),
                            q_all[:, dt_i * TC:(dt_i + 1) * TC].unsqueeze(1).broadcast_to([128, GS, TC]),
                            seg3(Bbc[pg], 2, SEG), ALU.mult), inc=True)
                        btt_marks[gi] = S.mark("vector")
                        S.emit("vector", lambda e, pk=pk: e.tensor_tensor_scan(
                            hcat[:, :], dA[pk][:, :], bcat[:, :], 0.0, ALU.mult, ALU.add),
                            inc=True)
                        scan_marks[k] = S.mark("vector")
                        wV = [cbc_mark]
                        if _ser and k >= 1:
                            wV.append(pe_marks[k - 1])
                        if k >= 2:
                            wV.append(pe_marks[k - 2])
                        S.emit("vector", lambda e, pk=pk, pg=pg: e.tensor_tensor(
                            seg3(vcat[pk], 2, SEG), seg3(hcat, 2, SEG),
                            seg3(Cbc[pg], 2, SEG), ALU.mult), waits=wV)
                        S.emit("vector", lambda e, dt_i=dt_i, g=g: e.tensor_copy(
                            carry[:, dt_i * 64 + g * GS:dt_i * 64 + (g + 1) * GS].unsqueeze(2),
                            seg3(hcat, SEG - 1, SEG)), inc=True)
                        v_marks[(gi, dt_i)] = S.mark("vector")
                        import os
                        _dbgg = int(os.environ.get("DBGG", "0"))
                        if tc == 0 and g == _dbgg and dt_i == 0 and i == 0 and "hc" in DBG2:
                            S.emit("sync", lambda e: e.dma_start(out=DBG2["hc"], in_=hcat[:, :]),
                                   dma="dbgh", waits=[v_marks[(gi, dt_i)]])
                            S.emit("sync", lambda e, pk=pk: e.dma_start(out=DBG2["vc"], in_=vcat[pk][:, :]),
                                   dma="dbgh", waits=[v_marks[(gi, dt_i)]])
                            S.emit("sync", lambda e: e.dma_start(out=DBG2["bc2"], in_=bcat[:, :]),
                                   dma="dbgh", waits=[v_marks[(gi, dt_i)]])
                            S.emit("sync", lambda e, pk=pk: e.dma_start(out=DBG2["da"], in_=dA[pk][:, :]),
                                   dma="dbgh", waits=[v_marks[(gi, dt_i)]])
                            S.emit("vector", lambda e: e.engine_nop(),
                                   waits=[S.dmark("dbgh")], inc=True)
                        for seg in range(GS):
                            S.emit("tensor", lambda e, seg=seg, pk=pk, dt_i=dt_i, g=g, NGl=NGl:
                                   e.matmul(
                                       p_mm[dt_i][:, :], eye_b[:, :],
                                       vcat[pk][:, seg * SEG + 2:seg * SEG + 2 + TC],
                                       start=(g == 0 and seg == 0),
                                       stop=(g == NGl - 1 and seg == GS - 1)),
                                   waits=[v_marks[(gi, dt_i)]] if seg == 0 else [],
                                   inc=(seg == GS - 1))
                        pe_marks[k] = S.mark("tensor")
                        if tc == 0 and g == int(_os2.environ.get("DBGPP", "0")) and dt_i == 0 and i == 0 and "pp" in DBG2:
                            S.emit("vector", lambda e: e.tensor_copy(
                                escr[:, :], p_mm[0][:, :]), waits=[pe_marks[k]], inc=True)
                            S.emit("sync", lambda e: e.dma_start(out=DBG2["pp"], in_=escr[:, :]),
                                   dma="dbgp", waits=[S.mark("vector")])
                            S.emit("tensor", lambda e: e.nop(), waits=[S.dmark("dbgp")])
                            S.emit("vector", lambda e: e.engine_nop(), waits=[S.dmark("dbgp")])
                        k_idx[0] += 1

                for dt_i in range(NT):
                    p = dt_i % 2
                    wE = [pe_marks[k_idx[0] - NT + dt_i]]
                    if tc * NT + dt_i >= 2:
                        wE.append(S.dmark(dma_y_name))
                    S.emit("vector", lambda e, dt_i=dt_i, p=p, ptc=ptc: e.scalar_tensor_tensor(
                        ystg[p][:, :], ucc[ptc][:, dt_i * TC:(dt_i + 1) * TC],
                        dskip[:, dt_i:dt_i + 1], p_mm[dt_i][:, :], ALU.mult, ALU.add),
                        waits=wE, inc=True)
                    evac_marks[(tc, dt_i)] = S.mark("vector")
                    S.emit("sync", lambda e, dt_i=dt_i, tc=tc, p=p: e.dma_start(
                        out=t_y[dt_i * 128:(dt_i + 1) * 128, tc * TC:(tc + 1) * TC],
                        in_=ystg[p][:, :]), dma=dma_y_name, waits=[evac_marks[(tc, dt_i)]])
                    if tc == 0 and dt_i == 0 and i == 0 and "ys" in DBG2:
                        S.emit("sync", lambda e, p=p: e.dma_start(
                            out=DBG2["ys"], in_=ystg[p][:, :]),
                            dma="dbgy", waits=[evac_marks[(tc, dt_i)]])
                        S.emit("sync", lambda e, ptc=ptc: e.dma_start(
                            out=DBG2["uc0"], in_=ucc[ptc][:, 0:TC]),
                            dma="dbgy", waits=[evac_marks[(tc, dt_i)]])
            return S.dmark(dma_y_name)

        def mamba_layer(i, rhs_in):
            tag = f"L{i}"
            for dst, src in [(convw, d_convw[i]), (convb, d_convb[i]), (dtb, d_dtb[i]),
                             (dskip, d_dskip[i])]:
                load_const(dst, src, f"cv{tag}",
                           waits=[S.mark("vector"), S.mark("scalar")] if i > 0 else ())
            S.emit("sync", lambda e: e.dma_start(out=aux[:, 0:NT * 64], in_=d_alog[i]),
                   dma=f"cv{tag}")
            S.emit("scalar", lambda e: e.activation(Aneg[:, :], aux[:, 0:NT * 64], AF.Exp),
                   waits=[S.dmark(f"cv{tag}")], inc=True)
            S.emit("vector", lambda e: e.tensor_scalar(
                Aneg[:, :], Aneg[:, :], -1.0, None, ALU.mult),
                waits=[S.mark("scalar")], inc=True)
            S.emit("vector", lambda e: e.memset(carry[:, :], 0.0), inc=True)

            rhs_ready = S.dmark("xnfm") if i == 0 else S.dmark("l2w")
            wu_of = prep_lhsT(d_inw[i][0:DH, :], DH, f"{tag}u")
            wt_ready = S.mark("vector")

            def evac_u(m, n, bank, pe_mark):
                p = (n * 8 + m) % 2
                wv = [pe_mark]
                if i == 0 and n == 0 and m == 0:
                    wv.append(zf_done)
                if n * 8 + m >= 2:
                    wv.append(S.dmark(f"ur{tag}"))
                S.emit("vector", lambda e, bank=bank, p=p: e.tensor_copy(
                    stg[p][:, :], p_mm[bank][:, :]), waits=wv, inc=True)
                S.emit("sync", lambda e, m=m, n=n, p=p: e.dma_start(
                    out=t_uraw[m * 128:(m + 1) * 128, n * TC:(n + 1) * TC],
                    in_=stg[p][:, :]), dma=f"ur{tag}", waits=[S.mark("vector")])
            mm_phase(rhs_in, 8, 8, wu_of, evac_u, f"{tag}u", extra_w=[wt_ready, rhs_ready])

            wz_of = prep_lhsT(d_inw[i][DH:2 * DH, :], DH, f"{tag}z")
            wt_ready = S.mark("vector")

            def evac_z(m, n, bank, pe_mark):
                p = (n * 8 + m) % 2
                wv = [pe_mark]
                if n * 8 + m >= 2:
                    wv.append(S.dmark(f"zr{tag}"))
                S.emit("vector", lambda e, bank=bank, p=p: e.tensor_copy(
                    stgb[p][:, :], p_mm[bank][:, :]), waits=wv, inc=True)
                S.emit("sync", lambda e, m=m, n=n, p=p: e.dma_start(
                    out=t_z[m * 128:(m + 1) * 128, n * TC:(n + 1) * TC],
                    in_=stgb[p][:, :]), dma=f"zr{tag}", waits=[S.mark("vector")])
            mm_phase(rhs_in, 8, 8, wz_of, evac_z, f"{tag}z", extra_w=[wt_ready, rhs_ready])

            CC = 512
            with ExitStack() as c2:
                ub = [c2.enter_context(nc.sbuf_tensor(f"ub{tag}{p}", [128, CC + 4], F32))
                      for p in range(2)]
                cacc = escr
                ucb = stgb
                for p in range(2):
                    S.emit("vector", lambda e, p=p: e.memset(ub[p][:, 0:3], 0.0), inc=True)
                silu_marks = {}
                nch = L // CC
                for ci in range(NT * nch):
                    dt_i, ch = divmod(ci, nch)
                    p = ci % 2
                    wl = [S.dmark(f"ur{tag}")]
                    if ci >= 2:
                        wl.append(silu_marks[ci - 2])
                    lo = ch * CC - 3
                    if ch == 0:
                        S.emit("sync", lambda e, dt_i=dt_i, ch=ch, p=p: e.dma_start(
                            out=ub[p][:, 3:CC + 3],
                            in_=t_uraw[dt_i * 128:(dt_i + 1) * 128, ch * CC:(ch + 1) * CC]),
                            dma=f"ul{p}", waits=wl)
                        if ci > 0:
                            S.emit("vector", lambda e, p=p: e.memset(ub[p][:, 0:3], 0.0),
                                   waits=[silu_marks[ci - 2]] if ci >= 2 else [])
                    else:
                        S.emit("sync", lambda e, dt_i=dt_i, lo=lo, p=p: e.dma_start(
                            out=ub[p][:, 0:CC + 3],
                            in_=t_uraw[dt_i * 128:(dt_i + 1) * 128, lo:lo + CC + 3]),
                            dma=f"ul{p}", waits=wl)
                    S.emit("vector", lambda e, dt_i=dt_i, p=p: e.tensor_scalar(
                        cacc[:, :], ub[p][:, 0:CC], convw[:, dt_i * 4:dt_i * 4 + 1], None,
                        ALU.mult), waits=[S.dmark(f"ul{p}"), S.mark("scalar")], inc=True)
                    for kk in range(1, 4):
                        S.emit("vector", lambda e, dt_i=dt_i, p=p, kk=kk: e.scalar_tensor_tensor(
                            cacc[:, :], ub[p][:, kk:CC + kk],
                            convw[:, dt_i * 4 + kk:dt_i * 4 + kk + 1],
                            cacc[:, :], ALU.mult, ALU.add), inc=(kk == 3))
                    wsl = [S.mark("vector")]
                    if ci == 0:
                        wsl.append(S.dmark(f"zr{tag}"))
                    S.emit("scalar", lambda e, dt_i=dt_i, p=p: e.activation(
                        ucb[p][:, :], cacc[:, :], AF.Silu, bias=convb[:, dt_i:dt_i + 1]),
                        waits=wsl, inc=True)
                    silu_marks[ci] = S.mark("scalar")
                    S.emit("sync", lambda e, dt_i=dt_i, ch=ch, p=p: e.dma_start(
                        out=t_uc[dt_i * 128:(dt_i + 1) * 128, ch * CC:(ch + 1) * CC],
                        in_=ucb[p][:, :]), dma=f"uc{tag}", waits=[silu_marks[ci]])

            nmx = 2
            for m in range(nmx):
                rows = 128 if m == 0 else 64
                wv = [S.mark("tensor")] if m > 0 else [S.mark("vector")]
                S.emit("sync", lambda e, m=m, rows=rows: e.dma_start(
                    out=wld[0][0:rows, 0:DM], in_=d_xpw[i][m * 128:m * 128 + rows, :]),
                    dma=f"wx{tag}", waits=wv)
                S.emit("vector", lambda e, rows=rows: e.tensor_copy(
                    wldb[0:rows, 0:DM], wld[0][0:rows, 0:DM]),
                    waits=[S.dmark(f"wx{tag}")], inc=True)
                cast_mark = S.mark("vector")
                for k in range(8):
                    pt = p_tr[k % 2]
                    wtp = [cast_mark]
                    if m * 8 + k >= 2:
                        wtp.append(tr_evac_marks[-2])
                    S.emit("tensor", lambda e, k=k, rows=rows, pt=pt: e.transpose(
                        pt[0:128, 0:rows], wldb[0:rows, k * 128:(k + 1) * 128],
                        eye_b[0:rows, 0:rows]), waits=wtp, inc=True)
                    col = (k * nmx + m) * 128
                    S.emit("vector", lambda e, col=col, rows=rows, pt=pt: e.tensor_copy(
                        wT[:, col:col + rows], pt[0:128, 0:rows]),
                        waits=[S.mark("tensor")], inc=True)
                    tr_evac_marks.append(S.mark("vector"))
            wt_ready = S.mark("vector")

            def evac_x(m, n, bank, pe_mark):
                rows = 128 if m == 0 else 64
                p = (n * 2 + m) % 2
                wv = [pe_mark]
                if n * 2 + m >= 2:
                    wv.append(S.dmark(f"xd{tag}"))
                S.emit("vector", lambda e, rows=rows, bank=bank, p=p: e.tensor_copy(
                    stg[p][0:rows, :], p_mm[bank][0:rows, :]), waits=wv, inc=True)
                S.emit("sync", lambda e, m=m, n=n, rows=rows, p=p: e.dma_start(
                    out=t_smp[m * 128:m * 128 + rows, n * TC:(n + 1) * TC],
                    in_=stg[p][0:rows, :]), dma=f"xd{tag}", waits=[S.mark("vector")])
            mm_phase(t_uc, 8, 2, lambda k, m: (k * nmx + m) * 128, evac_x, f"{tag}x",
                     extra_w=[wt_ready, S.dmark(f"uc{tag}")])
            slot_copy("gpsimd", "sci",
                      lambda p: t_ccsi[p * 192:(p + 1) * 192, :],
                      lambda p: t_smp[:, :], waits=[S.dmark(f"xd{tag}"), zf_done])
            S.emit("gpsimd", lambda e: e.collective_compute(
                "AllReduce", ALU.add, replica_groups=ALL8,
                ins=[t_ccsi.opt()], outs=[t_ccso.opt()]),
                waits=[S.dmark("sci")], dma="ccc", dmainc=1)
            slot_copy("scalar", "sco",
                      lambda p: t_smf[:, :],
                      lambda p: t_ccso[p * 192:(p + 1) * 192, :],
                      waits=[S.dmark("ccc")])
            ccc_mark = S.dmark("sco")
            for gi_, (dst, r0) in enumerate([(None, 0), (t_Bc, 64), (t_Cc, 128)]):
                for ch in range(4):
                    p = ch % 2
                    wv = [ccc_mark]
                    if gi_ * 4 + ch >= 2:
                        wv.append(S.mark("vector"))
                    S.emit("sync", lambda e, r0=r0, ch=ch, p=p: e.dma_start(
                        out=wld[p][0:64, 0:1024],
                        in_=t_smf[r0:r0 + 64, ch * 1024:(ch + 1) * 1024]),
                        dma=f"xu{tag}", waits=wv)
                    if dst is None:
                        S.emit("vector", lambda e, ch=ch, p=p: e.tensor_copy(
                            dtraw[:, ch * 1024:(ch + 1) * 1024], wld[p][0:64, 0:1024]),
                            waits=[S.dmark(f"xu{tag}")], inc=True)
                    else:
                        wbk = [S.dmark(f"xu{tag}")]
                        if gi_ * 4 + ch >= 2:
                            wbk.append(S.dmark(f"bc{tag}"))
                        S.emit("vector", lambda e, ch=ch, p=p: e.tensor_copy(
                            auxb[0:64, 0:1024], wld[p][0:64, 0:1024]), waits=wbk, inc=True)
                        S.emit("sync", lambda e, dst=dst, ch=ch: e.dma_start(
                            out=dst[0:64, ch * 1024:(ch + 1) * 1024], in_=auxb[0:64, 0:1024]),
                            dma=f"bc{tag}", waits=[S.mark("vector")])
            dtraw_mark = S.mark("vector")
            bc_done = S.dmark(f"bc{tag}")

            S.emit("sync", lambda e: e.dma_start(out=wld[1][:, 0:NT * 64], in_=d_dtw[i]),
                   dma=f"dw{tag}", waits=[S.mark("vector")] if i > 0 else ())
            S.emit("vector", lambda e: e.tensor_copy(wldb[:, 0:NT * 64], wld[1][:, 0:NT * 64]),
                   waits=[S.dmark(f"dw{tag}")], inc=True)
            cast_mark = S.mark("vector")
            for k in range(NT):
                pt = p_tr[k % 2]
                wtp = [cast_mark]
                if k >= 2:
                    wtp.append(tr_evac_marks[-2])
                S.emit("tensor", lambda e, k=k, pt=pt: e.transpose(
                    pt[0:64, 0:128], wldb[:, k * 64:(k + 1) * 64], eye_b[:, :]),
                    waits=wtp, inc=True)
                S.emit("vector", lambda e, k=k, pt=pt: e.tensor_copy(
                    dtwT[0:64, k * 128:(k + 1) * 128], pt[0:64, 0:128]),
                    waits=[S.mark("tensor")], inc=True)
                tr_evac_marks.append(S.mark("vector"))
            dtwT_mark = S.mark("vector")

            y_done = scan_phase(i, tag, dtraw_mark, dtwT_mark, bc_done)


            wo_of = prep_lhsT(d_outw[i], DM, f"{tag}o")
            wt_ready = S.mark("vector")
            with ExitStack() as c5:
                gt = gt_alias
                zl = stgb
                zs = escr
                yl = [c5.enter_context(nc.sbuf_tensor(f"yl{tag}{p}", [128, TC], F32))
                      for p in range(2)]
                g_marks = {}
                mm_done = {}
                bank_evac = {}
                for n in range(NTC):
                    for k in range(NT):
                        p = k % 2
                        wv = [S.dmark(f"zr{tag}"), y_done]
                        if (n, k) != (0, 0):
                            if n * NT + k >= 2:
                                wv.append(g_marks[n * NT + k - 2])
                        S.emit("sync", lambda e, n=n, k=k, p=p: e.dma_start(
                            out=zl[p][:, :], in_=t_z[k * 128:(k + 1) * 128, n * TC:(n + 1) * TC]),
                            dma=f"zl{p}", waits=wv)
                        S.emit("sync", lambda e, n=n, k=k, p=p: e.dma_start(
                            out=yl[p][:, :], in_=t_y[k * 128:(k + 1) * 128, n * TC:(n + 1) * TC]),
                            dma=f"yl{p}")
                        S.emit("scalar", lambda e, p=p: e.activation(
                            zs[:, :], zl[p][:, :], AF.Silu),
                            waits=[S.dmark(f"zl{p}"), S.mark("vector")], inc=True)
                        wg_ = [S.dmark(f"yl{p}"), S.mark("scalar")]
                        if n > 0:
                            wg_.append(mm_done.get((n - 1, NT - 1), (None, 0)))
                        S.emit("vector", lambda e, k=k, p=p: e.tensor_tensor(
                            gt[:, k * TC:(k + 1) * TC], yl[p][:, :], zs[:, :], ALU.mult),
                            waits=[x for x in wg_ if x[0] is not None], inc=True)
                        g_marks[n * NT + k] = S.mark("vector")
                    gm = S.mark("vector")
                    for m in range(8):
                        bank = m
                        wm = [gm]
                        prev = bank_evac.get(bank)
                        if prev is not None:
                            wm.append(prev)
                        for k in range(NT):
                            S.emit("tensor", lambda e, m=m, k=k, bank=bank: e.matmul(
                                p_mm[bank][:, :], wT[:, wo_of(k, m):wo_of(k, m) + 128],
                                gt[:, k * TC:(k + 1) * TC],
                                start=(k == 0), stop=(k == NT - 1)),
                                waits=(wm + [wt_ready]) if k == 0 else [], inc=(k == NT - 1))
                        mm_done[(n, m)] = S.mark("tensor")
                        p = (n * 8 + m) % 2
                        wv = [mm_done[(n, m)]]
                        if n * 8 + m >= 2:
                            wv.append(S.dmark(f"op{tag}"))
                        S.emit("vector", lambda e, bank=bank, p=p: e.tensor_copy(
                            stg[p][:, :], p_mm[bank][:, :]), waits=wv, inc=True)
                        bank_evac[bank] = S.mark("vector")
                        odst = t_bgp if i == 0 else t_m2p
                        S.emit("sync", lambda e, m=m, n=n, p=p, odst=odst: e.dma_start(
                            out=odst[m * 128:(m + 1) * 128, n * TC:(n + 1) * TC],
                            in_=stg[p][:, :]), dma=f"op{tag}", waits=[bank_evac[bank]])
            if i == 0:
                slot_copy("scalar", "bci",
                          lambda p: t_ccbi[p * DM:(p + 1) * DM, :],
                          lambda p: t_bgp[:, :], waits=[S.dmark(f"op{tag}"), zf_done])
                S.emit("gpsimd", lambda e: e.collective_compute(
                    "AllReduce", ALU.add, replica_groups=ALL8,
                    ins=[t_ccbi.opt()], outs=[t_ccbo.opt()]),
                    waits=[S.dmark("bci")], dma="ccc", dmainc=1)
                slot_copy("sync", "bco",
                          lambda p: t_bgf[:, :],
                          lambda p: t_ccbo[p * DM:(p + 1) * DM, :],
                          waits=[S.dmark("ccc")])
                for ci in range(NT * 4):
                    k, ch = divmod(ci, 4)
                    p = ci % 2
                    wv = [S.dmark("bco")]
                    if ci >= 1:
                        wv.append(S.mark("vector"))
                    S.emit("sync", lambda e, k=k, ch=ch, p=p: e.dma_start(
                        out=wld[p][:, 0:1024],
                        in_=t_bgf[k * 128:(k + 1) * 128, ch * 1024:(ch + 1) * 1024]),
                        dma="l2c", waits=wv)
                    S.emit("vector", lambda e, p=p: e.tensor_copy(auxb[:, 0:1024], wld[p][:, 0:1024]),
                           waits=[S.dmark("l2c")] + ([S.dmark("l2w")] if ci >= 1 else []),
                           inc=True)
                    S.emit("sync", lambda e, k=k, ch=ch: e.dma_start(
                        out=t_l2in[k * 128:(k + 1) * 128, ch * 1024:(ch + 1) * 1024],
                        in_=auxb[:, 0:1024]), dma="l2w", waits=[S.mark("vector")])

        def qdq3(dstb, src, inv_col, s_col, lo, hi, out_bf=False):
            S.emit("vector", lambda e: e.tensor_scalar(
                aux[:, 0:TC], src, inv_col, RND, ALU.mult, ALU.add), inc=True)
            S.emit("vector", lambda e: e.tensor_scalar(
                aux[:, 0:TC], aux[:, 0:TC], RND, hi, ALU.subtract, ALU.min))
            S.emit("vector", lambda e, dstb=dstb: e.tensor_scalar(
                dstb, aux[:, 0:TC], lo, s_col, ALU.max, ALU.mult), inc=True)

        def dequant_prep(w_dram, nrow_tiles, ncols, name, out_base=0):
            CW = 512
            ncc = ncols // CW
            nbc = CW // 32
            for m in range(nrow_tiles):
                for cch in range(ncc):
                    pw = (m * ncc + cch) % 2
                    wv = [S.mark("vector")] if m * ncc + cch == 0 else [S.mark("vector")]
                    S.emit("sync", lambda e, m=m, cch=cch, pw=pw: e.dma_start(
                        out=wld[pw][:, 0:CW],
                        in_=w_dram[m * 128:(m + 1) * 128, cch * CW:(cch + 1) * CW]),
                        dma="wp", waits=wv)
                    w3 = wld[pw][:, 0:CW].rearrange("p (nb b) -> p nb b", b=32)
                    S.emit("vector", lambda e, w3=w3: e.tensor_reduce(
                        aux[:, 0:nbc], w3, AX, ALU.max, apply_absolute_value=True),
                        waits=[S.dmark("wp")], inc=True)
                    S.gap("vector")
                    S.emit("vector", lambda e: e.tensor_scalar(
                        aux[:, 0:nbc], aux[:, 0:nbc], 1.0 / 7.0, 1e-12, ALU.mult, ALU.max))
                    S.gap("vector")
                    S.emit("vector", lambda e: e.reciprocal(
                        aux[:, nbc:2 * nbc], aux[:, 0:nbc]), inc=True)
                    S.gap("vector")
                    rq3 = aux[:, 512:512 + CW].rearrange("p (nb b) -> p nb b", b=32)
                    S.emit("vector", lambda e, w3=w3, rq3=rq3: e.tensor_tensor(
                        rq3, w3, aux[:, nbc:2 * nbc].unsqueeze(2).broadcast_to([128, nbc, 32]),
                        ALU.mult), waits=[S.mark("tensor")] if cch == 0 else [], inc=True)
                    S.emit("vector", lambda e, pw=pw: e.tensor_scalar(
                        wld[pw][:, 0:CW], aux[:, 512:512 + CW],
                        RND, RND, ALU.add, ALU.subtract), inc=True)
                    S.emit("vector", lambda e, pw=pw: e.tensor_scalar(
                        wld[pw][:, 0:CW], wld[pw][:, 0:CW], 7.0, -8.0, ALU.min, ALU.max))
                    wq3 = wldb[:, cch * CW:(cch + 1) * CW].rearrange("p (nb b) -> p nb b", b=32)
                    S.emit("vector", lambda e, pw=pw, wq3=wq3: e.tensor_tensor(
                        wq3, wld[pw][:, 0:CW].rearrange("p (nb b) -> p nb b", b=32),
                        aux[:, 0:nbc].unsqueeze(2).broadcast_to([128, nbc, 32]),
                        ALU.mult), inc=True)
                cast_mark = S.mark("vector")
                kt = ncols // 128
                for k in range(kt):
                    pt = p_tr[k % 2]
                    wtp = [cast_mark]
                    if m * kt + k >= 2:
                        wtp.append(tr_evac_marks[-2])
                    S.emit("tensor", lambda e, k=k, pt=pt: e.transpose(
                        pt[:, :], wldb[:, k * 128:(k + 1) * 128], eye_b[:, :]),
                        waits=wtp, inc=True)
                    col = out_base + (k * nrow_tiles + m) * 128
                    S.emit("vector", lambda e, col=col, pt=pt: e.tensor_copy(
                        wT[:, col:col + 128], pt[:, :]), waits=[S.mark("tensor")], inc=True)
                    tr_evac_marks.append(S.mark("vector"))
                prep_marks.append(S.mark("vector"))
            return lambda k, m: out_base + (k * nrow_tiles + m) * 128

        def collective_scalar_max(src_colidx, dst_colidx, ccname):
            S.emit("tensor", lambda e, src_colidx=src_colidx: e.transpose(
                p_mm[0][0:1, 0:128], acc1[:, src_colidx:src_colidx + 1], eye_f[:, :]),
                waits=[S.mark("vector")], inc=True)
            S.emit("vector", lambda e, dst_colidx=dst_colidx: e.reduce_max(
                sc11[0:1, dst_colidx:dst_colidx + 1], p_mm[0][0:1, 0:128], AX),
                waits=[S.mark("tensor")], inc=True)
            S.emit("sync", lambda e, dst_colidx=dst_colidx: e.dma_start(
                out=t_ccx1[0:1, 0:1], in_=sc11[0:1, dst_colidx:dst_colidx + 1]),
                dma="ccs", waits=[S.mark("vector")])
            S.emit("gpsimd", lambda e: e.collective_compute(
                "AllReduce", ALU.max, replica_groups=ALL8,
                ins=[t_ccx1.opt()], outs=[t_ccx2.opt()]),
                waits=[S.dmark("ccs")], dma="ccse", dmainc=1)
            S.emit("sync", lambda e, dst_colidx=dst_colidx: e.dma_start(
                out=sc11[0:1, dst_colidx:dst_colidx + 1], in_=t_ccx2[0:1, 0:1]),
                dma="ccs", waits=[S.dmark("ccse")])

        def ffn_phase():
            for which, w_dram, t_raw, acc_col in [("g", d_wg, t_grw, 1), ("u", d_wu, t_upw, 2)]:
                wf_of = dequant_prep(w_dram, NKI, DM, f"f{which}")
                wt_ready = S.mark("vector")
                S.emit("vector", lambda e: e.memset(stmaxf[:, :], 0.0), inc=True)

                def evac_f(m, n, bank, pe_mark, t_raw=t_raw, acc_col=acc_col, which=which):
                    p = (n * NKI + m) % 2
                    wv = [pe_mark]
                    if n * NKI + m >= 2:
                        wv.append(S.dmark("fr"))
                    S.emit("vector", lambda e, bank=bank, p=p: e.tensor_copy(
                        stg[p][:, :], p_mm[bank][:, :]), waits=wv, inc=True)
                    S.emit("vector", lambda e, p=p, m=m, n=n: e.reduce_max(
                        stmaxf[:, (m * NTC + n) % 128:(m * NTC + n) % 128 + 1],
                        stg[p][:, :], AX, apply_absolute_value=True), inc=True)
                    S.emit("sync", lambda e, m=m, n=n, p=p, t_raw=t_raw: e.dma_start(
                        out=t_raw[m * 128:(m + 1) * 128, n * TC:(n + 1) * TC],
                        in_=stg[p][:, :]), dma="fr", waits=[S.mark("vector")])
                mm_phase(t_xq, 8, NKI, wf_of, evac_f, f"f{which}",
                         extra_w=[wt_ready, S.dmark("xqfm")])
                S.emit("vector", lambda e, acc_col=acc_col: e.reduce_max(
                    acc1[:, acc_col:acc_col + 1], stmaxf[:, :], AX),
                    waits=[S.mark("vector")], inc=True)
                S.gap("vector")
                collective_scalar_max(acc_col, 4 + (0 if which == "g" else 1), f"m{which}")
            for src, dst in [(4, (2, 3)), (5, (4, 5))]:
                S.emit("vector", lambda e, src=src: e.tensor_scalar(
                    sc11[0:1, 6:7], sc11[0:1, src:src + 1], 1.0 / QMAX, 1e-12, ALU.mult, ALU.max),
                    waits=[S.dmark("ccs"), S.dmark("qb")], inc=True)
                S.gap("vector")
                S.emit("vector", lambda e: e.reciprocal(sc11[0:1, 7:8], sc11[0:1, 6:7]), inc=True)
                S.gap("vector")
                bcast_scalar(qsc[:, dst[1]:dst[1] + 1], sc11[0:1, 6:7], "qb",
                             waits=[S.mark("vector")])
                bcast_scalar(qsc[:, dst[0]:dst[0] + 1], sc11[0:1, 7:8], "qb")

            with ExitStack() as c6:
                fl = [c6.enter_context(nc.sbuf_tensor(f"fl{p}", [128, TC], F32)) for p in range(2)]
                gq = escr
                sg = stg[0]
                S.emit("vector", lambda e: e.memset(stmaxf[:, :], 0.0), inc=True)
                nchunks = NKI * NTC
                st_marks = {}
                for ci in range(nchunks):
                    m, n = divmod(ci, NTC)
                    p = ci % 2
                    wv = [S.dmark("qb"), S.dmark("fr")]
                    if ci >= 2:
                        wv.append(st_marks[ci - 2])
                    S.emit("sync", lambda e, m=m, n=n, p=p: e.dma_start(
                        out=fl[p][:, :], in_=t_grw[m * 128:(m + 1) * 128, n * TC:(n + 1) * TC]),
                        dma=f"g2l{p}", waits=wv)
                    S.emit("vector", lambda e, p=p: e.tensor_scalar(
                        aux[:, 0:TC], fl[p][:, :], qsc[:, 2:3], RND, ALU.mult, ALU.add),
                        waits=[S.dmark(f"g2l{p}")], inc=True)
                    S.emit("vector", lambda e: e.tensor_scalar(
                        aux[:, 0:TC], aux[:, 0:TC], RND, QMAX, ALU.subtract, ALU.min))
                    S.emit("vector", lambda e: e.tensor_scalar(
                        gq[:, :], aux[:, 0:TC], -QMAX - 1.0, qsc[:, 3:4], ALU.max, ALU.mult),
                        inc=True)
                    S.emit("scalar", lambda e: e.activation(sg[:, :], gq[:, :], AF.Sigmoid),
                           waits=[S.mark("vector")], inc=True)
                    S.emit("vector", lambda e: e.tensor_scalar(
                        aux[:, 0:TC], sg[:, :], 65536.0, RND, ALU.mult, ALU.add),
                        waits=[S.mark("scalar")], inc=True)
                    S.emit("vector", lambda e: e.tensor_scalar(
                        aux[:, 0:TC], aux[:, 0:TC], RND, 65535.0, ALU.subtract, ALU.min))
                    S.emit("vector", lambda e: e.tensor_scalar(
                        sg[:, :], aux[:, 0:TC], 0.0, 1.0 / 65536.0, ALU.max, ALU.mult))
                    S.emit("vector", lambda e, p=p: e.tensor_tensor(
                        fl[p][:, :], gq[:, :], sg[:, :], ALU.mult), inc=True)
                    st_marks[ci] = S.mark("vector")
                    S.emit("vector", lambda e, p=p, ci=ci: e.reduce_max(
                        stmaxf[:, ci % 128:ci % 128 + 1], fl[p][:, :], AX,
                        apply_absolute_value=True))
                    S.emit("sync", lambda e, m=m, n=n, p=p: e.dma_start(
                        out=t_g2w[m * 128:(m + 1) * 128, n * TC:(n + 1) * TC],
                        in_=fl[p][:, :]), dma="g2w", waits=[S.mark("vector")])
                S.emit("vector", lambda e: e.reduce_max(acc1[:, 3:4], stmaxf[:, :], AX),
                       waits=[S.mark("vector")], inc=True)
                S.gap("vector")
                collective_scalar_max(3, 6, "m2")
                S.emit("vector", lambda e: e.tensor_scalar(
                    sc11[0:1, 6:7], sc11[0:1, 6:7], 1.0 / QMAX, 1e-12, ALU.mult, ALU.max),
                    waits=[S.dmark("ccs"), S.dmark("qb")], inc=True)
                S.gap("vector")
                S.emit("vector", lambda e: e.reciprocal(sc11[0:1, 7:8], sc11[0:1, 6:7]), inc=True)
                S.gap("vector")
                bcast_scalar(qsc[:, 3:4], sc11[0:1, 6:7], "qb", waits=[S.mark("vector")])
                bcast_scalar(qsc[:, 2:3], sc11[0:1, 7:8], "qb")

                S.emit("vector", lambda e: e.memset(stmaxf[:, :], 0.0),
                       waits=[S.dmark("qb")], inc=True)
                st_marks = {}
                for ci in range(nchunks):
                    m, n = divmod(ci, NTC)
                    p = ci % 2
                    wv = [S.dmark("g2w"), S.dmark("fr")]
                    if ci >= 2:
                        wv.append(st_marks[ci - 2])
                    S.emit("sync", lambda e, m=m, n=n, p=p: e.dma_start(
                        out=fl[p][:, :], in_=t_g2w[m * 128:(m + 1) * 128, n * TC:(n + 1) * TC]),
                        dma=f"o3a{p}", waits=wv)
                    S.emit("sync", lambda e, m=m, n=n, p=p: e.dma_start(
                        out=gq[:, :] if p == 0 else sg[:, :],
                        in_=t_upw[m * 128:(m + 1) * 128, n * TC:(n + 1) * TC]),
                        dma=f"o3b{p}")
                    upbuf = lambda p: gq if p == 0 else sg
                    S.emit("vector", lambda e, p=p: e.tensor_scalar(
                        aux[:, 0:TC], fl[p][:, :], qsc[:, 2:3], RND, ALU.mult, ALU.add),
                        waits=[S.dmark(f"o3a{p}"), S.dmark(f"o3b{p}")], inc=True)
                    S.emit("vector", lambda e: e.tensor_scalar(
                        aux[:, 0:TC], aux[:, 0:TC], RND, QMAX, ALU.subtract, ALU.min))
                    S.emit("vector", lambda e, p=p: e.tensor_scalar(
                        fl[p][:, :], aux[:, 0:TC], -QMAX - 1.0, qsc[:, 3:4], ALU.max, ALU.mult))
                    S.emit("vector", lambda e, p=p: e.tensor_scalar(
                        aux[:, 0:TC], upbuf(p)[:, :], qsc[:, 4:5], RND, ALU.mult, ALU.add))
                    S.emit("vector", lambda e: e.tensor_scalar(
                        aux[:, 0:TC], aux[:, 0:TC], RND, QMAX, ALU.subtract, ALU.min))
                    S.emit("vector", lambda e, p=p: e.tensor_scalar(
                        upbuf(p)[:, :], aux[:, 0:TC], -QMAX - 1.0, qsc[:, 5:6], ALU.max, ALU.mult))
                    S.emit("vector", lambda e, p=p: e.tensor_tensor(
                        fl[p][:, :], fl[p][:, :], upbuf(p)[:, :], ALU.mult), inc=True)
                    st_marks[ci] = S.mark("vector")
                    S.emit("vector", lambda e, p=p, ci=ci: e.reduce_max(
                        stmaxf[:, ci % 128:ci % 128 + 1], fl[p][:, :], AX,
                        apply_absolute_value=True))
                    S.emit("sync", lambda e, m=m, n=n, p=p: e.dma_start(
                        out=t_ow[m * 128:(m + 1) * 128, n * TC:(n + 1) * TC],
                        in_=fl[p][:, :]), dma="o3w", waits=[S.mark("vector")])
                S.emit("vector", lambda e: e.reduce_max(acc1[:, 3:4], stmaxf[:, :], AX),
                       waits=[S.mark("vector")], inc=True)
                S.gap("vector")
                collective_scalar_max(3, 6, "m3")
                S.emit("vector", lambda e: e.tensor_scalar(
                    sc11[0:1, 6:7], sc11[0:1, 6:7], 1.0 / QMAX, 1e-12, ALU.mult, ALU.max),
                    waits=[S.dmark("ccs"), S.dmark("qb")], inc=True)
                S.gap("vector")
                S.emit("vector", lambda e: e.reciprocal(sc11[0:1, 7:8], sc11[0:1, 6:7]), inc=True)
                S.gap("vector")
                bcast_scalar(qsc[:, 7:8], sc11[0:1, 6:7], "qb", waits=[S.mark("vector")])
                bcast_scalar(qsc[:, 6:7], sc11[0:1, 7:8], "qb")

            wd_of = dequant_prep(d_wd, 8, IH, "fd")
            wt_ready = S.mark("vector")
            with ExitStack() as c7:
                oq = oq_alias
                ol = [c7.enter_context(nc.sbuf_tensor(f"ol{p}", [128, TC], F32)) for p in range(2)]
                m2l = [escr, ystg[0]]
                oq_marks = {}
                mm_done = {}
                bank_evac = {}
                for n in range(NTC):
                    for k in range(NKI):
                        p = k % 2
                        wv = [S.dmark("o3w"), S.dmark("qb")]
                        if n * NKI + k >= 2:
                            wv.append(oq_marks[n * NKI + k - 2])
                        if n > 0:
                            wv.append(mm_done.get((n - 1, 7), (None, 0)))
                        S.emit("sync", lambda e, n=n, k=k, p=p: e.dma_start(
                            out=ol[p][:, :], in_=t_ow[k * 128:(k + 1) * 128, n * TC:(n + 1) * TC]),
                            dma=f"odl{p}", waits=[x for x in wv if x[0] is not None])
                        S.emit("vector", lambda e, p=p: e.tensor_scalar(
                            aux[:, 0:TC], ol[p][:, :], qsc[:, 6:7], RND, ALU.mult, ALU.add),
                            waits=[S.dmark(f"odl{p}")], inc=True)
                        S.emit("vector", lambda e: e.tensor_scalar(
                            aux[:, 0:TC], aux[:, 0:TC], RND, QMAX, ALU.subtract, ALU.min))
                        S.emit("vector", lambda e, k=k: e.tensor_scalar(
                            oq[:, k * TC:(k + 1) * TC], aux[:, 0:TC], -QMAX - 1.0, qsc[:, 7:8],
                            ALU.max, ALU.mult), inc=True)
                        oq_marks[n * NKI + k] = S.mark("vector")
                    oqm = S.mark("vector")
                    for m in range(8):
                        bank = m
                        wm = [oqm, wt_ready]
                        prev = bank_evac.get(bank)
                        if prev is not None:
                            wm.append(prev)
                        for k in range(NKI):
                            S.emit("tensor", lambda e, m=m, k=k, bank=bank: e.matmul(
                                p_mm[bank][:, :], wT[:, wd_of(k, m):wd_of(k, m) + 128],
                                oq[:, k * TC:(k + 1) * TC],
                                start=(k == 0), stop=(k == NKI - 1)),
                                waits=wm if k == 0 else [], inc=(k == NKI - 1))
                        mm_done[(n, m)] = S.mark("tensor")
                        p = (n * 8 + m) % 2
                        S.emit("sync", lambda e, m=m, n=n, p=p: e.dma_start(
                            out=m2l[p][:, :], in_=t_m2p[m * 128:(m + 1) * 128, n * TC:(n + 1) * TC]),
                            dma=f"m2l{p}", waits=[S.dmark("op")])
                        wv = [mm_done[(n, m)], S.dmark(f"m2l{p}")]
                        if n * 8 + m >= 2:
                            wv.append(S.dmark("sumw"))
                        S.emit("vector", lambda e, bank=bank, p=p: e.tensor_tensor(
                            stg[p][:, :], p_mm[bank][:, :], m2l[p][:, :], ALU.add),
                            waits=wv, inc=True)
                        bank_evac[bank] = S.mark("vector")
                        S.emit("sync", lambda e, m=m, n=n, p=p: e.dma_start(
                            out=t_bgp[m * 128:(m + 1) * 128, n * TC:(n + 1) * TC],
                            in_=stg[p][:, :]), dma="sumw", waits=[bank_evac[bank]])
            slot_copy("scalar", "bci",
                      lambda p: t_ccbi[p * DM:(p + 1) * DM, :],
                      lambda p: t_bgp[:, :], waits=[S.dmark("sumw"), zf_done])
            S.emit("gpsimd", lambda e: e.collective_compute(
                "AllReduce", ALU.add, replica_groups=ALL8,
                ins=[t_ccbi.opt()], outs=[t_ccbo.opt()]),
                waits=[S.dmark("bci")], dma="ccc", dmainc=1)
            slot_copy("sync", "bco",
                      lambda p: t_bgf[:, :],
                      lambda p: t_ccbo[p * DM:(p + 1) * DM, :],
                      waits=[S.dmark("ccc")])

        def final_phase():
            fstart = S.mark("vector")
            with ExitStack() as c8:
                sl = [aux[:, i * 128:(i + 1) * 128] for i in range(4)]
                xl = [rhsb[p].ap().bitcast(F32)[:, 0:DM] for p in range(2)]
                ob = [dA[p].ap()[:, 0:DM] for p in range(2)]
                ntok = L // 128
                add_marks = {}
                pe_tr = {}
                for tt in range(ntok):
                    p = tt % 2
                    wv = [S.dmark("bco")]
                    if tt >= 2:
                        wv.append(add_marks[tt - 2])
                    S.emit("sync", lambda e, tt=tt, p=p: e.dma_start(
                        out=xl[p][:, :], in_=d_x[tt * 128:(tt + 1) * 128, :]), dma=f"xf{p}", waits=wv)
                    for c in range(8):
                        psl = sl[c % 4]
                        wv2 = [S.dmark("bco")]
                        if (tt * 8 + c) >= 4:
                            wv2.append(pe_tr[tt * 8 + c - 4])
                        S.emit("sync", lambda e, tt=tt, c=c, psl=psl: e.dma_start(
                            out=psl[:, :],
                            in_=t_bgf[c * 128:(c + 1) * 128, tt * 128:(tt + 1) * 128]),
                            dma=f"sf{c % 4}", waits=wv2)
                        ptp = p_tr[c % 2]
                        wv3 = [S.dmark(f"sf{c % 4}"), fstart]
                        if (tt * 8 + c) >= 2:
                            wv3.append(add_marks.get(tt - 2, (None, 0)))
                        S.emit("tensor", lambda e, psl=psl, c=c: e.transpose(
                            p_mm[c % 4][:, 0:128], psl[:, :], eye_f[:, :]),
                            waits=[x for x in wv3 if x[0] is not None], inc=True)
                        pe_tr[tt * 8 + c] = S.mark("tensor")
                        S.emit("vector", lambda e, c=c, p=p: e.tensor_tensor(
                            ob[p][:, c * 128:(c + 1) * 128], p_mm[c % 4][0:128, 0:128],
                            xl[p][:, c * 128:(c + 1) * 128], ALU.add),
                            waits=[pe_tr[tt * 8 + c], S.dmark(f"xf{p}")], inc=True)
                    add_marks[tt] = S.mark("vector")
                    S.emit("sync", lambda e, tt=tt, p=p: e.dma_start(
                        out=d_out[tt * 128:(tt + 1) * 128, :], in_=ob[p][:, :]),
                        dma="outw", waits=[add_marks[tt]])
                S.emit("sync", lambda e: e.dma_start(out=sc11[0:1, 9:10], in_=t_ccx2[0:1, 0:1]),
                       dma="fin", waits=[S.dmark("outw")])

        if STAGE >= 1:
            pass
        if STAGE >= 2:
            mamba_layer(0, t_xn)
        if STAGE >= 4:
            mamba_layer(1, t_l2in)
        if STAGE >= 5:
            ffn_phase()
        if STAGE >= 6:
            final_phase()
        for tag, (dap, src, shp, dt) in dbg_outs.items():
            name = f"dbg{tag}"
            wv = [S.mark(e) for e in ENGINES] + [S.dmark(n) for n in list(S.dsems)]
            first = True
            for r in range(0, shp[0], 128):
                rows = min(128, shp[0] - r)
                S.emit("sync", lambda e, r=r, rows=rows, dap=dap, src=src: e.dma_start(
                    out=dap[r:r + rows, :], in_=src[r:r + rows, :]),
                    dma=name, waits=wv if first else [])
                first = False
            S.emit("sync", lambda e: e.dma_start(out=sc11[0:1, 10:11], in_=t_ccx2[0:1, 0:1]),
                   dma="fin2", waits=[S.dmark(name)])
        if STAGE < 6:
            wv = [S.mark(e) for e in ENGINES] + [S.dmark("xntok")]
            for tt in range(L // 128):
                S.emit("sync", lambda e, tt=tt: e.dma_start(
                    out=d_out[tt * 128:(tt + 1) * 128, :],
                    in_=t_xn_tok[tt * 128:(tt + 1) * 128, :]),
                    dma="outw", waits=wv if tt == 0 else [])
            S.emit("sync", lambda e: e.dma_start(out=sc11[0:1, 9:10], in_=t_ccx2[0:1, 0:1]),
                   dma="fin", waits=[S.dmark("outw")])

        blk = ctx.enter_context(nc.Block())
        S.finalize(blk)
    return nc, dbg_outs


def _slice_inputs(inputs, core):
    b, j = divmod(core, 2)
    f32 = np.float32

    def resh(v):
        v = np.ascontiguousarray(v)
        if v.ndim == 1:
            return np.ascontiguousarray(v.reshape(8, 128).T)
        k = v.shape[1]
        return np.ascontiguousarray(v.reshape(8, 128, k).transpose(1, 0, 2).reshape(128, 8 * k))

    m = {}
    x = np.asarray(inputs["L0_attn_x"], f32)
    m["x_tok"] = np.ascontiguousarray(x[b])
    for i in range(2):
        inw = np.asarray(inputs["L0_attn_in_proj_w"], f32)[i]
        m[f"inw{i}"] = np.ascontiguousarray(
            np.concatenate([inw[j * DH:(j + 1) * DH], inw[2 * DH + j * DH:2 * DH + (j + 1) * DH]], 0))
        m[f"convw{i}"] = resh(np.asarray(inputs["L0_attn_conv_w"], f32)[i][j * DH:(j + 1) * DH])
        m[f"convb{i}"] = resh(np.asarray(inputs["L0_attn_conv_b"], f32)[i][j * DH:(j + 1) * DH])
        m[f"xpw{i}"] = np.ascontiguousarray(
            np.asarray(inputs["L0_attn_x_proj_w"], f32)[i][:, j * DH:(j + 1) * DH])
        m[f"dtw{i}"] = resh(np.asarray(inputs["L0_attn_dt_proj_w"], f32)[i][j * DH:(j + 1) * DH])
        m[f"dtb{i}"] = resh(np.asarray(inputs["L0_attn_dt_proj_b"], f32)[i][j * DH:(j + 1) * DH])
        m[f"alog{i}"] = resh(np.asarray(inputs["L0_attn_A_log"], f32)[i][j * DH:(j + 1) * DH])
        m[f"dskip{i}"] = resh(np.asarray(inputs["L0_attn_D_skip"], f32)[i][j * DH:(j + 1) * DH])
        m[f"outw{i}"] = np.ascontiguousarray(
            np.asarray(inputs["L0_attn_out_proj_w"], f32)[i][:, j * DH:(j + 1) * DH])
    m["wg"] = np.ascontiguousarray(np.asarray(inputs["L0_ffn_w_gate"], f32)[j * IH:(j + 1) * IH])
    m["wu"] = np.ascontiguousarray(np.asarray(inputs["L0_ffn_w_up"], f32)[j * IH:(j + 1) * IH])
    m["wd"] = np.ascontiguousarray(np.asarray(inputs["L0_ffn_w_down"], f32)[:, j * IH:(j + 1) * IH])
    m["eye_b_in"] = np.eye(128, dtype=ml_dtypes.bfloat16)
    m["eye_f_in"] = np.eye(128, dtype=np.float32)
    return m


def kernel(**inputs):
    nc, _ = build_program(dbg=DBG)
    in_maps = [_slice_inputs(inputs, c) for c in range(8)]
    res = run_bass_kernel_spmd(nc, in_maps, core_ids=list(range(8)))
    out = np.stack([np.asarray(res.results[2 * b]["out"], np.float32) for b in range(4)], 0)
    return out


# version 04
# speedup vs baseline: 1.0281x; 1.0281x over previous
import numpy as np
import ml_dtypes
from contextlib import ExitStack

import concourse.bass as bass
import concourse.mybir as mybir
from concourse.bass_utils import run_bass_kernel_spmd

F32 = mybir.dt.float32
BF16 = mybir.dt.bfloat16
AF = mybir.ActivationFunctionType
ALU = mybir.AluOpType
AX = mybir.AxisListType.X

L = 4096
DM = 1024
DH = 1024
NT = 8
NG = 8
GS = 8
TC = 512
NTC = L // TC
SEG = TC + 2
CAT = GS * SEG
IH = 1536
NKI = IH // 128
RND = 12582912.0
QMAX = 32767.0

STAGE = 6
DBG = ()

ENGINES = ("sync", "vector", "scalar", "tensor", "gpsimd")
PAIRS = [[0, 1], [2, 3], [4, 5], [6, 7]]
ALL8 = [list(range(8))]


class Sched:
    def __init__(self, nc, ctx):
        self.nc = nc
        self.ops = {e: [] for e in ENGINES}
        self.psem = {e: ctx.enter_context(nc.semaphore(f"sp_{e}")) for e in ENGINES}
        self.pcnt = {e: 0 for e in ENGINES}
        self.dsems = {}
        self.ctx = ctx

    def _dsem(self, name):
        name = name.replace("L0", "").replace("L1", "")
        if name not in self.dsems:
            h = self.ctx.enter_context(self.nc.semaphore(f"sd_{name}"))
            self.dsems[name] = [h, 0]
        return self.dsems[name]

    def emit(self, eng, fn, waits=(), inc=False, dma=None, dmainc=16):
        incs = []
        if inc:
            self.pcnt[eng] += 1
            incs.append((self.psem[eng], 1))
        if dma is not None:
            s = self._dsem(dma)
            s[1] += dmainc
            incs.append((s[0], dmainc))
        ws = tuple((h, v) for (h, v) in waits if v > 0)
        self.ops[eng].append((fn, ws, tuple(incs)))

    def mark(self, eng):
        return (self.psem[eng], self.pcnt[eng])

    def gap(self, eng, n=6):
        for _ in range(n):
            self.emit(eng, lambda e: e.engine_nop())

    def dmark(self, name):
        s = self._dsem(name)
        return (s[0], s[1])

    def finalize(self, block):
        def mk(name):
            def body(eng):
                for fn, waits, incs in self.ops[name]:
                    for (h, v) in waits:
                        eng.wait_ge(h, v)
                    ins = fn(eng)
                    for (h, v) in incs:
                        ins = ins.then_inc(h, v)
            return body
        block.sync(mk("sync"))
        block.vector(mk("vector"))
        block.scalar(mk("scalar"))
        block.tensor(mk("tensor"))
        block.gpsimd(mk("gpsimd"))


def seg3(t, c0, c1):
    return t.ap().rearrange("p (s c) -> p s c", c=SEG)[:, :, c0:c1]


def build_program(dbg=()):
    nc = bass.Bass("TRN2", target_bir_lowering=False, debug=False)

    d_x = nc.dram_tensor("x_tok", [L, DM], F32, kind="ExternalInput").ap()
    d_inw = [nc.dram_tensor(f"inw{i}", [2 * DH, DM], F32, kind="ExternalInput").ap() for i in range(2)]
    d_convw = [nc.dram_tensor(f"convw{i}", [128, NT * 4], F32, kind="ExternalInput").ap() for i in range(2)]
    d_convb = [nc.dram_tensor(f"convb{i}", [128, NT], F32, kind="ExternalInput").ap() for i in range(2)]
    d_xpw = [nc.dram_tensor(f"xpw{i}", [192, DH], F32, kind="ExternalInput").ap() for i in range(2)]
    d_dtw = [nc.dram_tensor(f"dtw{i}", [128, NT * 64], F32, kind="ExternalInput").ap() for i in range(2)]
    d_dtb = [nc.dram_tensor(f"dtb{i}", [128, NT], F32, kind="ExternalInput").ap() for i in range(2)]
    d_alog = [nc.dram_tensor(f"alog{i}", [128, NT * 64], F32, kind="ExternalInput").ap() for i in range(2)]
    d_dskip = [nc.dram_tensor(f"dskip{i}", [128, NT], F32, kind="ExternalInput").ap() for i in range(2)]
    d_outw = [nc.dram_tensor(f"outw{i}", [DM, DH], F32, kind="ExternalInput").ap() for i in range(2)]
    d_wg = nc.dram_tensor("wg", [IH, DM], F32, kind="ExternalInput").ap()
    d_wu = nc.dram_tensor("wu", [IH, DM], F32, kind="ExternalInput").ap()
    d_wd = nc.dram_tensor("wd", [DM, IH], F32, kind="ExternalInput").ap()
    d_eyeb = nc.dram_tensor("eye_b_in", [128, 128], BF16, kind="ExternalInput").ap()
    d_eyef = nc.dram_tensor("eye_f_in", [128, 128], F32, kind="ExternalInput").ap()
    d_out = nc.dram_tensor("out", [L, DM], F32, kind="ExternalOutput").ap()

    ds = lambda n, shp, dt=F32: nc.dram_tensor(n, shp, dt).ap()
    t_xn_tok = ds("s_xn_tok", [L, DM])
    t_xn = ds("s_xn", [DM, L], BF16)
    t_xq = ds("s_xq", [DM, L], BF16)
    t_uraw = ds("s_uraw", [DH, L])
    t_uc = ds("s_uc", [DH, L], BF16)
    t_z = ds("s_z", [DH, L], BF16)
    t_y = ds("s_y", [DH, L])
    t_m2p = ds("s_m2p", [DM, L])
    t_l2in = ds("s_l2in", [DM, L], BF16)
    t_ccsi = nc.dram_tensor("cc_sm_in", [4 * 192, L], F32).ap()
    t_ccso = nc.dram_tensor("cc_sm_out", [4 * 192, L], F32, addr_space="Shared").ap()
    t_ccbi_a = nc.dram_tensor("cc_big_in_a", [4 * DM, L // 2], F32).ap()
    t_ccbi_b = nc.dram_tensor("cc_big_in_b", [4 * DM, L // 2], F32).ap()
    t_ccbo_a = nc.dram_tensor("cc_big_out_a", [4 * DM, L // 2], F32, addr_space="Shared").ap()
    t_ccbo_b = nc.dram_tensor("cc_big_out_b", [4 * DM, L // 2], F32, addr_space="Shared").ap()
    t_ccli_a = nc.dram_tensor("cc_l2_in_a", [4 * DM, L // 2], BF16).ap()
    t_ccli_b = nc.dram_tensor("cc_l2_in_b", [4 * DM, L // 2], BF16).ap()
    t_cclo_a = nc.dram_tensor("cc_l2_out_a", [4 * DM, L // 2], BF16, addr_space="Shared").ap()
    t_cclo_b = nc.dram_tensor("cc_l2_out_b", [4 * DM, L // 2], BF16, addr_space="Shared").ap()
    t_bg1 = ds("s_bg1", [DM, L], BF16)
    t_ccx1 = ds("cc_x_in", [1, 4])
    t_ccx2 = nc.dram_tensor("cc_x_out", [1, 4], F32, addr_space="Shared").ap()
    t_grw = ds("s_gate", [IH, L])
    t_upw = ds("s_up", [IH, L])
    t_g2w = ds("s_gate2", [IH, L])
    t_ow = ds("s_o", [IH, L])
    t_Bc = ds("s_Bc", [64, L], BF16)
    t_Cc = ds("s_Cc", [64, L], BF16)
    t_smp = ds("s_smp", [192, L])
    t_smf = ds("s_smf", [192, L])
    t_bgp = ds("s_bgp", [DM, L])
    t_bgf = ds("s_bgf", [DM, L])

    dbg_specs = {"xn": (t_xn, [DM, L], BF16), "uraw": (t_uraw, [DH, L], F32),
                 "uc": (t_uc, [DH, L], BF16), "z": (t_z, [DH, L], BF16),
                 "xdbl": (t_ccso, [192, L], F32), "y": (t_y, [DH, L], F32),
                 "m1": (t_bgf, [DM, L], F32), "xq": (t_xq, [DM, L], BF16),
                 "gate": (t_grw, [IH, L], F32), "up": (t_upw, [IH, L], F32),
                 "g2": (t_g2w, [IH, L], F32), "o": (t_ow, [IH, L], F32),
                 "m2p": (t_m2p, [DM, L], F32), "xntok": (t_xn_tok, [L, DM], F32)}
    DBG2 = {}
    if "sp" in dbg:
        DBG2["sp"] = nc.dram_tensor("dbg2_sp", [128, NT * TC], F32, kind="ExternalOutput").ap()
        DBG2["q"] = nc.dram_tensor("dbg2_q", [128, NT * TC], BF16, kind="ExternalOutput").ap()
        DBG2["an"] = nc.dram_tensor("dbg2_an", [128, NT * 64], F32, kind="ExternalOutput").ap()
        DBG2["bb"] = nc.dram_tensor("dbg2_bb", [128, CAT], BF16, kind="ExternalOutput").ap()
        DBG2["cb"] = nc.dram_tensor("dbg2_cb", [128, CAT], BF16, kind="ExternalOutput").ap()
    if "ys" in dbg:
        DBG2["ys"] = nc.dram_tensor("dbg2_ys", [128, TC], F32, kind="ExternalOutput").ap()
        DBG2["uc0"] = nc.dram_tensor("dbg2_uc0", [128, TC], BF16, kind="ExternalOutput").ap()
    if "pp" in dbg:
        DBG2["pp"] = nc.dram_tensor("dbg2_pp", [128, TC], F32, kind="ExternalOutput").ap()
    if "hc" in dbg:
        DBG2["hc"] = nc.dram_tensor("dbg2_hc", [128, CAT], BF16, kind="ExternalOutput").ap()
        DBG2["vc"] = nc.dram_tensor("dbg2_vc", [128, CAT], BF16, kind="ExternalOutput").ap()
        DBG2["bc2"] = nc.dram_tensor("dbg2_bc2", [128, CAT], BF16, kind="ExternalOutput").ap()
        DBG2["da"] = nc.dram_tensor("dbg2_da", [128, CAT], F32, kind="ExternalOutput").ap()
    dbg_outs = {}
    for tag in dbg:
        if tag in ("sp", "hc", "ys", "pp"):
            continue
        src, shp, dt = dbg_specs[tag]
        dbg_outs[tag] = (nc.dram_tensor("dbg_" + tag, shp, dt, kind="ExternalOutput").ap(), src, shp, dt)

    with ExitStack() as ctx:
        sbt = lambda n, shp, dt=F32: ctx.enter_context(nc.sbuf_tensor(n, shp, dt))
        pst = lambda n, shp, dt=F32: ctx.enter_context(nc.psum_tensor(n, shp, dt))
        S = Sched(nc, ctx)

        eye_b = sbt("eye_b", [128, 128], BF16)
        eye_f = sbt("eye_f", [128, 128], F32)
        convw = sbt("convw", [128, NT * 4])
        convb = sbt("convb", [128, NT])
        dtb = sbt("dtb", [128, NT])
        dskip = sbt("dskip", [128, NT])
        Aneg = sbt("Aneg", [128, NT * 64])
        carry = sbt("carry", [128, NT * 64], BF16)
        qsc = sbt("qsc", [128, 8])
        sc11 = sbt("sc11", [1, 16])
        cst = sbt("cstc", [128, 4])

        aux = sbt("aux", [128, 1024])
        auxb = sbt("auxb", [128, 1024], BF16)
        acc1 = sbt("acc1", [128, 4])
        st1 = sbt("st1", [128, 4])
        stmaxf = sbt("stmaxf", [128, 128])

        p_mm = [pst(f"p_mm{i}", [128, 512]) for i in range(8)]
        p_tr = [p_mm[6].ap().bitcast(BF16)[:, 0:128], p_mm[7].ap().bitcast(BF16)[:, 0:128]]

        def load_const(dst, src, name, waits=()):
            S.emit("sync", lambda e, d=dst, s=src: e.dma_start(out=d[:, :], in_=s),
                   dma=name, waits=waits)

        def bcast_scalar(dst_col, src_row, name, waits=()):
            S.emit("sync", lambda e, d=dst_col, s=src_row: e.dma_start(
                out=d, in_=s.unsqueeze(1).broadcast_to([1, 128, 1])), dma=name, waits=waits)

        def prep_lhsT(w_dram, rows, name, out_base=0, ktiles=8, eng="vector"):
            nm = rows // 128

            def cp(e, dst, src):
                if eng == "scalar":
                    return e.activation(dst, src, AF.Copy)
                return e.tensor_copy(dst, src)
            for m in range(nm):
                pw = m % 2
                w = [S.mark("vector"), S.mark("scalar")] if m == 0 else [prep_marks[-1]]
                S.emit("sync", lambda e, m=m, pw=pw: e.dma_start(
                    out=wld[pw][:, 0:ktiles * 128], in_=w_dram[m * 128:(m + 1) * 128, :]),
                    dma="wp", waits=w)
                S.emit(eng, lambda e, pw=pw: cp(
                    e, wldb[:, 0:ktiles * 128], wld[pw][:, 0:ktiles * 128]),
                    waits=[S.dmark("wp")], inc=True)
                cast_mark = S.mark(eng)
                for k in range(ktiles):
                    pt = p_tr[k % 2]
                    wtp = [cast_mark]
                    if m * ktiles + k >= 2:
                        wtp.append(tr_evac_marks[-2])
                    S.emit("tensor", lambda e, k=k, pt=pt: e.transpose(
                        pt[:, :], wldb[:, k * 128:(k + 1) * 128], eye_b[:, :]),
                        waits=wtp, inc=True)
                    col = out_base + (k * nm + m) * 128
                    S.emit(eng, lambda e, col=col, pt=pt: cp(
                        e, wT[:, col:col + 128], pt[:, :]), waits=[S.mark("tensor")], inc=True)
                    tr_evac_marks.append(S.mark(eng))
                prep_marks.append(S.mark(eng))
            return lambda k, m: out_base + (k * nm + m) * 128

        tr_evac_marks = []
        prep_marks = []

        def mm_phase(rhs_dram, nk, nm, wt_of, evac, tag, extra_w=(), rhs_ktiles=None,
                     evac_eng="vector", rhs_w=None, first_w=()):
            kt = rhs_ktiles if rhs_ktiles is not None else nk
            rhs_marks = {}
            grp_marks = {}
            bank_evac = {}
            for n in range(NTC):
                p = n % 2
                w = list(extra_w) if n == 0 else []
                if rhs_w is not None:
                    w += list(rhs_w(n))
                if n >= 2:
                    w.append(grp_marks[(n - 2, nm - 1)])
                for k in range(kt):
                    S.emit("sync", lambda e, n=n, k=k, p=p: e.dma_start(
                        out=rhsb[p][:, k * TC:(k + 1) * TC],
                        in_=rhs_dram[k * 128:(k + 1) * 128, n * TC:(n + 1) * TC]),
                        dma=f"rhs{p}", waits=w if k == 0 else [])
                rhs_marks[n] = S.dmark(f"rhs{p}")
                for m in range(nm):
                    bank = m % 8
                    wm = [rhs_marks[n]]
                    if (n, m) == (0, 0):
                        wm += list(first_w)
                    if (n, m) != (0, 0) and (n * nm + m) >= (8 if nm >= 8 else nm):
                        prev = bank_evac.get(bank)
                        if prev is not None:
                            wm.append(prev)
                    for k in range(nk):
                        S.emit("tensor", lambda e, m=m, k=k, bank=bank, p=p: e.matmul(
                            p_mm[bank][:, :], wT[:, wt_of(k, m):wt_of(k, m) + 128],
                            rhsb[p][:, (k % kt) * TC:((k % kt) + 1) * TC],
                            start=(k == 0), stop=(k == nk - 1)),
                            waits=wm if k == 0 else [], inc=(k == nk - 1))
                    grp_marks[(n, m)] = S.mark("tensor")
                    evac(m, n, bank, grp_marks[(n, m)])
                    bank_evac[bank] = S.mark(evac_eng)

        dyn = {}

        def _conds(e, queue):
            key = f"conds_{queue}"
            if key not in dyn:
                b = e.compute_val(e.partition_id() // 2)
                dyn[key] = [e.compute_val(b == p) for p in range(4)]
            return dyn[key]

        def slot_copy(queue, name, dst_of_p, src_of_p, waits=()):
            for p in range(4):
                S.emit(queue, lambda e, p=p, queue=queue: e.dma_start(
                    out=dst_of_p(p), in_=src_of_p(p), cond=_conds(e, queue)[p]),
                    dma=name, waits=waits if p == 0 else [])

        load_const(eye_b, d_eyeb, "cst")
        load_const(eye_f, d_eyef, "cst")
        cst_done = S.dmark("cst")
        S.emit("vector", lambda e: e.memset(acc1[:, :], 0.0), inc=True)
        S.emit("vector", lambda e: e.memset(cst[:, 0:1], 1e-5), inc=True)
        S.emit("vector", lambda e: e.memset(cst[:, 1:2], 1.0 / DM), inc=True)
        S.emit("vector", lambda e: e.memset(cst[:, 2:3], 1.0), inc=True)

        dA = [sbt(f"dA{p}", [128, CAT]) for p in range(2)]
        bcat = sbt("bcat", [128, CAT], BF16)
        hcat = sbt("hcat", [128, CAT], BF16)
        vcat = sbt("vcat", [128, CAT], BF16)
        Bbc = [sbt(f"Bbc{p}", [128, CAT], BF16) for p in range(2)]
        Cbc = [sbt(f"Cbc{p}", [128, CAT], BF16) for p in range(2)]
        sp_all = sbt("sp_all", [128, NT * TC])
        q_all = sbt("q_all", [128, NT * TC], BF16)
        dtraw = sbt("dtraw", [64, L], BF16)
        escr = sbt("escr", [128, TC])
        dtwT = sbt("dtwT", [64, NT * 128], BF16)

        wld = [sbt(f"wld{p}", [128, 1024]) for p in range(2)]
        wldb = sbt("wldb", [128, 2048], BF16)
        wT = sbt("wT", [128, 12288], BF16)
        rhsb = [sbt(f"rhsb{p}", [128, NT * TC], BF16) for p in range(2)]
        stg = [sbt(f"stg{p}", [128, TC]) for p in range(2)]
        stgb = [sbt(f"stgb{p}", [128, TC], BF16) for p in range(2)]
        zb = sbt("zb", [128, TC], BF16)
        for p in range(2):
            S.emit("vector", lambda e, p=p: e.memset(dA[p][:, :], 0.0), inc=True)
        S.emit("vector", lambda e: e.memset(bcat[:, :], 0.0), inc=True)
        ucc = rhsb
        ystg = stg
        S.emit("vector", lambda e: e.memset(zb[:, :], 0.0), inc=True)
        zf_mark = S.mark("vector")
        zbf = zb.ap().bitcast(F32)[:, 0:256]
        for ti, t4 in enumerate([t_ccbi_a, t_ccbi_b]):
            for rg in range(4 * DM // 128):
                S.emit("gpsimd", lambda e, rg=rg, t4=t4: e.dma_start(
                    out=t4[rg * 128:(rg + 1) * 128, :],
                    in_=zbf.unsqueeze(1).broadcast_to([128, 8, 256])),
                    dma="zf", waits=[zf_mark] if (ti == 0 and rg == 0) else [])
        for t4 in [t_ccli_a, t_ccli_b]:
            for rg in range(4 * DM // 128):
                S.emit("gpsimd", lambda e, rg=rg, t4=t4: e.dma_start(
                    out=t4[rg * 128:(rg + 1) * 128, :],
                    in_=zb[:, :].unsqueeze(1).broadcast_to([128, 4, TC])),
                    dma="zf")
        for rg in range(6):
            S.emit("gpsimd", lambda e, rg=rg: e.dma_start(
                out=t_ccsi[rg * 128:(rg + 1) * 128, :],
                in_=zbf.unsqueeze(1).broadcast_to([128, 16, 256])),
                dma="zf")
        zf_done = S.dmark("zf")
        gt_alias = q_all
        oq_alias = sp_all.ap().bitcast(BF16)

        NXB = 3
        xt = [Bbc[0].ap().bitcast(F32)[:, 0:DM], Bbc[1].ap().bitcast(F32)[:, 0:DM],
              vcat.ap().bitcast(F32)[:, 0:DM]]
        xc = [Cbc[p].ap().bitcast(F32)[:, 0:DM] for p in range(2)]
        sq = aux
        xnb = hcat.ap()[:, 0:DM]
        mstat = sbt("mstat", [128, 8])
        stmax = sbt("stmax", [128, 32])
        ntok = L // 128
        tile_done = {}
        xn_tile = {}

        def fm_pass(tt, out_dram, name, extra_first=()):
            for cblk in range(8):
                pt = p_tr[cblk % 2]
                wtp = [S.mark("vector")]
                if cblk < 2:
                    wtp += list(extra_first)
                if tt == 0 and cblk == 0:
                    wtp.append(cst_done)
                if cblk >= 2:
                    wtp.append(tr_evac_marks[-2])
                S.emit("tensor", lambda e, cblk=cblk, pt=pt: e.transpose(
                    pt[:, :], xnb[:, cblk * 128:(cblk + 1) * 128], eye_b[:, :]),
                    waits=wtp, inc=True)
                wev = [S.mark("tensor")]
                if cblk == 0:
                    wev.append(S.dmark(name))
                S.emit("vector", lambda e, cblk=cblk, pt=pt: e.tensor_copy(
                    auxb[:, cblk * 128:(cblk + 1) * 128], pt[:, :]),
                    waits=wev, inc=True)
                tr_evac_marks.append(S.mark("vector"))
            evm = S.mark("vector")
            out3 = out_dram.rearrange("(c p) l -> p c l", p=128)[:, :, tt * 128:(tt + 1) * 128]
            in3 = auxb.ap().rearrange("p (c w) -> p c w", w=128)
            S.emit("sync", lambda e, out3=out3, in3=in3: e.dma_start(
                out=out3, in_=in3), dma=name, waits=[evm])

        for tt in range(ntok):
            p = tt % NXB
            pc = tt % 2
            w = [tile_done[tt - NXB]] if tt >= NXB else []
            S.emit("sync", lambda e, tt=tt, p=p: e.dma_start(
                out=xt[p][:, :], in_=d_x[tt * 128:(tt + 1) * 128, :]), dma=f"xld{p}", waits=w)
            wldx = S.dmark(f"xld{p}")
            S.emit("vector", lambda e, p=p, pc=pc: e.reduce_sum(
                mstat[:, 4 * pc:4 * pc + 1], xt[p][:, :], AX), waits=[wldx], inc=True)
            S.emit("scalar", lambda e, pc=pc: e.activation(
                mstat[:, 4 * pc + 1:4 * pc + 2], mstat[:, 4 * pc:4 * pc + 1],
                AF.Copy, scale=cst[:, 1:2]), waits=[S.mark("vector")], inc=True)
            S.emit("vector", lambda e, p=p, pc=pc: e.tensor_scalar(
                xc[pc][:, :], xt[p][:, :], mstat[:, 4 * pc + 1:4 * pc + 2], None,
                ALU.subtract), waits=[S.mark("scalar")], inc=True)
            S.emit("scalar", lambda e, pc=pc: e.activation(sq[:, :], xc[pc][:, :], AF.Square),
                   waits=[S.mark("vector")], inc=True)
            S.emit("vector", lambda e, pc=pc: e.reduce_sum(
                mstat[:, 4 * pc + 2:4 * pc + 3], sq[:, :], AX),
                waits=[S.mark("scalar")], inc=True)
            S.emit("scalar", lambda e, pc=pc: e.activation(
                mstat[:, 4 * pc + 3:4 * pc + 4], mstat[:, 4 * pc + 2:4 * pc + 3],
                AF.Sqrt, bias=cst[:, 0:1], scale=cst[:, 1:2]),
                waits=[S.mark("vector")], inc=True)
            S.emit("vector", lambda e, pc=pc: e.reciprocal(
                mstat[:, 4 * pc:4 * pc + 1], mstat[:, 4 * pc + 3:4 * pc + 4]),
                waits=[S.mark("scalar")], inc=True)
            S.gap("vector")
            S.emit("vector", lambda e, pc=pc: e.tensor_scalar(
                xc[pc][:, :], xc[pc][:, :], mstat[:, 4 * pc:4 * pc + 1], None,
                ALU.mult), inc=True)
            xn_mark = S.mark("vector")
            S.emit("vector", lambda e, tt=tt, pc=pc: e.reduce_max(
                stmax[:, tt:tt + 1], xc[pc][:, :], AX, apply_absolute_value=True))
            S.emit("sync", lambda e, tt=tt, pc=pc: e.dma_start(
                out=t_xn_tok[tt * 128:(tt + 1) * 128, :], in_=xc[pc][:, :]),
                dma="xntok", waits=[xn_mark])
            S.emit("vector", lambda e, pc=pc: e.tensor_copy(xnb[:, :], xc[pc][:, :]),
                   waits=[S.mark("tensor")], inc=True)
            fm_pass(tt, t_xn, "xnfm")
            xn_tile[tt] = S.dmark("xnfm")
            tile_done[tt] = S.mark("vector")

        S.emit("vector", lambda e: e.reduce_max(acc1[:, 0:1], stmax[:, :], AX),
               waits=[S.mark("vector")], inc=True)
        S.gap("vector")
        S.emit("tensor", lambda e: e.transpose(p_mm[0][0:1, 0:128],
                                               acc1[:, 0:1], eye_f[:, :]),
               waits=[S.mark("vector")], inc=True)
        S.emit("vector", lambda e: e.reduce_max(sc11[0:1, 0:1], p_mm[0][0:1, 0:128], AX),
               waits=[S.mark("tensor")], inc=True)
        maxred_mark = S.mark("vector")
        S.emit("sync", lambda e: e.dma_start(out=t_ccx1[0:1, 0:1], in_=sc11[0:1, 0:1]),
               dma="ccs", waits=[maxred_mark])
        S.emit("gpsimd", lambda e: e.collective_compute(
            "AllReduce", ALU.max, replica_groups=ALL8,
            ins=[t_ccx1.opt()], outs=[t_ccx2.opt()]),
            waits=[S.dmark("ccs")], dma="ccse", dmainc=1)
        S.emit("sync", lambda e: e.dma_start(out=sc11[0:1, 1:2], in_=t_ccx2[0:1, 0:1]),
               dma="ccs", waits=[S.dmark("ccse")])
        S.emit("vector", lambda e: e.tensor_scalar(
            sc11[0:1, 2:3], sc11[0:1, 1:2], 1.0 / QMAX, 1e-12, ALU.mult, ALU.max),
            waits=[S.dmark("ccs")], inc=True)
        S.gap("vector")
        S.emit("vector", lambda e: e.reciprocal(sc11[0:1, 3:4], sc11[0:1, 2:3]), inc=True)
        S.gap("vector")
        bcast_scalar(qsc[:, 1:2], sc11[0:1, 2:3], "qb", waits=[S.mark("vector")])
        bcast_scalar(qsc[:, 0:1], sc11[0:1, 3:4], "qb")

        def emit_quant(mu_done):
            for tt in range(ntok):
                p = tt % NXB
                pc = tt % 2
                w = [S.dmark("qb"), S.dmark("xntok")] if tt == 0 else []
                if tt >= NXB:
                    w.append(tile_done[ntok + tt - NXB])
                S.emit("sync", lambda e, tt=tt, p=p: e.dma_start(
                    out=xt[p][:, :], in_=t_xn_tok[tt * 128:(tt + 1) * 128, :]),
                    dma=f"xld{p}", waits=w)
                S.emit("vector", lambda e, p=p, pc=pc: e.tensor_scalar(
                    xc[pc][:, :], xt[p][:, :], qsc[:, 0:1], RND, ALU.mult, ALU.add),
                    waits=[S.dmark(f"xld{p}")], inc=True)
                S.emit("vector", lambda e, pc=pc: e.tensor_scalar(
                    xc[pc][:, :], xc[pc][:, :], RND, QMAX, ALU.subtract, ALU.min))
                S.emit("vector", lambda e, pc=pc: e.tensor_scalar(
                    xc[pc][:, :], xc[pc][:, :], -QMAX - 1.0, qsc[:, 1:2], ALU.max, ALU.mult))
                S.emit("vector", lambda e, pc=pc: e.tensor_copy(xnb[:, :], xc[pc][:, :]),
                       waits=[S.mark("tensor")], inc=True)
                fm_pass(tt, t_xq, "xqfm", extra_first=[mu_done] if tt == 0 else ())
                tile_done[ntok + tt] = S.mark("vector")

        def scan_phase(i, tag, dtraw_mark, dtwT_mark, bc_done):
            k_idx = [0]
            q_marks = {}
            scan_marks = {}
            v_marks = {}
            pe_marks = {}
            btt_marks = {}
            evac_marks = {}
            dma_y_name = f"ydma{tag}"

            rhs_free = S.mark("tensor")
            for tc in range(NTC):
                ptc = tc % 2
                w = [S.dmark(f"uc{tag}")]
                if tc < 2:
                    w.append(rhs_free)
                if tc >= 2:
                    w.append(evac_marks[(tc - 2, NT - 1)])
                for dt_i in range(NT):
                    S.emit("sync", lambda e, tc=tc, dt_i=dt_i, ptc=ptc: e.dma_start(
                        out=ucc[ptc][:, dt_i * TC:(dt_i + 1) * TC],
                        in_=t_uc[dt_i * 128:(dt_i + 1) * 128, tc * TC:(tc + 1) * TC]),
                        dma=f"ucc{ptc}", waits=w if dt_i == 0 else [])
                ucc_mark = S.dmark(f"ucc{ptc}")
                prev_q_mark = q_marks.get(tc - 1)
                for dt_i in range(NT):
                    bank = dt_i
                    wm = [dtraw_mark, dtwT_mark]
                    if tc > 0:
                        wm.append(evac_marks[(tc - 1, dt_i)])
                    S.emit("tensor", lambda e, tc=tc, dt_i=dt_i, bank=bank: e.matmul(
                        p_mm[bank][:, :], dtwT[0:64, dt_i * 128:(dt_i + 1) * 128],
                        dtraw[0:64, tc * TC:(tc + 1) * TC], start=True, stop=True),
                        waits=wm, inc=True)
                    wsc = [S.mark("tensor")]
                    if dt_i == 0 and prev_q_mark is not None:
                        wsc.append(prev_q_mark)
                    S.emit("scalar", lambda e, dt_i=dt_i, bank=bank: e.activation(
                        escr[:, :], p_mm[bank][:, :], AF.Exp, bias=dtb[:, dt_i:dt_i + 1]),
                        waits=wsc, inc=True)
                    S.emit("scalar", lambda e, dt_i=dt_i: e.activation(
                        sp_all[:, dt_i * TC:(dt_i + 1) * TC], escr[:, :], AF.Ln,
                        bias=cst[:, 2:3]), inc=True)
                    S.emit("vector", lambda e, dt_i=dt_i, ptc=ptc: e.tensor_tensor(
                        q_all[:, dt_i * TC:(dt_i + 1) * TC],
                        sp_all[:, dt_i * TC:(dt_i + 1) * TC],
                        ucc[ptc][:, dt_i * TC:(dt_i + 1) * TC], ALU.mult),
                        waits=[S.mark("scalar"), ucc_mark], inc=True)
                q_mark = S.mark("vector")
                q_marks[tc] = q_mark
                sp_mark = S.mark("scalar")
                if tc == 0 and i == 0 and "sp" in DBG2:
                    S.emit("sync", lambda e: e.dma_start(out=DBG2["sp"], in_=sp_all[:, :]),
                           dma="dbgsp", waits=[sp_mark, q_mark])
                    S.emit("sync", lambda e: e.dma_start(out=DBG2["q"], in_=q_all[:, :]),
                           dma="dbgsp", waits=[sp_mark, q_mark])
                    S.emit("sync", lambda e: e.dma_start(out=DBG2["an"], in_=Aneg[:, :]),
                           dma="dbgsp", waits=[sp_mark, q_mark])

                import os as _os2
                NGl = int(_os2.environ.get("DBGNG", str(NG)))
                for g in range(NGl):
                    pg = (tc * NG + g) % 2
                    gi = tc * NG + g
                    wB = [q_mark, bc_done]
                    if gi >= 2:
                        wB.append(btt_marks.get(gi - 2, S.mark("vector")))
                    for seg in range(GS):
                        srow = g * GS + seg
                        S.emit("sync", lambda e, srow=srow, seg=seg, pg=pg, tc=tc: e.dma_start(
                            out=Bbc[pg][:, seg * SEG + 2:seg * SEG + 2 + TC],
                            in_=t_Bc[srow:srow + 1, tc * TC:(tc + 1) * TC]
                            .unsqueeze(1).broadcast_to([1, 128, TC])),
                            dma=f"bb{pg}", waits=wB if seg == 0 else [])
                    wC = [bc_done]
                    if gi >= 2:
                        wC.append(v_marks.get((gi - 2, NT - 1), S.mark("vector")))
                    for seg in range(GS):
                        srow = g * GS + seg
                        S.emit("gpsimd", lambda e, srow=srow, seg=seg, pg=pg, tc=tc: e.dma_start(
                            out=Cbc[pg][:, seg * SEG + 2:seg * SEG + 2 + TC],
                            in_=t_Cc[srow:srow + 1, tc * TC:(tc + 1) * TC]
                            .unsqueeze(1).broadcast_to([1, 128, TC])),
                            dma=f"cb{pg}", waits=wC if seg == 0 else [])
                    bbc_mark = S.dmark(f"bb{pg}")
                    cbc_mark = S.dmark(f"cb{pg}")
                    if tc == 0 and g == 0 and i == 0 and "sp" in DBG2:
                        S.emit("sync", lambda e: e.dma_start(out=DBG2["bb"], in_=Bbc[0][:, :]),
                               dma="dbgsp", waits=[bbc_mark, cbc_mark])
                        S.emit("sync", lambda e: e.dma_start(out=DBG2["cb"], in_=Cbc[0][:, :]),
                               dma="dbgsp", waits=[bbc_mark, cbc_mark])

                    for dt_i in range(NT):
                        k = k_idx[0]
                        pk = k % 2
                        import os as _os
                        _ser = bool(_os.environ.get("DBGSER"))
                        wA = [sp_mark]
                        if _ser and k >= 1:
                            wA.append(scan_marks[k - 1])
                        if k >= 2:
                            wA.append(scan_marks[k - 2])
                        for seg in range(GS):
                            s_idx = g * GS + seg
                            S.emit("scalar", lambda e, dt_i=dt_i, seg=seg, s_idx=s_idx, pk=pk:
                                   e.activation(
                                       dA[pk][:, seg * SEG + 2:seg * SEG + 2 + TC],
                                       sp_all[:, dt_i * TC:(dt_i + 1) * TC], AF.Exp,
                                       scale=Aneg[:, dt_i * 64 + s_idx:dt_i * 64 + s_idx + 1]),
                                   waits=wA if seg == 0 else [], inc=(seg == GS - 1))
                        dA_mark = S.mark("scalar")
                        wD = [bbc_mark, dA_mark]
                        S.emit("vector", lambda e, dt_i=dt_i, g=g: e.tensor_copy(
                            seg3(bcat, 1, 2),
                            carry[:, dt_i * 64 + g * GS:dt_i * 64 + (g + 1) * GS].unsqueeze(2)),
                            waits=wD)
                        S.emit("vector", lambda e, dt_i=dt_i, pg=pg: e.tensor_tensor(
                            seg3(bcat, 2, SEG),
                            q_all[:, dt_i * TC:(dt_i + 1) * TC].unsqueeze(1).broadcast_to([128, GS, TC]),
                            seg3(Bbc[pg], 2, SEG), ALU.mult), inc=True)
                        btt_marks[gi] = S.mark("vector")
                        S.emit("vector", lambda e, pk=pk: e.tensor_tensor_scan(
                            hcat[:, :], dA[pk][:, :], bcat[:, :], 0.0, ALU.mult, ALU.add),
                            inc=True)
                        scan_marks[k] = S.mark("vector")
                        wV = [cbc_mark]
                        if k >= 1:
                            wV.append(pe_marks[k - 1])
                        S.emit("vector", lambda e, pg=pg: e.tensor_tensor(
                            seg3(vcat, 2, SEG), seg3(hcat, 2, SEG),
                            seg3(Cbc[pg], 2, SEG), ALU.mult), waits=wV)
                        S.emit("vector", lambda e, dt_i=dt_i, g=g: e.tensor_copy(
                            carry[:, dt_i * 64 + g * GS:dt_i * 64 + (g + 1) * GS].unsqueeze(2),
                            seg3(hcat, SEG - 1, SEG)), inc=True)
                        v_marks[(gi, dt_i)] = S.mark("vector")
                        import os
                        _dbgg = int(os.environ.get("DBGG", "0"))
                        if tc == 0 and g == _dbgg and dt_i == 0 and i == 0 and "hc" in DBG2:
                            S.emit("sync", lambda e: e.dma_start(out=DBG2["hc"], in_=hcat[:, :]),
                                   dma="dbgh", waits=[v_marks[(gi, dt_i)]])
                            S.emit("sync", lambda e: e.dma_start(out=DBG2["vc"], in_=vcat[:, :]),
                                   dma="dbgh", waits=[v_marks[(gi, dt_i)]])
                            S.emit("sync", lambda e: e.dma_start(out=DBG2["bc2"], in_=bcat[:, :]),
                                   dma="dbgh", waits=[v_marks[(gi, dt_i)]])
                            S.emit("sync", lambda e, pk=pk: e.dma_start(out=DBG2["da"], in_=dA[pk][:, :]),
                                   dma="dbgh", waits=[v_marks[(gi, dt_i)]])
                            S.emit("vector", lambda e: e.engine_nop(),
                                   waits=[S.dmark("dbgh")], inc=True)
                        for seg in range(GS):
                            S.emit("tensor", lambda e, seg=seg, dt_i=dt_i, g=g, NGl=NGl:
                                   e.matmul(
                                       p_mm[dt_i][:, :], eye_b[:, :],
                                       vcat[:, seg * SEG + 2:seg * SEG + 2 + TC],
                                       start=(g == 0 and seg == 0),
                                       stop=(g == NGl - 1 and seg == GS - 1)),
                                   waits=[v_marks[(gi, dt_i)]] if seg == 0 else [],
                                   inc=(seg == GS - 1))
                        pe_marks[k] = S.mark("tensor")
                        if tc == 0 and g == int(_os2.environ.get("DBGPP", "0")) and dt_i == 0 and i == 0 and "pp" in DBG2:
                            S.emit("vector", lambda e: e.tensor_copy(
                                escr[:, :], p_mm[0][:, :]), waits=[pe_marks[k]], inc=True)
                            S.emit("sync", lambda e: e.dma_start(out=DBG2["pp"], in_=escr[:, :]),
                                   dma="dbgp", waits=[S.mark("vector")])
                            S.emit("tensor", lambda e: e.nop(), waits=[S.dmark("dbgp")])
                            S.emit("vector", lambda e: e.engine_nop(), waits=[S.dmark("dbgp")])
                        k_idx[0] += 1

                for dt_i in range(NT):
                    p = dt_i % 2
                    wE = [pe_marks[k_idx[0] - NT + dt_i]]
                    if tc * NT + dt_i >= 2:
                        wE.append(S.dmark(dma_y_name))
                    S.emit("vector", lambda e, dt_i=dt_i, p=p, ptc=ptc: e.scalar_tensor_tensor(
                        ystg[p][:, :], ucc[ptc][:, dt_i * TC:(dt_i + 1) * TC],
                        dskip[:, dt_i:dt_i + 1], p_mm[dt_i][:, :], ALU.mult, ALU.add),
                        waits=wE, inc=True)
                    evac_marks[(tc, dt_i)] = S.mark("vector")
                    S.emit("sync", lambda e, dt_i=dt_i, tc=tc, p=p: e.dma_start(
                        out=t_y[dt_i * 128:(dt_i + 1) * 128, tc * TC:(tc + 1) * TC],
                        in_=ystg[p][:, :]), dma=dma_y_name, waits=[evac_marks[(tc, dt_i)]])
                    if tc == 0 and dt_i == 0 and i == 0 and "ys" in DBG2:
                        S.emit("sync", lambda e, p=p: e.dma_start(
                            out=DBG2["ys"], in_=ystg[p][:, :]),
                            dma="dbgy", waits=[evac_marks[(tc, dt_i)]])
                        S.emit("sync", lambda e, ptc=ptc: e.dma_start(
                            out=DBG2["uc0"], in_=ucc[ptc][:, 0:TC]),
                            dma="dbgy", waits=[evac_marks[(tc, dt_i)]])
            return S.dmark(dma_y_name)

        def mamba_layer(i, rhs_in, rhs_w_of=None, mid_emit=None):
            tag = f"L{i}"
            for dst, src in [(convw, d_convw[i]), (convb, d_convb[i]), (dtb, d_dtb[i]),
                             (dskip, d_dskip[i])]:
                load_const(dst, src, f"cv{tag}",
                           waits=[S.mark("vector"), S.mark("scalar")] if i > 0 else ())
            S.emit("sync", lambda e: e.dma_start(out=escr[:, 0:NT * 64], in_=d_alog[i]),
                   dma=f"cv{tag}")
            S.emit("scalar", lambda e: e.activation(Aneg[:, :], escr[:, 0:NT * 64], AF.Exp),
                   waits=[S.dmark(f"cv{tag}")], inc=True)
            S.emit("vector", lambda e: e.tensor_scalar(
                Aneg[:, :], Aneg[:, :], -1.0, None, ALU.mult),
                waits=[S.mark("scalar")], inc=True)
            S.emit("vector", lambda e: e.memset(carry[:, :], 0.0), inc=True)

            wu_of = prep_lhsT(d_inw[i][0:DH, :], DH, f"{tag}u", eng="scalar")
            wt_ready = S.mark("scalar")
            ur_marks = {}

            def evac_u(m, n, bank, pe_mark):
                p = (n * 8 + m) % 2
                wv = [pe_mark]
                if n * 8 + m >= 2:
                    wv.append(S.dmark(f"ur{tag}"))
                S.emit("scalar", lambda e, bank=bank, p=p: e.activation(
                    stg[p][:, :], p_mm[bank][:, :], AF.Copy), waits=wv, inc=True)
                S.emit("sync", lambda e, m=m, n=n, p=p: e.dma_start(
                    out=t_uraw[m * 128:(m + 1) * 128, n * TC:(n + 1) * TC],
                    in_=stg[p][:, :]), dma=f"ur{tag}", waits=[S.mark("scalar")])
                ur_marks[(m, n)] = S.dmark(f"ur{tag}")
            mm_phase(rhs_in, 8, 8, wu_of, evac_u, f"{tag}u", extra_w=[wt_ready],
                     evac_eng="scalar", rhs_w=rhs_w_of,
                     first_w=[maxred_mark] if i == 0 else [S.mark("vector")])
            mu_done = S.mark("scalar")

            CC = 512
            ub = [sp_all.ap()[:, j * 520:j * 520 + CC + 4] for j in range(4)]
            ucb = [oq_alias[:, 4160 + p * CC:4160 + (p + 1) * CC] for p in range(2)]
            cacc = escr
            for j in range(4):
                S.emit("vector", lambda e, j=j: e.memset(ub[j][:, 0:3], 0.0), inc=True)
            silu_marks = {}
            uc_marks = {}
            nch = L // CC
            for ci in range(NT * nch):
                dt_i, ch = divmod(ci, nch)
                p = ci % 4
                wl = [ur_marks[(dt_i, ch)]]
                if ch > 0:
                    wl.append(ur_marks[(dt_i, ch - 1)])
                if ci >= 4:
                    wl.append(silu_marks[ci - 4])
                lo = ch * CC - 3
                if ch == 0:
                    S.emit("sync", lambda e, dt_i=dt_i, ch=ch, p=p: e.dma_start(
                        out=ub[p][:, 3:CC + 3],
                        in_=t_uraw[dt_i * 128:(dt_i + 1) * 128, ch * CC:(ch + 1) * CC]),
                        dma=f"ul{p}", waits=wl)
                    if ci > 0:
                        S.emit("vector", lambda e, p=p: e.memset(ub[p][:, 0:3], 0.0),
                               waits=[silu_marks[ci - 4]] if ci >= 4 else [])
                else:
                    S.emit("sync", lambda e, dt_i=dt_i, lo=lo, p=p: e.dma_start(
                        out=ub[p][:, 0:CC + 3],
                        in_=t_uraw[dt_i * 128:(dt_i + 1) * 128, lo:lo + CC + 3]),
                        dma=f"ul{p}", waits=wl)
                S.emit("vector", lambda e, dt_i=dt_i, p=p: e.tensor_scalar(
                    cacc[:, :], ub[p][:, 0:CC], convw[:, dt_i * 4:dt_i * 4 + 1], None,
                    ALU.mult), waits=[S.dmark(f"ul{p}"), S.mark("scalar")], inc=True)
                for kk in range(1, 4):
                    S.emit("vector", lambda e, dt_i=dt_i, p=p, kk=kk: e.scalar_tensor_tensor(
                        cacc[:, :], ub[p][:, kk:CC + kk],
                        convw[:, dt_i * 4 + kk:dt_i * 4 + kk + 1],
                        cacc[:, :], ALU.mult, ALU.add), inc=(kk == 3))
                wsl = [S.mark("vector")]
                if ci >= 2:
                    wsl.append(uc_marks[ci - 2])
                S.emit("scalar", lambda e, dt_i=dt_i, ci=ci: e.activation(
                    ucb[ci % 2][:, :], cacc[:, :], AF.Silu, bias=convb[:, dt_i:dt_i + 1]),
                    waits=wsl, inc=True)
                silu_marks[ci] = S.mark("scalar")
                S.emit("sync", lambda e, dt_i=dt_i, ch=ch, ci=ci: e.dma_start(
                    out=t_uc[dt_i * 128:(dt_i + 1) * 128, ch * CC:(ch + 1) * CC],
                    in_=ucb[ci % 2][:, :]), dma=f"uc{tag}", waits=[silu_marks[ci]])
                uc_marks[ci] = S.dmark(f"uc{tag}")

            if mid_emit is not None:
                mid_emit(mu_done)

            wz_of = prep_lhsT(d_inw[i][DH:2 * DH, :], DH, f"{tag}z", eng="scalar")
            wt_ready = S.mark("scalar")

            def evac_z(m, n, bank, pe_mark):
                p = (n * 8 + m) % 2
                wv = [pe_mark]
                if n * 8 + m >= 2:
                    wv.append(S.dmark(f"zr{tag}"))
                S.emit("scalar", lambda e, bank=bank, p=p: e.activation(
                    stgb[p][:, :], p_mm[bank][:, :], AF.Copy), waits=wv, inc=True)
                S.emit("sync", lambda e, m=m, n=n, p=p: e.dma_start(
                    out=t_z[m * 128:(m + 1) * 128, n * TC:(n + 1) * TC],
                    in_=stgb[p][:, :]), dma=f"zr{tag}", waits=[S.mark("scalar")])
            mm_phase(rhs_in, 8, 8, wz_of, evac_z, f"{tag}z", extra_w=[wt_ready],
                     evac_eng="scalar", rhs_w=rhs_w_of)
            mz_done = S.mark("scalar")

            nmx = 2
            for m in range(nmx):
                rows = 128 if m == 0 else 64
                wv = [S.mark("tensor")] if m > 0 else [S.mark("vector"), S.mark("scalar")]
                S.emit("sync", lambda e, m=m, rows=rows: e.dma_start(
                    out=wld[0][0:rows, 0:DM], in_=d_xpw[i][m * 128:m * 128 + rows, :]),
                    dma=f"wx{tag}", waits=wv)
                S.emit("vector", lambda e, rows=rows: e.tensor_copy(
                    wldb[0:rows, 0:DM], wld[0][0:rows, 0:DM]),
                    waits=[S.dmark(f"wx{tag}")], inc=True)
                cast_mark = S.mark("vector")
                for k in range(8):
                    pt = p_tr[k % 2]
                    wtp = [cast_mark]
                    if m * 8 + k < 2:
                        wtp.append(mz_done)
                    if m * 8 + k >= 2:
                        wtp.append(tr_evac_marks[-2])
                    S.emit("tensor", lambda e, k=k, rows=rows, pt=pt: e.transpose(
                        pt[0:128, 0:rows], wldb[0:rows, k * 128:(k + 1) * 128],
                        eye_b[0:rows, 0:rows]), waits=wtp, inc=True)
                    col = (k * nmx + m) * 128
                    S.emit("vector", lambda e, col=col, rows=rows, pt=pt: e.tensor_copy(
                        wT[:, col:col + rows], pt[0:128, 0:rows]),
                        waits=[S.mark("tensor")], inc=True)
                    tr_evac_marks.append(S.mark("vector"))
            wt_ready = S.mark("vector")

            def evac_x(m, n, bank, pe_mark):
                rows = 128 if m == 0 else 64
                p = (n * 2 + m) % 2
                wv = [pe_mark]
                if n * 2 + m >= 2:
                    wv.append(S.dmark(f"xd{tag}"))
                S.emit("vector", lambda e, rows=rows, bank=bank, p=p: e.tensor_copy(
                    stg[p][0:rows, :], p_mm[bank][0:rows, :]), waits=wv, inc=True)
                S.emit("sync", lambda e, m=m, n=n, rows=rows, p=p: e.dma_start(
                    out=t_smp[m * 128:m * 128 + rows, n * TC:(n + 1) * TC],
                    in_=stg[p][0:rows, :]), dma=f"xd{tag}", waits=[S.mark("vector")])
            mm_phase(t_uc, 8, 2, lambda k, m: (k * nmx + m) * 128, evac_x, f"{tag}x",
                     extra_w=[wt_ready, S.dmark(f"uc{tag}")], first_w=[mz_done])
            slot_copy("gpsimd", "sci",
                      lambda p: t_ccsi[p * 192:(p + 1) * 192, :],
                      lambda p: t_smp[:, :], waits=[S.dmark(f"xd{tag}"), zf_done])
            S.emit("gpsimd", lambda e: e.collective_compute(
                "AllReduce", ALU.add, replica_groups=ALL8,
                ins=[t_ccsi.opt()], outs=[t_ccso.opt()]),
                waits=[S.dmark("sci")], dma="ccc", dmainc=1)
            slot_copy("scalar", "sco",
                      lambda p: t_smf[:, :],
                      lambda p: t_ccso[p * 192:(p + 1) * 192, :],
                      waits=[S.dmark("ccc")])
            ccc_mark = S.dmark("sco")
            for gi_, (dst, r0) in enumerate([(None, 0), (t_Bc, 64), (t_Cc, 128)]):
                for ch in range(4):
                    p = ch % 2
                    wv = [ccc_mark]
                    if gi_ * 4 + ch >= 2:
                        wv.append(S.mark("vector"))
                    S.emit("sync", lambda e, r0=r0, ch=ch, p=p: e.dma_start(
                        out=wld[p][0:64, 0:1024],
                        in_=t_smf[r0:r0 + 64, ch * 1024:(ch + 1) * 1024]),
                        dma=f"xu{tag}", waits=wv)
                    if dst is None:
                        S.emit("vector", lambda e, ch=ch, p=p: e.tensor_copy(
                            dtraw[:, ch * 1024:(ch + 1) * 1024], wld[p][0:64, 0:1024]),
                            waits=[S.dmark(f"xu{tag}")], inc=True)
                    else:
                        wbk = [S.dmark(f"xu{tag}")]
                        if gi_ * 4 + ch < 2:
                            wbk += [S.dmark("xqfm"), S.dmark("op")]
                        if gi_ * 4 + ch >= 2:
                            wbk.append(S.dmark(f"bc{tag}"))
                        S.emit("vector", lambda e, ch=ch, p=p: e.tensor_copy(
                            auxb[0:64, 0:1024], wld[p][0:64, 0:1024]), waits=wbk, inc=True)
                        S.emit("sync", lambda e, dst=dst, ch=ch: e.dma_start(
                            out=dst[0:64, ch * 1024:(ch + 1) * 1024], in_=auxb[0:64, 0:1024]),
                            dma=f"bc{tag}", waits=[S.mark("vector")])
            dtraw_mark = S.mark("vector")
            bc_done = S.dmark(f"bc{tag}")

            S.emit("sync", lambda e: e.dma_start(out=wld[1][:, 0:NT * 64], in_=d_dtw[i]),
                   dma=f"dw{tag}", waits=[S.mark("vector"), S.mark("scalar")])
            S.emit("vector", lambda e: e.tensor_copy(wldb[:, 0:NT * 64], wld[1][:, 0:NT * 64]),
                   waits=[S.dmark(f"dw{tag}")], inc=True)
            cast_mark = S.mark("vector")
            for k in range(NT):
                pt = p_tr[k % 2]
                wtp = [cast_mark]
                if k >= 2:
                    wtp.append(tr_evac_marks[-2])
                S.emit("tensor", lambda e, k=k, pt=pt: e.transpose(
                    pt[0:64, 0:128], wldb[:, k * 64:(k + 1) * 64], eye_b[:, :]),
                    waits=wtp, inc=True)
                S.emit("vector", lambda e, k=k, pt=pt: e.tensor_copy(
                    dtwT[0:64, k * 128:(k + 1) * 128], pt[0:64, 0:128]),
                    waits=[S.mark("tensor")], inc=True)
                tr_evac_marks.append(S.mark("vector"))
            dtwT_mark = S.mark("vector")

            wo_of = prep_lhsT(d_outw[i], DM, f"{tag}o", eng="scalar")
            wt_ready_o = S.mark("scalar")

            y_done = scan_phase(i, tag, dtraw_mark, dtwT_mark, bc_done)

            gt = gt_alias
            zl = stgb
            zs = escr
            yl = [sp_all.ap()[:, p * TC:(p + 1) * TC] for p in range(2)]
            g_marks = {}
            mm_done = {}
            bank_evac = {}
            for n in range(NTC):
                for k in range(NT):
                    p = k % 2
                    wv = [S.dmark(f"zr{tag}"), y_done]
                    if (n, k) != (0, 0):
                        if n * NT + k >= 2:
                            wv.append(g_marks[n * NT + k - 2])
                    S.emit("sync", lambda e, n=n, k=k, p=p: e.dma_start(
                        out=zl[p][:, :], in_=t_z[k * 128:(k + 1) * 128, n * TC:(n + 1) * TC]),
                        dma=f"zl{p}", waits=wv)
                    S.emit("sync", lambda e, n=n, k=k, p=p: e.dma_start(
                        out=yl[p][:, :], in_=t_y[k * 128:(k + 1) * 128, n * TC:(n + 1) * TC]),
                        dma=f"yl{p}")
                    S.emit("scalar", lambda e, p=p: e.activation(
                        zs[:, :], zl[p][:, :], AF.Silu),
                        waits=[S.dmark(f"zl{p}"), S.mark("vector")], inc=True)
                    wg_ = [S.dmark(f"yl{p}"), S.mark("scalar")]
                    if n > 0:
                        wg_.append(mm_done.get((n - 1, NT - 1), (None, 0)))
                    S.emit("vector", lambda e, k=k, p=p: e.tensor_tensor(
                        gt[:, k * TC:(k + 1) * TC], yl[p][:, :], zs[:, :], ALU.mult),
                        waits=[x for x in wg_ if x[0] is not None], inc=True)
                    g_marks[n * NT + k] = S.mark("vector")
                gm = S.mark("vector")
                for m in range(8):
                    bank = m
                    wm = [gm]
                    prev = bank_evac.get(bank)
                    if prev is not None:
                        wm.append(prev)
                    for k in range(NT):
                        S.emit("tensor", lambda e, m=m, k=k, bank=bank: e.matmul(
                            p_mm[bank][:, :], wT[:, wo_of(k, m):wo_of(k, m) + 128],
                            gt[:, k * TC:(k + 1) * TC],
                            start=(k == 0), stop=(k == NT - 1)),
                            waits=(wm + [wt_ready_o]) if k == 0 else [], inc=(k == NT - 1))
                    mm_done[(n, m)] = S.mark("tensor")
                    p = (n * 8 + m) % 2
                    wv = [mm_done[(n, m)]]
                    if n * 8 + m >= 2:
                        wv.append(S.dmark(f"op{tag}"))
                    if i == 0:
                        S.emit("vector", lambda e, bank=bank, p=p: e.tensor_copy(
                            auxb[:, p * TC:(p + 1) * TC], p_mm[bank][:, :]),
                            waits=wv, inc=True)
                        bank_evac[bank] = S.mark("vector")
                        S.emit("sync", lambda e, m=m, n=n, p=p: e.dma_start(
                            out=t_bg1[m * 128:(m + 1) * 128, n * TC:(n + 1) * TC],
                            in_=auxb[:, p * TC:(p + 1) * TC]),
                            dma=f"op{tag}", waits=[bank_evac[bank]])
                    else:
                        S.emit("vector", lambda e, bank=bank, p=p: e.tensor_copy(
                            stg[p][:, :], p_mm[bank][:, :]), waits=wv, inc=True)
                        bank_evac[bank] = S.mark("vector")
                        S.emit("sync", lambda e, m=m, n=n, p=p: e.dma_start(
                            out=t_m2p[m * 128:(m + 1) * 128, n * TC:(n + 1) * TC],
                            in_=stg[p][:, :]), dma=f"op{tag}", waits=[bank_evac[bank]])
                if i == 0 and n == 3:
                    opA = S.dmark(f"op{tag}")
                    slot_copy("scalar", "bcia",
                              lambda p: t_ccli_a[p * DM:(p + 1) * DM, :],
                              lambda p: t_bg1[:, 0:L // 2], waits=[opA, zf_done])
                    S.emit("gpsimd", lambda e: e.collective_compute(
                        "AllReduce", ALU.add, replica_groups=ALL8,
                        ins=[t_ccli_a.opt()], outs=[t_cclo_a.opt()]),
                        waits=[S.dmark("bcia")], dma="cca", dmainc=1)
            if i == 0:
                opB = S.dmark(f"op{tag}")
                slot_copy("scalar", "bcib",
                          lambda p: t_ccli_b[p * DM:(p + 1) * DM, :],
                          lambda p: t_bg1[:, L // 2:L], waits=[opB, zf_done])
                S.emit("gpsimd", lambda e: e.collective_compute(
                    "AllReduce", ALU.add, replica_groups=ALL8,
                    ins=[t_ccli_b.opt()], outs=[t_cclo_b.opt()]),
                    waits=[S.dmark("bcib")], dma="ccb", dmainc=1)
                slot_copy("scalar", "bcoa",
                          lambda p: t_l2in[:, 0:L // 2],
                          lambda p: t_cclo_a[p * DM:(p + 1) * DM, :],
                          waits=[S.dmark("cca")])
                bcoa_m = S.dmark("bcoa")
                slot_copy("gpsimd", "bcob",
                          lambda p: t_l2in[:, L // 2:L],
                          lambda p: t_cclo_b[p * DM:(p + 1) * DM, :],
                          waits=[S.dmark("ccb")])
                bcob_m = S.dmark("bcob")
                return bcoa_m, bcob_m
            return None

        def qdq3(dstb, src, inv_col, s_col, lo, hi, out_bf=False):
            S.emit("vector", lambda e: e.tensor_scalar(
                aux[:, 0:TC], src, inv_col, RND, ALU.mult, ALU.add), inc=True)
            S.emit("vector", lambda e: e.tensor_scalar(
                aux[:, 0:TC], aux[:, 0:TC], RND, hi, ALU.subtract, ALU.min))
            S.emit("vector", lambda e, dstb=dstb: e.tensor_scalar(
                dstb, aux[:, 0:TC], lo, s_col, ALU.max, ALU.mult), inc=True)

        def dequant_prep(w_dram, nrow_tiles, ncols, name, out_base=0):
            CW = 512
            ncc = ncols // CW
            nbc = CW // 32
            for m in range(nrow_tiles):
                for cch in range(ncc):
                    pw = (m * ncc + cch) % 2
                    wv = [S.mark("vector")] if m * ncc + cch == 0 else [S.mark("vector")]
                    S.emit("sync", lambda e, m=m, cch=cch, pw=pw: e.dma_start(
                        out=wld[pw][:, 0:CW],
                        in_=w_dram[m * 128:(m + 1) * 128, cch * CW:(cch + 1) * CW]),
                        dma="wp", waits=wv)
                    w3 = wld[pw][:, 0:CW].rearrange("p (nb b) -> p nb b", b=32)
                    S.emit("vector", lambda e, w3=w3: e.tensor_reduce(
                        aux[:, 0:nbc], w3, AX, ALU.max, apply_absolute_value=True),
                        waits=[S.dmark("wp")], inc=True)
                    S.gap("vector")
                    S.emit("vector", lambda e: e.tensor_scalar(
                        aux[:, 0:nbc], aux[:, 0:nbc], 1.0 / 7.0, 1e-12, ALU.mult, ALU.max))
                    S.gap("vector")
                    S.emit("vector", lambda e: e.reciprocal(
                        aux[:, nbc:2 * nbc], aux[:, 0:nbc]), inc=True)
                    S.gap("vector")
                    rq3 = aux[:, 512:512 + CW].rearrange("p (nb b) -> p nb b", b=32)
                    S.emit("vector", lambda e, w3=w3, rq3=rq3: e.tensor_tensor(
                        rq3, w3, aux[:, nbc:2 * nbc].unsqueeze(2).broadcast_to([128, nbc, 32]),
                        ALU.mult), waits=[S.mark("tensor")] if cch == 0 else [], inc=True)
                    S.emit("vector", lambda e, pw=pw: e.tensor_scalar(
                        wld[pw][:, 0:CW], aux[:, 512:512 + CW],
                        RND, RND, ALU.add, ALU.subtract), inc=True)
                    S.emit("vector", lambda e, pw=pw: e.tensor_scalar(
                        wld[pw][:, 0:CW], wld[pw][:, 0:CW], 7.0, -8.0, ALU.min, ALU.max))
                    wq3 = wldb[:, cch * CW:(cch + 1) * CW].rearrange("p (nb b) -> p nb b", b=32)
                    S.emit("vector", lambda e, pw=pw, wq3=wq3: e.tensor_tensor(
                        wq3, wld[pw][:, 0:CW].rearrange("p (nb b) -> p nb b", b=32),
                        aux[:, 0:nbc].unsqueeze(2).broadcast_to([128, nbc, 32]),
                        ALU.mult), inc=True)
                cast_mark = S.mark("vector")
                kt = ncols // 128
                for k in range(kt):
                    pt = p_tr[k % 2]
                    wtp = [cast_mark]
                    if m * kt + k >= 2:
                        wtp.append(tr_evac_marks[-2])
                    S.emit("tensor", lambda e, k=k, pt=pt: e.transpose(
                        pt[:, :], wldb[:, k * 128:(k + 1) * 128], eye_b[:, :]),
                        waits=wtp, inc=True)
                    col = out_base + (k * nrow_tiles + m) * 128
                    S.emit("vector", lambda e, col=col, pt=pt: e.tensor_copy(
                        wT[:, col:col + 128], pt[:, :]), waits=[S.mark("tensor")], inc=True)
                    tr_evac_marks.append(S.mark("vector"))
                prep_marks.append(S.mark("vector"))
            return lambda k, m: out_base + (k * nrow_tiles + m) * 128

        def collective_scalar_max(src_colidx, dst_colidx, ccname):
            S.emit("tensor", lambda e, src_colidx=src_colidx: e.transpose(
                p_mm[0][0:1, 0:128], acc1[:, src_colidx:src_colidx + 1], eye_f[:, :]),
                waits=[S.mark("vector")], inc=True)
            S.emit("vector", lambda e, dst_colidx=dst_colidx: e.reduce_max(
                sc11[0:1, dst_colidx:dst_colidx + 1], p_mm[0][0:1, 0:128], AX),
                waits=[S.mark("tensor")], inc=True)
            S.emit("sync", lambda e, dst_colidx=dst_colidx: e.dma_start(
                out=t_ccx1[0:1, 0:1], in_=sc11[0:1, dst_colidx:dst_colidx + 1]),
                dma="ccs", waits=[S.mark("vector")])
            S.emit("gpsimd", lambda e: e.collective_compute(
                "AllReduce", ALU.max, replica_groups=ALL8,
                ins=[t_ccx1.opt()], outs=[t_ccx2.opt()]),
                waits=[S.dmark("ccs")], dma="ccse", dmainc=1)
            S.emit("sync", lambda e, dst_colidx=dst_colidx: e.dma_start(
                out=sc11[0:1, dst_colidx:dst_colidx + 1], in_=t_ccx2[0:1, 0:1]),
                dma="ccs", waits=[S.dmark("ccse")])

        def ffn_phase():
            for which, w_dram, t_raw, acc_col in [("g", d_wg, t_grw, 1), ("u", d_wu, t_upw, 2)]:
                wf_of = dequant_prep(w_dram, NKI, DM, f"f{which}")
                wt_ready = S.mark("vector")
                S.emit("vector", lambda e: e.memset(stmaxf[:, :], 0.0), inc=True)

                def evac_f(m, n, bank, pe_mark, t_raw=t_raw, acc_col=acc_col, which=which):
                    p = (n * NKI + m) % 2
                    wv = [pe_mark]
                    if n * NKI + m >= 2:
                        wv.append(S.dmark("fr"))
                    S.emit("vector", lambda e, bank=bank, p=p: e.tensor_copy(
                        stg[p][:, :], p_mm[bank][:, :]), waits=wv, inc=True)
                    S.emit("vector", lambda e, p=p, m=m, n=n: e.reduce_max(
                        stmaxf[:, (m * NTC + n) % 128:(m * NTC + n) % 128 + 1],
                        stg[p][:, :], AX, apply_absolute_value=True), inc=True)
                    S.emit("sync", lambda e, m=m, n=n, p=p, t_raw=t_raw: e.dma_start(
                        out=t_raw[m * 128:(m + 1) * 128, n * TC:(n + 1) * TC],
                        in_=stg[p][:, :]), dma="fr", waits=[S.mark("vector")])
                mm_phase(t_xq, 8, NKI, wf_of, evac_f, f"f{which}",
                         extra_w=[wt_ready, S.dmark("xqfm")])
                S.emit("vector", lambda e, acc_col=acc_col: e.reduce_max(
                    acc1[:, acc_col:acc_col + 1], stmaxf[:, :], AX),
                    waits=[S.mark("vector")], inc=True)
                S.gap("vector")
                collective_scalar_max(acc_col, 4 + (0 if which == "g" else 1), f"m{which}")
            for src, dst in [(4, (2, 3)), (5, (4, 5))]:
                S.emit("vector", lambda e, src=src: e.tensor_scalar(
                    sc11[0:1, 6:7], sc11[0:1, src:src + 1], 1.0 / QMAX, 1e-12, ALU.mult, ALU.max),
                    waits=[S.dmark("ccs"), S.dmark("qb")], inc=True)
                S.gap("vector")
                S.emit("vector", lambda e: e.reciprocal(sc11[0:1, 7:8], sc11[0:1, 6:7]), inc=True)
                S.gap("vector")
                bcast_scalar(qsc[:, dst[1]:dst[1] + 1], sc11[0:1, 6:7], "qb",
                             waits=[S.mark("vector")])
                bcast_scalar(qsc[:, dst[0]:dst[0] + 1], sc11[0:1, 7:8], "qb")

            with ExitStack() as c6:
                fl = [dA[0].ap()[:, p * TC:(p + 1) * TC] for p in range(2)]
                gq = escr
                sg = stg[0]
                S.emit("vector", lambda e: e.memset(stmaxf[:, :], 0.0), inc=True)
                nchunks = NKI * NTC
                st_marks = {}
                for ci in range(nchunks):
                    m, n = divmod(ci, NTC)
                    p = ci % 2
                    wv = [S.dmark("qb"), S.dmark("fr")]
                    if ci >= 2:
                        wv.append(st_marks[ci - 2])
                    S.emit("sync", lambda e, m=m, n=n, p=p: e.dma_start(
                        out=fl[p][:, :], in_=t_grw[m * 128:(m + 1) * 128, n * TC:(n + 1) * TC]),
                        dma=f"g2l{p}", waits=wv)
                    S.emit("vector", lambda e, p=p: e.tensor_scalar(
                        aux[:, 0:TC], fl[p][:, :], qsc[:, 2:3], RND, ALU.mult, ALU.add),
                        waits=[S.dmark(f"g2l{p}")], inc=True)
                    S.emit("vector", lambda e: e.tensor_scalar(
                        aux[:, 0:TC], aux[:, 0:TC], RND, QMAX, ALU.subtract, ALU.min))
                    S.emit("vector", lambda e: e.tensor_scalar(
                        gq[:, :], aux[:, 0:TC], -QMAX - 1.0, qsc[:, 3:4], ALU.max, ALU.mult),
                        inc=True)
                    S.emit("scalar", lambda e: e.activation(sg[:, :], gq[:, :], AF.Sigmoid),
                           waits=[S.mark("vector")], inc=True)
                    S.emit("vector", lambda e: e.tensor_scalar(
                        aux[:, 0:TC], sg[:, :], 65536.0, RND, ALU.mult, ALU.add),
                        waits=[S.mark("scalar")], inc=True)
                    S.emit("vector", lambda e: e.tensor_scalar(
                        aux[:, 0:TC], aux[:, 0:TC], RND, 65535.0, ALU.subtract, ALU.min))
                    S.emit("vector", lambda e: e.tensor_scalar(
                        sg[:, :], aux[:, 0:TC], 0.0, 1.0 / 65536.0, ALU.max, ALU.mult))
                    S.emit("vector", lambda e, p=p: e.tensor_tensor(
                        fl[p][:, :], gq[:, :], sg[:, :], ALU.mult), inc=True)
                    st_marks[ci] = S.mark("vector")
                    S.emit("vector", lambda e, p=p, ci=ci: e.reduce_max(
                        stmaxf[:, ci % 128:ci % 128 + 1], fl[p][:, :], AX,
                        apply_absolute_value=True))
                    S.emit("sync", lambda e, m=m, n=n, p=p: e.dma_start(
                        out=t_g2w[m * 128:(m + 1) * 128, n * TC:(n + 1) * TC],
                        in_=fl[p][:, :]), dma="g2w", waits=[S.mark("vector")])
                S.emit("vector", lambda e: e.reduce_max(acc1[:, 3:4], stmaxf[:, :], AX),
                       waits=[S.mark("vector")], inc=True)
                S.gap("vector")
                collective_scalar_max(3, 6, "m2")
                S.emit("vector", lambda e: e.tensor_scalar(
                    sc11[0:1, 6:7], sc11[0:1, 6:7], 1.0 / QMAX, 1e-12, ALU.mult, ALU.max),
                    waits=[S.dmark("ccs"), S.dmark("qb")], inc=True)
                S.gap("vector")
                S.emit("vector", lambda e: e.reciprocal(sc11[0:1, 7:8], sc11[0:1, 6:7]), inc=True)
                S.gap("vector")
                bcast_scalar(qsc[:, 3:4], sc11[0:1, 6:7], "qb", waits=[S.mark("vector")])
                bcast_scalar(qsc[:, 2:3], sc11[0:1, 7:8], "qb")

                S.emit("vector", lambda e: e.memset(stmaxf[:, :], 0.0),
                       waits=[S.dmark("qb")], inc=True)
                st_marks = {}
                for ci in range(nchunks):
                    m, n = divmod(ci, NTC)
                    p = ci % 2
                    wv = [S.dmark("g2w"), S.dmark("fr")]
                    if ci >= 2:
                        wv.append(st_marks[ci - 2])
                    S.emit("sync", lambda e, m=m, n=n, p=p: e.dma_start(
                        out=fl[p][:, :], in_=t_g2w[m * 128:(m + 1) * 128, n * TC:(n + 1) * TC]),
                        dma=f"o3a{p}", waits=wv)
                    S.emit("sync", lambda e, m=m, n=n, p=p: e.dma_start(
                        out=gq[:, :] if p == 0 else sg[:, :],
                        in_=t_upw[m * 128:(m + 1) * 128, n * TC:(n + 1) * TC]),
                        dma=f"o3b{p}")
                    upbuf = lambda p: gq if p == 0 else sg
                    S.emit("vector", lambda e, p=p: e.tensor_scalar(
                        aux[:, 0:TC], fl[p][:, :], qsc[:, 2:3], RND, ALU.mult, ALU.add),
                        waits=[S.dmark(f"o3a{p}"), S.dmark(f"o3b{p}")], inc=True)
                    S.emit("vector", lambda e: e.tensor_scalar(
                        aux[:, 0:TC], aux[:, 0:TC], RND, QMAX, ALU.subtract, ALU.min))
                    S.emit("vector", lambda e, p=p: e.tensor_scalar(
                        fl[p][:, :], aux[:, 0:TC], -QMAX - 1.0, qsc[:, 3:4], ALU.max, ALU.mult))
                    S.emit("vector", lambda e, p=p: e.tensor_scalar(
                        aux[:, 0:TC], upbuf(p)[:, :], qsc[:, 4:5], RND, ALU.mult, ALU.add))
                    S.emit("vector", lambda e: e.tensor_scalar(
                        aux[:, 0:TC], aux[:, 0:TC], RND, QMAX, ALU.subtract, ALU.min))
                    S.emit("vector", lambda e, p=p: e.tensor_scalar(
                        upbuf(p)[:, :], aux[:, 0:TC], -QMAX - 1.0, qsc[:, 5:6], ALU.max, ALU.mult))
                    S.emit("vector", lambda e, p=p: e.tensor_tensor(
                        fl[p][:, :], fl[p][:, :], upbuf(p)[:, :], ALU.mult), inc=True)
                    st_marks[ci] = S.mark("vector")
                    S.emit("vector", lambda e, p=p, ci=ci: e.reduce_max(
                        stmaxf[:, ci % 128:ci % 128 + 1], fl[p][:, :], AX,
                        apply_absolute_value=True))
                    S.emit("sync", lambda e, m=m, n=n, p=p: e.dma_start(
                        out=t_ow[m * 128:(m + 1) * 128, n * TC:(n + 1) * TC],
                        in_=fl[p][:, :]), dma="o3w", waits=[S.mark("vector")])
                S.emit("vector", lambda e: e.reduce_max(acc1[:, 3:4], stmaxf[:, :], AX),
                       waits=[S.mark("vector")], inc=True)
                S.gap("vector")
                collective_scalar_max(3, 6, "m3")
                S.emit("vector", lambda e: e.tensor_scalar(
                    sc11[0:1, 6:7], sc11[0:1, 6:7], 1.0 / QMAX, 1e-12, ALU.mult, ALU.max),
                    waits=[S.dmark("ccs"), S.dmark("qb")], inc=True)
                S.gap("vector")
                S.emit("vector", lambda e: e.reciprocal(sc11[0:1, 7:8], sc11[0:1, 6:7]), inc=True)
                S.gap("vector")
                bcast_scalar(qsc[:, 7:8], sc11[0:1, 6:7], "qb", waits=[S.mark("vector")])
                bcast_scalar(qsc[:, 6:7], sc11[0:1, 7:8], "qb")

            wd_of = dequant_prep(d_wd, 8, IH, "fd")
            wt_ready = S.mark("vector")
            with ExitStack() as c7:
                oq = oq_alias
                ol = [dA[1].ap()[:, p * TC:(p + 1) * TC] for p in range(2)]
                m2l = [escr, ystg[0]]
                oq_marks = {}
                mm_done = {}
                bank_evac = {}
                for n in range(NTC):
                    for k in range(NKI):
                        p = k % 2
                        wv = [S.dmark("o3w"), S.dmark("qb")]
                        if n * NKI + k >= 2:
                            wv.append(oq_marks[n * NKI + k - 2])
                        if n > 0:
                            wv.append(mm_done.get((n - 1, 7), (None, 0)))
                        S.emit("sync", lambda e, n=n, k=k, p=p: e.dma_start(
                            out=ol[p][:, :], in_=t_ow[k * 128:(k + 1) * 128, n * TC:(n + 1) * TC]),
                            dma=f"odl{p}", waits=[x for x in wv if x[0] is not None])
                        S.emit("vector", lambda e, p=p: e.tensor_scalar(
                            aux[:, 0:TC], ol[p][:, :], qsc[:, 6:7], RND, ALU.mult, ALU.add),
                            waits=[S.dmark(f"odl{p}")], inc=True)
                        S.emit("vector", lambda e: e.tensor_scalar(
                            aux[:, 0:TC], aux[:, 0:TC], RND, QMAX, ALU.subtract, ALU.min))
                        S.emit("vector", lambda e, k=k: e.tensor_scalar(
                            oq[:, k * TC:(k + 1) * TC], aux[:, 0:TC], -QMAX - 1.0, qsc[:, 7:8],
                            ALU.max, ALU.mult), inc=True)
                        oq_marks[n * NKI + k] = S.mark("vector")
                    oqm = S.mark("vector")
                    for m in range(8):
                        bank = m
                        wm = [oqm, wt_ready]
                        prev = bank_evac.get(bank)
                        if prev is not None:
                            wm.append(prev)
                        for k in range(NKI):
                            S.emit("tensor", lambda e, m=m, k=k, bank=bank: e.matmul(
                                p_mm[bank][:, :], wT[:, wd_of(k, m):wd_of(k, m) + 128],
                                oq[:, k * TC:(k + 1) * TC],
                                start=(k == 0), stop=(k == NKI - 1)),
                                waits=wm if k == 0 else [], inc=(k == NKI - 1))
                        mm_done[(n, m)] = S.mark("tensor")
                        p = (n * 8 + m) % 2
                        S.emit("sync", lambda e, m=m, n=n, p=p: e.dma_start(
                            out=m2l[p][:, :], in_=t_m2p[m * 128:(m + 1) * 128, n * TC:(n + 1) * TC]),
                            dma=f"m2l{p}", waits=[S.dmark("op")])
                        wv = [mm_done[(n, m)], S.dmark(f"m2l{p}")]
                        if n * 8 + m >= 2:
                            wv.append(S.dmark("sumw"))
                        S.emit("vector", lambda e, bank=bank, p=p: e.tensor_tensor(
                            stg[p][:, :], p_mm[bank][:, :], m2l[p][:, :], ALU.add),
                            waits=wv, inc=True)
                        bank_evac[bank] = S.mark("vector")
                        S.emit("sync", lambda e, m=m, n=n, p=p: e.dma_start(
                            out=t_bgp[m * 128:(m + 1) * 128, n * TC:(n + 1) * TC],
                            in_=stg[p][:, :]), dma="sumw", waits=[bank_evac[bank]])
                    if n == 3:
                        sumA = S.dmark("sumw")
                        slot_copy("scalar", "bfia",
                                  lambda p: t_ccbi_a[p * DM:(p + 1) * DM, :],
                                  lambda p: t_bgp[:, 0:L // 2], waits=[sumA, zf_done])
                        S.emit("gpsimd", lambda e: e.collective_compute(
                            "AllReduce", ALU.add, replica_groups=ALL8,
                            ins=[t_ccbi_a.opt()], outs=[t_ccbo_a.opt()]),
                            waits=[S.dmark("bfia")], dma="cfa", dmainc=1)
            sumB = S.dmark("sumw")
            slot_copy("scalar", "bfib",
                      lambda p: t_ccbi_b[p * DM:(p + 1) * DM, :],
                      lambda p: t_bgp[:, L // 2:L], waits=[sumB, zf_done])
            S.emit("gpsimd", lambda e: e.collective_compute(
                "AllReduce", ALU.add, replica_groups=ALL8,
                ins=[t_ccbi_b.opt()], outs=[t_ccbo_b.opt()]),
                waits=[S.dmark("bfib")], dma="cfb", dmainc=1)
            slot_copy("sync", "bfoa",
                      lambda p: t_bgf[:, 0:L // 2],
                      lambda p: t_ccbo_a[p * DM:(p + 1) * DM, :],
                      waits=[S.dmark("cfa")])
            bfoa_m = S.dmark("bfoa")
            slot_copy("gpsimd", "bfob",
                      lambda p: t_bgf[:, L // 2:L],
                      lambda p: t_ccbo_b[p * DM:(p + 1) * DM, :],
                      waits=[S.dmark("cfb")])
            bfob_m = S.dmark("bfob")
            return bfoa_m, bfob_m

        def final_phase(bfoa_m, bfob_m):
            fstart = S.mark("vector")
            with ExitStack() as c8:
                sl = [aux[:, i * 128:(i + 1) * 128] for i in range(4)]
                xl = [rhsb[p].ap().bitcast(F32)[:, 0:DM] for p in range(2)]
                ob = [dA[p].ap()[:, 0:DM] for p in range(2)]
                ntok = L // 128
                add_marks = {}
                pe_tr = {}
                for tt in range(ntok):
                    p = tt % 2
                    half_m = bfoa_m if tt < ntok // 2 else bfob_m
                    wv = [half_m]
                    if tt >= 2:
                        wv.append(add_marks[tt - 2])
                    S.emit("sync", lambda e, tt=tt, p=p: e.dma_start(
                        out=xl[p][:, :], in_=d_x[tt * 128:(tt + 1) * 128, :]), dma=f"xf{p}", waits=wv)
                    for c in range(8):
                        psl = sl[c % 4]
                        wv2 = [half_m]
                        if (tt * 8 + c) >= 4:
                            wv2.append(pe_tr[tt * 8 + c - 4])
                        S.emit("sync", lambda e, tt=tt, c=c, psl=psl: e.dma_start(
                            out=psl[:, :],
                            in_=t_bgf[c * 128:(c + 1) * 128, tt * 128:(tt + 1) * 128]),
                            dma=f"sf{c % 4}", waits=wv2)
                        ptp = p_tr[c % 2]
                        wv3 = [S.dmark(f"sf{c % 4}"), fstart]
                        if (tt * 8 + c) >= 2:
                            wv3.append(add_marks.get(tt - 2, (None, 0)))
                        S.emit("tensor", lambda e, psl=psl, c=c: e.transpose(
                            p_mm[c % 4][:, 0:128], psl[:, :], eye_f[:, :]),
                            waits=[x for x in wv3 if x[0] is not None], inc=True)
                        pe_tr[tt * 8 + c] = S.mark("tensor")
                        S.emit("vector", lambda e, c=c, p=p: e.tensor_tensor(
                            ob[p][:, c * 128:(c + 1) * 128], p_mm[c % 4][0:128, 0:128],
                            xl[p][:, c * 128:(c + 1) * 128], ALU.add),
                            waits=[pe_tr[tt * 8 + c], S.dmark(f"xf{p}")], inc=True)
                    add_marks[tt] = S.mark("vector")
                    S.emit("sync", lambda e, tt=tt, p=p: e.dma_start(
                        out=d_out[tt * 128:(tt + 1) * 128, :], in_=ob[p][:, :]),
                        dma="outw", waits=[add_marks[tt]])
                S.emit("sync", lambda e: e.dma_start(out=sc11[0:1, 9:10], in_=t_ccx2[0:1, 0:1]),
                       dma="fin", waits=[S.dmark("outw")])

        if STAGE >= 1:
            pass
        if STAGE >= 2:
            l0_marks = mamba_layer(0, t_xn,
                                   rhs_w_of=lambda n: [xn_tile[4 * n + 3]],
                                   mid_emit=emit_quant)
        if STAGE >= 4:
            bcoa_m, bcob_m = l0_marks
            mamba_layer(1, t_l2in,
                        rhs_w_of=lambda n, a=bcoa_m, b=bcob_m: [a if n < 4 else b])
        if STAGE >= 5:
            ff_marks = ffn_phase()
        if STAGE >= 6:
            final_phase(*ff_marks)
        for tag, (dap, src, shp, dt) in dbg_outs.items():
            name = f"dbg{tag}"
            wv = [S.mark(e) for e in ENGINES] + [S.dmark(n) for n in list(S.dsems)]
            first = True
            for r in range(0, shp[0], 128):
                rows = min(128, shp[0] - r)
                S.emit("sync", lambda e, r=r, rows=rows, dap=dap, src=src: e.dma_start(
                    out=dap[r:r + rows, :], in_=src[r:r + rows, :]),
                    dma=name, waits=wv if first else [])
                first = False
            S.emit("sync", lambda e: e.dma_start(out=sc11[0:1, 10:11], in_=t_ccx2[0:1, 0:1]),
                   dma="fin2", waits=[S.dmark(name)])
        if STAGE < 6:
            wv = [S.mark(e) for e in ENGINES] + [S.dmark("xntok")]
            for tt in range(L // 128):
                S.emit("sync", lambda e, tt=tt: e.dma_start(
                    out=d_out[tt * 128:(tt + 1) * 128, :],
                    in_=t_xn_tok[tt * 128:(tt + 1) * 128, :]),
                    dma="outw", waits=wv if tt == 0 else [])
            S.emit("sync", lambda e: e.dma_start(out=sc11[0:1, 9:10], in_=t_ccx2[0:1, 0:1]),
                   dma="fin", waits=[S.dmark("outw")])

        blk = ctx.enter_context(nc.Block())
        S.finalize(blk)
    return nc, dbg_outs


def _slice_inputs(inputs, core):
    b, j = divmod(core, 2)
    f32 = np.float32

    def resh(v):
        v = np.ascontiguousarray(v)
        if v.ndim == 1:
            return np.ascontiguousarray(v.reshape(8, 128).T)
        k = v.shape[1]
        return np.ascontiguousarray(v.reshape(8, 128, k).transpose(1, 0, 2).reshape(128, 8 * k))

    m = {}
    x = np.asarray(inputs["L0_attn_x"], f32)
    m["x_tok"] = np.ascontiguousarray(x[b])
    for i in range(2):
        inw = np.asarray(inputs["L0_attn_in_proj_w"], f32)[i]
        m[f"inw{i}"] = np.ascontiguousarray(
            np.concatenate([inw[j * DH:(j + 1) * DH], inw[2 * DH + j * DH:2 * DH + (j + 1) * DH]], 0))
        m[f"convw{i}"] = resh(np.asarray(inputs["L0_attn_conv_w"], f32)[i][j * DH:(j + 1) * DH])
        m[f"convb{i}"] = resh(np.asarray(inputs["L0_attn_conv_b"], f32)[i][j * DH:(j + 1) * DH])
        m[f"xpw{i}"] = np.ascontiguousarray(
            np.asarray(inputs["L0_attn_x_proj_w"], f32)[i][:, j * DH:(j + 1) * DH])
        m[f"dtw{i}"] = resh(np.asarray(inputs["L0_attn_dt_proj_w"], f32)[i][j * DH:(j + 1) * DH])
        m[f"dtb{i}"] = resh(np.asarray(inputs["L0_attn_dt_proj_b"], f32)[i][j * DH:(j + 1) * DH])
        m[f"alog{i}"] = resh(np.asarray(inputs["L0_attn_A_log"], f32)[i][j * DH:(j + 1) * DH])
        m[f"dskip{i}"] = resh(np.asarray(inputs["L0_attn_D_skip"], f32)[i][j * DH:(j + 1) * DH])
        m[f"outw{i}"] = np.ascontiguousarray(
            np.asarray(inputs["L0_attn_out_proj_w"], f32)[i][:, j * DH:(j + 1) * DH])
    m["wg"] = np.ascontiguousarray(np.asarray(inputs["L0_ffn_w_gate"], f32)[j * IH:(j + 1) * IH])
    m["wu"] = np.ascontiguousarray(np.asarray(inputs["L0_ffn_w_up"], f32)[j * IH:(j + 1) * IH])
    m["wd"] = np.ascontiguousarray(np.asarray(inputs["L0_ffn_w_down"], f32)[:, j * IH:(j + 1) * IH])
    m["eye_b_in"] = np.eye(128, dtype=ml_dtypes.bfloat16)
    m["eye_f_in"] = np.eye(128, dtype=np.float32)
    return m


def kernel(**inputs):
    nc, _ = build_program(dbg=DBG)
    in_maps = [_slice_inputs(inputs, c) for c in range(8)]
    res = run_bass_kernel_spmd(nc, in_maps, core_ids=list(range(8)))
    out = np.stack([np.asarray(res.results[2 * b]["out"], np.float32) for b in range(4)], 0)
    return out


# version 05
# speedup vs baseline: 1.0446x; 1.0160x over previous
import numpy as np
import ml_dtypes
from contextlib import ExitStack

import concourse.bass as bass
import concourse.mybir as mybir
from concourse.bass_utils import run_bass_kernel_spmd

F32 = mybir.dt.float32
BF16 = mybir.dt.bfloat16
AF = mybir.ActivationFunctionType
ALU = mybir.AluOpType
AX = mybir.AxisListType.X

L = 4096
DM = 1024
DH = 1024
NT = 8
NG = 8
GS = 8
TC = 512
NTC = L // TC
SEG = TC + 2
CAT = GS * SEG
IH = 1536
NKI = IH // 128
RND = 12582912.0
QMAX = 32767.0

STAGE = 6
DBG = ()

ENGINES = ("sync", "vector", "scalar", "tensor", "gpsimd")
PAIRS = [[0, 1], [2, 3], [4, 5], [6, 7]]
ALL8 = [list(range(8))]


class Sched:
    def __init__(self, nc, ctx):
        self.nc = nc
        self.ops = {e: [] for e in ENGINES}
        self.psem = {e: ctx.enter_context(nc.semaphore(f"sp_{e}")) for e in ENGINES}
        self.pcnt = {e: 0 for e in ENGINES}
        self.dsems = {}
        self.ctx = ctx

    def _dsem(self, name):
        name = name.replace("L0", "").replace("L1", "")
        if name not in self.dsems:
            h = self.ctx.enter_context(self.nc.semaphore(f"sd_{name}"))
            self.dsems[name] = [h, 0]
        return self.dsems[name]

    def emit(self, eng, fn, waits=(), inc=False, dma=None, dmainc=16):
        incs = []
        if inc:
            self.pcnt[eng] += 1
            incs.append((self.psem[eng], 1))
        if dma is not None:
            s = self._dsem(dma)
            s[1] += dmainc
            incs.append((s[0], dmainc))
        ws = tuple((h, v) for (h, v) in waits if v > 0)
        self.ops[eng].append((fn, ws, tuple(incs)))

    def mark(self, eng):
        return (self.psem[eng], self.pcnt[eng])

    def gap(self, eng, n=6):
        for _ in range(n):
            self.emit(eng, lambda e: e.engine_nop())

    def dmark(self, name):
        s = self._dsem(name)
        return (s[0], s[1])

    def finalize(self, block):
        def mk(name):
            def body(eng):
                for fn, waits, incs in self.ops[name]:
                    for (h, v) in waits:
                        eng.wait_ge(h, v)
                    ins = fn(eng)
                    for (h, v) in incs:
                        ins = ins.then_inc(h, v)
            return body
        block.sync(mk("sync"))
        block.vector(mk("vector"))
        block.scalar(mk("scalar"))
        block.tensor(mk("tensor"))
        block.gpsimd(mk("gpsimd"))


def seg3(t, c0, c1):
    return t.ap().rearrange("p (s c) -> p s c", c=SEG)[:, :, c0:c1]


def build_program(dbg=()):
    nc = bass.Bass("TRN2", target_bir_lowering=False, debug=False)

    d_x = nc.dram_tensor("x_tok", [L, DM], F32, kind="ExternalInput").ap()
    d_inw = [nc.dram_tensor(f"inw{i}", [2 * DH, DM], F32, kind="ExternalInput").ap() for i in range(2)]
    d_convw = [nc.dram_tensor(f"convw{i}", [128, NT * 4], F32, kind="ExternalInput").ap() for i in range(2)]
    d_convb = [nc.dram_tensor(f"convb{i}", [128, NT], F32, kind="ExternalInput").ap() for i in range(2)]
    d_xpw = [nc.dram_tensor(f"xpw{i}", [192, DH], F32, kind="ExternalInput").ap() for i in range(2)]
    d_dtw = [nc.dram_tensor(f"dtw{i}", [128, NT * 64], F32, kind="ExternalInput").ap() for i in range(2)]
    d_dtb = [nc.dram_tensor(f"dtb{i}", [128, NT], F32, kind="ExternalInput").ap() for i in range(2)]
    d_alog = [nc.dram_tensor(f"alog{i}", [128, NT * 64], F32, kind="ExternalInput").ap() for i in range(2)]
    d_dskip = [nc.dram_tensor(f"dskip{i}", [128, NT], F32, kind="ExternalInput").ap() for i in range(2)]
    d_outw = [nc.dram_tensor(f"outw{i}", [DM, DH], F32, kind="ExternalInput").ap() for i in range(2)]
    d_wg = nc.dram_tensor("wg", [IH, DM], F32, kind="ExternalInput").ap()
    d_wu = nc.dram_tensor("wu", [IH, DM], F32, kind="ExternalInput").ap()
    d_wd = nc.dram_tensor("wd", [DM, IH], F32, kind="ExternalInput").ap()
    d_eyeb = nc.dram_tensor("eye_b_in", [128, 128], BF16, kind="ExternalInput").ap()
    d_eyef = nc.dram_tensor("eye_f_in", [128, 128], F32, kind="ExternalInput").ap()
    d_out = nc.dram_tensor("out", [L, DM], F32, kind="ExternalOutput").ap()

    ds = lambda n, shp, dt=F32: nc.dram_tensor(n, shp, dt).ap()
    t_xn_tok = ds("s_xn_tok", [L, DM])
    t_xn = ds("s_xn", [DM, L], BF16)
    t_xq = ds("s_xq", [DM, L], BF16)
    t_uraw = ds("s_uraw", [DH, L])
    t_uc = ds("s_uc", [DH, L], BF16)
    t_z = ds("s_z", [DH, L], BF16)
    t_y = ds("s_y", [DH, L])
    t_m2p = ds("s_m2p", [DM, L])
    t_l2in = ds("s_l2in", [DM, L], BF16)
    t_ccsi = nc.dram_tensor("cc_sm_in", [4 * 192, L], F32).ap()
    t_ccso = nc.dram_tensor("cc_sm_out", [4 * 192, L], F32, addr_space="Shared").ap()
    t_ccbi_a = nc.dram_tensor("cc_big_in_a", [4 * DM, L // 2], F32).ap()
    t_ccbi_b = nc.dram_tensor("cc_big_in_b", [4 * DM, L // 2], F32).ap()
    t_ccbo_a = nc.dram_tensor("cc_big_out_a", [4 * DM, L // 2], F32, addr_space="Shared").ap()
    t_ccbo_b = nc.dram_tensor("cc_big_out_b", [4 * DM, L // 2], F32, addr_space="Shared").ap()
    t_ccli_a = nc.dram_tensor("cc_l2_in_a", [4 * DM, L // 2], BF16).ap()
    t_ccli_b = nc.dram_tensor("cc_l2_in_b", [4 * DM, L // 2], BF16).ap()
    t_cclo_a = nc.dram_tensor("cc_l2_out_a", [4 * DM, L // 2], BF16, addr_space="Shared").ap()
    t_cclo_b = nc.dram_tensor("cc_l2_out_b", [4 * DM, L // 2], BF16, addr_space="Shared").ap()
    t_bg1 = ds("s_bg1", [DM, L], BF16)
    t_ccx1 = ds("cc_x_in", [1, 4])
    t_ccx2 = nc.dram_tensor("cc_x_out", [1, 4], F32, addr_space="Shared").ap()
    t_grw = ds("s_gate", [IH, L])
    t_upw = ds("s_up", [IH, L])
    t_g2w = ds("s_gate2", [IH, L])
    t_ow = ds("s_o", [IH, L])
    t_Bc = ds("s_Bc", [64, L], BF16)
    t_Cc = ds("s_Cc", [64, L], BF16)
    t_smp = ds("s_smp", [192, L])
    t_smf = ds("s_smf", [192, L])
    t_bgp = ds("s_bgp", [DM, L])
    t_bgf = ds("s_bgf", [DM, L])

    dbg_specs = {"xn": (t_xn, [DM, L], BF16), "uraw": (t_uraw, [DH, L], F32),
                 "uc": (t_uc, [DH, L], BF16), "z": (t_z, [DH, L], BF16),
                 "xdbl": (t_ccso, [192, L], F32), "y": (t_y, [DH, L], F32),
                 "m1": (t_bgf, [DM, L], F32), "xq": (t_xq, [DM, L], BF16),
                 "gate": (t_grw, [IH, L], F32), "up": (t_upw, [IH, L], F32),
                 "g2": (t_g2w, [IH, L], F32), "o": (t_ow, [IH, L], F32),
                 "m2p": (t_m2p, [DM, L], F32), "xntok": (t_xn_tok, [L, DM], F32)}
    DBG2 = {}
    if "sp" in dbg:
        DBG2["sp"] = nc.dram_tensor("dbg2_sp", [128, NT * TC], F32, kind="ExternalOutput").ap()
        DBG2["q"] = nc.dram_tensor("dbg2_q", [128, NT * TC], BF16, kind="ExternalOutput").ap()
        DBG2["an"] = nc.dram_tensor("dbg2_an", [128, NT * 64], F32, kind="ExternalOutput").ap()
        DBG2["bb"] = nc.dram_tensor("dbg2_bb", [128, CAT], BF16, kind="ExternalOutput").ap()
        DBG2["cb"] = nc.dram_tensor("dbg2_cb", [128, CAT], BF16, kind="ExternalOutput").ap()
    if "ys" in dbg:
        DBG2["ys"] = nc.dram_tensor("dbg2_ys", [128, TC], F32, kind="ExternalOutput").ap()
        DBG2["uc0"] = nc.dram_tensor("dbg2_uc0", [128, TC], BF16, kind="ExternalOutput").ap()
    if "pp" in dbg:
        DBG2["pp"] = nc.dram_tensor("dbg2_pp", [128, TC], F32, kind="ExternalOutput").ap()
    if "hc" in dbg:
        DBG2["hc"] = nc.dram_tensor("dbg2_hc", [128, CAT], BF16, kind="ExternalOutput").ap()
        DBG2["vc"] = nc.dram_tensor("dbg2_vc", [128, CAT], BF16, kind="ExternalOutput").ap()
        DBG2["bc2"] = nc.dram_tensor("dbg2_bc2", [128, CAT], BF16, kind="ExternalOutput").ap()
        DBG2["da"] = nc.dram_tensor("dbg2_da", [128, CAT], F32, kind="ExternalOutput").ap()
    dbg_outs = {}
    for tag in dbg:
        if tag in ("sp", "hc", "ys", "pp"):
            continue
        src, shp, dt = dbg_specs[tag]
        dbg_outs[tag] = (nc.dram_tensor("dbg_" + tag, shp, dt, kind="ExternalOutput").ap(), src, shp, dt)

    with ExitStack() as ctx:
        sbt = lambda n, shp, dt=F32: ctx.enter_context(nc.sbuf_tensor(n, shp, dt))
        pst = lambda n, shp, dt=F32: ctx.enter_context(nc.psum_tensor(n, shp, dt))
        S = Sched(nc, ctx)

        eye_b = sbt("eye_b", [128, 128], BF16)
        eye_f = sbt("eye_f", [128, 128], F32)
        convw = sbt("convw", [128, NT * 4])
        convb = sbt("convb", [128, NT])
        dtb = sbt("dtb", [128, NT])
        dskip = sbt("dskip", [128, NT])
        Aneg = sbt("Aneg", [128, NT * 64])
        carry = sbt("carry", [128, NT * 64], BF16)
        qsc = sbt("qsc", [128, 8])
        sc11 = sbt("sc11", [1, 16])
        cst = sbt("cstc", [128, 4])

        aux = sbt("aux", [128, 1024])
        auxb = sbt("auxb", [128, 1024], BF16)
        acc1 = sbt("acc1", [128, 4])
        st1 = sbt("st1", [128, 4])
        stmaxf = sbt("stmaxf", [128, 128])

        p_mm = [pst(f"p_mm{i}", [128, 512]) for i in range(8)]
        p_tr = [p_mm[6].ap().bitcast(BF16)[:, 0:128], p_mm[7].ap().bitcast(BF16)[:, 0:128]]

        def load_const(dst, src, name, waits=()):
            S.emit("sync", lambda e, d=dst, s=src: e.dma_start(out=d[:, :], in_=s),
                   dma=name, waits=waits)

        def bcast_scalar(dst_col, src_row, name, waits=()):
            S.emit("sync", lambda e, d=dst_col, s=src_row: e.dma_start(
                out=d, in_=s.unsqueeze(1).broadcast_to([1, 128, 1])), dma=name, waits=waits)

        def prep_lhsT(w_dram, rows, name, out_base=0, ktiles=8, eng="vector"):
            nm = rows // 128

            def cp(e, dst, src):
                if eng == "scalar":
                    return e.activation(dst, src, AF.Copy)
                return e.tensor_copy(dst, src)
            for m in range(nm):
                pw = m % 2
                w = [S.mark("vector"), S.mark("scalar")] if m == 0 else [prep_marks[-1]]
                S.emit("sync", lambda e, m=m, pw=pw: e.dma_start(
                    out=wld[pw][:, 0:ktiles * 128], in_=w_dram[m * 128:(m + 1) * 128, :]),
                    dma="wp", waits=w)
                S.emit(eng, lambda e, pw=pw: cp(
                    e, wldb[:, 0:ktiles * 128], wld[pw][:, 0:ktiles * 128]),
                    waits=[S.dmark("wp")], inc=True)
                cast_mark = S.mark(eng)
                for k in range(ktiles):
                    pt = p_tr[k % 2]
                    wtp = [cast_mark]
                    if m * ktiles + k >= 2:
                        wtp.append(tr_evac_marks[-2])
                    S.emit("tensor", lambda e, k=k, pt=pt: e.transpose(
                        pt[:, :], wldb[:, k * 128:(k + 1) * 128], eye_b[:, :]),
                        waits=wtp, inc=True)
                    col = out_base + (k * nm + m) * 128
                    S.emit(eng, lambda e, col=col, pt=pt: cp(
                        e, wT[:, col:col + 128], pt[:, :]), waits=[S.mark("tensor")], inc=True)
                    tr_evac_marks.append(S.mark(eng))
                prep_marks.append(S.mark(eng))
            return lambda k, m: out_base + (k * nm + m) * 128

        tr_evac_marks = []
        prep_marks = []

        def mm_phase(rhs_dram, nk, nm, wt_of, evac, tag, extra_w=(), rhs_ktiles=None,
                     evac_eng="vector", rhs_w=None, first_w=()):
            kt = rhs_ktiles if rhs_ktiles is not None else nk
            rhs_marks = {}
            grp_marks = {}
            bank_evac = {}
            for n in range(NTC):
                p = n % 2
                w = list(extra_w) if n == 0 else []
                if rhs_w is not None:
                    w += list(rhs_w(n))
                if n >= 2:
                    w.append(grp_marks[(n - 2, nm - 1)])
                for k in range(kt):
                    S.emit("sync", lambda e, n=n, k=k, p=p: e.dma_start(
                        out=rhsb[p][:, k * TC:(k + 1) * TC],
                        in_=rhs_dram[k * 128:(k + 1) * 128, n * TC:(n + 1) * TC]),
                        dma=f"rhs{p}", waits=w if k == 0 else [])
                rhs_marks[n] = S.dmark(f"rhs{p}")
                for m in range(nm):
                    bank = m % 8
                    wm = [rhs_marks[n]]
                    if (n, m) == (0, 0):
                        wm += list(first_w)
                    if (n, m) != (0, 0) and (n * nm + m) >= (8 if nm >= 8 else nm):
                        prev = bank_evac.get(bank)
                        if prev is not None:
                            wm.append(prev)
                    for k in range(nk):
                        S.emit("tensor", lambda e, m=m, k=k, bank=bank, p=p: e.matmul(
                            p_mm[bank][:, :], wT[:, wt_of(k, m):wt_of(k, m) + 128],
                            rhsb[p][:, (k % kt) * TC:((k % kt) + 1) * TC],
                            start=(k == 0), stop=(k == nk - 1)),
                            waits=wm if k == 0 else [], inc=(k == nk - 1))
                    grp_marks[(n, m)] = S.mark("tensor")
                    evac(m, n, bank, grp_marks[(n, m)])
                    bank_evac[bank] = S.mark(evac_eng)

        dyn = {}

        def _conds(e, queue):
            key = f"conds_{queue}"
            if key not in dyn:
                b = e.compute_val(e.partition_id() // 2)
                dyn[key] = [e.compute_val(b == p) for p in range(4)]
            return dyn[key]

        def slot_copy(queue, name, dst_of_p, src_of_p, waits=()):
            for p in range(4):
                S.emit(queue, lambda e, p=p, queue=queue: e.dma_start(
                    out=dst_of_p(p), in_=src_of_p(p), cond=_conds(e, queue)[p]),
                    dma=name, waits=waits if p == 0 else [])

        load_const(eye_b, d_eyeb, "cst")
        load_const(eye_f, d_eyef, "cst")
        cst_done = S.dmark("cst")
        S.emit("vector", lambda e: e.memset(acc1[:, :], 0.0), inc=True)
        S.emit("vector", lambda e: e.memset(cst[:, 0:1], 1e-5), inc=True)
        S.emit("vector", lambda e: e.memset(cst[:, 1:2], 1.0 / DM), inc=True)
        S.emit("vector", lambda e: e.memset(cst[:, 2:3], 1.0), inc=True)

        dA = [sbt(f"dA{p}", [128, CAT]) for p in range(2)]
        bcat = sbt("bcat", [128, CAT], BF16)
        hcat = sbt("hcat", [128, CAT], BF16)
        vcat = sbt("vcat", [128, CAT], BF16)
        Bbc = [sbt(f"Bbc{p}", [128, CAT], BF16) for p in range(2)]
        Cbc = [sbt(f"Cbc{p}", [128, CAT], BF16) for p in range(2)]
        sp_all = sbt("sp_all", [128, NT * TC])
        q_all = sbt("q_all", [128, NT * TC], BF16)
        dtraw = sbt("dtraw", [64, L], BF16)
        escr = sbt("escr", [128, TC])
        dtwT = sbt("dtwT", [64, NT * 128], BF16)

        wld = [sbt(f"wld{p}", [128, 1024]) for p in range(2)]
        wldb = sbt("wldb", [128, 2048], BF16)
        wT = sbt("wT", [128, 12288], BF16)
        rhsb = [sbt(f"rhsb{p}", [128, NT * TC], BF16) for p in range(2)]
        stg = [sbt(f"stg{p}", [128, TC]) for p in range(2)]
        stgb = [sbt(f"stgb{p}", [128, TC], BF16) for p in range(2)]
        zb = sbt("zb", [128, TC], BF16)
        for p in range(2):
            S.emit("vector", lambda e, p=p: e.memset(dA[p][:, :], 0.0), inc=True)
        S.emit("vector", lambda e: e.memset(bcat[:, :], 0.0), inc=True)
        ucc = rhsb
        ystg = stg
        S.emit("vector", lambda e: e.memset(zb[:, :], 0.0), inc=True)
        zf_mark = S.mark("vector")
        zbf = zb.ap().bitcast(F32)[:, 0:256]
        for ti, t4 in enumerate([t_ccbi_a, t_ccbi_b]):
            for rg in range(4 * DM // 128):
                S.emit("gpsimd", lambda e, rg=rg, t4=t4: e.dma_start(
                    out=t4[rg * 128:(rg + 1) * 128, :],
                    in_=zbf.unsqueeze(1).broadcast_to([128, 8, 256])),
                    dma="zf", waits=[zf_mark] if (ti == 0 and rg == 0) else [])
        for t4 in [t_ccli_a, t_ccli_b]:
            for rg in range(4 * DM // 128):
                S.emit("gpsimd", lambda e, rg=rg, t4=t4: e.dma_start(
                    out=t4[rg * 128:(rg + 1) * 128, :],
                    in_=zb[:, :].unsqueeze(1).broadcast_to([128, 4, TC])),
                    dma="zf")
        for rg in range(6):
            S.emit("gpsimd", lambda e, rg=rg: e.dma_start(
                out=t_ccsi[rg * 128:(rg + 1) * 128, :],
                in_=zbf.unsqueeze(1).broadcast_to([128, 16, 256])),
                dma="zf")
        zf_done = S.dmark("zf")
        gt_alias = q_all
        oq_alias = sp_all.ap().bitcast(BF16)

        NXB = 3
        xt = [Bbc[0].ap().bitcast(F32)[:, 0:DM], Bbc[1].ap().bitcast(F32)[:, 0:DM],
              vcat.ap().bitcast(F32)[:, 0:DM]]
        xc = [Cbc[p].ap().bitcast(F32)[:, 0:DM] for p in range(2)]
        sq = aux
        xnb = hcat.ap()[:, 0:DM]
        mstat = sbt("mstat", [128, 8])
        stmax = sbt("stmax", [128, 32])
        ntok = L // 128
        tile_done = {}
        xn_tile = {}

        def fm_pass(tt, out_dram, name, extra_first=()):
            for cblk in range(8):
                pt = p_tr[cblk % 2]
                wtp = [S.mark("vector")]
                if cblk < 2:
                    wtp += list(extra_first)
                if tt == 0 and cblk == 0:
                    wtp.append(cst_done)
                if cblk >= 2:
                    wtp.append(tr_evac_marks[-2])
                S.emit("tensor", lambda e, cblk=cblk, pt=pt: e.transpose(
                    pt[:, :], xnb[:, cblk * 128:(cblk + 1) * 128], eye_b[:, :]),
                    waits=wtp, inc=True)
                wev = [S.mark("tensor")]
                if cblk == 0:
                    wev.append(S.dmark(name))
                S.emit("vector", lambda e, cblk=cblk, pt=pt: e.tensor_copy(
                    auxb[:, cblk * 128:(cblk + 1) * 128], pt[:, :]),
                    waits=wev, inc=True)
                tr_evac_marks.append(S.mark("vector"))
            evm = S.mark("vector")
            out3 = out_dram.rearrange("(c p) l -> p c l", p=128)[:, :, tt * 128:(tt + 1) * 128]
            in3 = auxb.ap().rearrange("p (c w) -> p c w", w=128)
            S.emit("sync", lambda e, out3=out3, in3=in3: e.dma_start(
                out=out3, in_=in3), dma=name, waits=[evm])

        sq2 = [sbt(f"sq{p}", [128, DM]) for p in range(2)]

        def ln_h1(tt):
            p = tt % NXB
            pc = tt % 2
            w = [tile_done[tt - NXB]] if tt >= NXB else []
            S.emit("sync", lambda e, tt=tt, p=p: e.dma_start(
                out=xt[p][:, :], in_=d_x[tt * 128:(tt + 1) * 128, :]), dma=f"xld{p}", waits=w)
            wldx = S.dmark(f"xld{p}")
            S.emit("vector", lambda e, p=p, pc=pc: e.reduce_sum(
                mstat[:, 4 * pc:4 * pc + 1], xt[p][:, :], AX), waits=[wldx], inc=True)
            S.emit("scalar", lambda e, pc=pc: e.activation(
                mstat[:, 4 * pc + 1:4 * pc + 2], mstat[:, 4 * pc:4 * pc + 1],
                AF.Copy, scale=cst[:, 1:2]), waits=[S.mark("vector")], inc=True)
            wsub = [S.mark("scalar")]
            if tt >= 1:
                wsub.append(S.dmark("xntok"))
            S.emit("vector", lambda e, p=p, pc=pc: e.tensor_scalar(
                xc[pc][:, :], xt[p][:, :], mstat[:, 4 * pc + 1:4 * pc + 2], None,
                ALU.subtract), waits=wsub, inc=True)
            S.emit("scalar", lambda e, pc=pc: e.activation(
                sq2[pc][:, :], xc[pc][:, :], AF.Square),
                waits=[S.mark("vector")], inc=True)

        def ln_h2(tt):
            pc = tt % 2
            S.emit("vector", lambda e, pc=pc: e.reduce_sum(
                mstat[:, 4 * pc + 2:4 * pc + 3], sq2[pc][:, :], AX),
                waits=[S.mark("scalar")], inc=True)
            S.emit("scalar", lambda e, pc=pc: e.activation(
                mstat[:, 4 * pc + 3:4 * pc + 4], mstat[:, 4 * pc + 2:4 * pc + 3],
                AF.Sqrt, bias=cst[:, 0:1], scale=cst[:, 1:2]),
                waits=[S.mark("vector")], inc=True)
            S.emit("vector", lambda e, pc=pc: e.reciprocal(
                mstat[:, 4 * pc:4 * pc + 1], mstat[:, 4 * pc + 3:4 * pc + 4]),
                waits=[S.mark("scalar")], inc=True)
            S.gap("vector")
            S.emit("vector", lambda e, pc=pc: e.tensor_scalar(
                xc[pc][:, :], xc[pc][:, :], mstat[:, 4 * pc:4 * pc + 1], None,
                ALU.mult), inc=True)
            xn_mark = S.mark("vector")
            S.emit("vector", lambda e, tt=tt, pc=pc: e.reduce_max(
                stmax[:, tt:tt + 1], xc[pc][:, :], AX, apply_absolute_value=True))
            S.emit("sync", lambda e, tt=tt, pc=pc: e.dma_start(
                out=t_xn_tok[tt * 128:(tt + 1) * 128, :], in_=xc[pc][:, :]),
                dma="xntok", waits=[xn_mark])
            S.emit("vector", lambda e, pc=pc: e.tensor_copy(xnb[:, :], xc[pc][:, :]),
                   waits=[S.mark("tensor")], inc=True)
            fm_pass(tt, t_xn, "xnfm")
            xn_tile[tt] = S.dmark("xnfm")
            tile_done[tt] = S.mark("vector")

        for tt in range(ntok):
            ln_h1(tt)
            if tt >= 1:
                ln_h2(tt - 1)
        ln_h2(ntok - 1)

        S.emit("vector", lambda e: e.reduce_max(acc1[:, 0:1], stmax[:, :], AX),
               waits=[S.mark("vector")], inc=True)
        S.gap("vector")
        S.emit("tensor", lambda e: e.transpose(p_mm[0][0:1, 0:128],
                                               acc1[:, 0:1], eye_f[:, :]),
               waits=[S.mark("vector")], inc=True)
        S.emit("vector", lambda e: e.reduce_max(sc11[0:1, 0:1], p_mm[0][0:1, 0:128], AX),
               waits=[S.mark("tensor")], inc=True)
        maxred_mark = S.mark("vector")
        S.emit("sync", lambda e: e.dma_start(out=t_ccx1[0:1, 0:1], in_=sc11[0:1, 0:1]),
               dma="ccs", waits=[maxred_mark])
        S.emit("gpsimd", lambda e: e.collective_compute(
            "AllReduce", ALU.max, replica_groups=ALL8,
            ins=[t_ccx1.opt()], outs=[t_ccx2.opt()]),
            waits=[S.dmark("ccs")], dma="ccse", dmainc=1)
        S.emit("sync", lambda e: e.dma_start(out=sc11[0:1, 1:2], in_=t_ccx2[0:1, 0:1]),
               dma="ccs", waits=[S.dmark("ccse")])
        S.emit("vector", lambda e: e.tensor_scalar(
            sc11[0:1, 2:3], sc11[0:1, 1:2], 1.0 / QMAX, 1e-12, ALU.mult, ALU.max),
            waits=[S.dmark("ccs")], inc=True)
        S.gap("vector")
        S.emit("vector", lambda e: e.reciprocal(sc11[0:1, 3:4], sc11[0:1, 2:3]), inc=True)
        S.gap("vector")
        bcast_scalar(qsc[:, 1:2], sc11[0:1, 2:3], "qb", waits=[S.mark("vector")])
        bcast_scalar(qsc[:, 0:1], sc11[0:1, 3:4], "qb")

        def emit_quant(mu_done):
            for tt in range(ntok):
                p = tt % NXB
                pc = tt % 2
                w = [S.dmark("qb"), S.dmark("xntok")] if tt == 0 else []
                if tt >= NXB:
                    w.append(tile_done[ntok + tt - NXB])
                S.emit("sync", lambda e, tt=tt, p=p: e.dma_start(
                    out=xt[p][:, :], in_=t_xn_tok[tt * 128:(tt + 1) * 128, :]),
                    dma=f"xld{p}", waits=w)
                S.emit("vector", lambda e, p=p, pc=pc: e.tensor_scalar(
                    xc[pc][:, :], xt[p][:, :], qsc[:, 0:1], RND, ALU.mult, ALU.add),
                    waits=[S.dmark(f"xld{p}")], inc=True)
                S.emit("vector", lambda e, pc=pc: e.tensor_scalar(
                    xc[pc][:, :], xc[pc][:, :], RND, QMAX, ALU.subtract, ALU.min))
                S.emit("vector", lambda e, pc=pc: e.tensor_scalar(
                    xc[pc][:, :], xc[pc][:, :], -QMAX - 1.0, qsc[:, 1:2], ALU.max, ALU.mult))
                S.emit("vector", lambda e, pc=pc: e.tensor_copy(xnb[:, :], xc[pc][:, :]),
                       waits=[S.mark("tensor")], inc=True)
                fm_pass(tt, t_xq, "xqfm", extra_first=[mu_done] if tt == 0 else ())
                tile_done[ntok + tt] = S.mark("vector")

        def scan_phase(i, tag, dtraw_mark, dtwT_mark, bc_done):
            k_idx = [0]
            q_marks = {}
            scan_marks = {}
            v_marks = {}
            pe_marks = {}
            btt_marks = {}
            evac_marks = {}
            dma_y_name = f"ydma{tag}"

            rhs_free = S.mark("tensor")
            for tc in range(NTC):
                ptc = tc % 2
                w = [S.dmark(f"uc{tag}")]
                if tc < 2:
                    w.append(rhs_free)
                if tc >= 2:
                    w.append(evac_marks[(tc - 2, NT - 1)])
                for dt_i in range(NT):
                    S.emit("sync", lambda e, tc=tc, dt_i=dt_i, ptc=ptc: e.dma_start(
                        out=ucc[ptc][:, dt_i * TC:(dt_i + 1) * TC],
                        in_=t_uc[dt_i * 128:(dt_i + 1) * 128, tc * TC:(tc + 1) * TC]),
                        dma=f"ucc{ptc}", waits=w if dt_i == 0 else [])
                ucc_mark = S.dmark(f"ucc{ptc}")
                prev_q_mark = q_marks.get(tc - 1)
                for dt_i in range(NT):
                    bank = dt_i
                    wm = [dtraw_mark, dtwT_mark]
                    if tc > 0:
                        wm.append(evac_marks[(tc - 1, dt_i)])
                    S.emit("tensor", lambda e, tc=tc, dt_i=dt_i, bank=bank: e.matmul(
                        p_mm[bank][:, :], dtwT[0:64, dt_i * 128:(dt_i + 1) * 128],
                        dtraw[0:64, tc * TC:(tc + 1) * TC], start=True, stop=True),
                        waits=wm, inc=True)
                    wsc = [S.mark("tensor")]
                    if dt_i == 0 and prev_q_mark is not None:
                        wsc.append(prev_q_mark)
                    S.emit("scalar", lambda e, dt_i=dt_i, bank=bank: e.activation(
                        escr[:, :], p_mm[bank][:, :], AF.Exp, bias=dtb[:, dt_i:dt_i + 1]),
                        waits=wsc, inc=True)
                    S.emit("scalar", lambda e, dt_i=dt_i: e.activation(
                        sp_all[:, dt_i * TC:(dt_i + 1) * TC], escr[:, :], AF.Ln,
                        bias=cst[:, 2:3]), inc=True)
                    S.emit("vector", lambda e, dt_i=dt_i, ptc=ptc: e.tensor_tensor(
                        q_all[:, dt_i * TC:(dt_i + 1) * TC],
                        sp_all[:, dt_i * TC:(dt_i + 1) * TC],
                        ucc[ptc][:, dt_i * TC:(dt_i + 1) * TC], ALU.mult),
                        waits=[S.mark("scalar"), ucc_mark], inc=True)
                q_mark = S.mark("vector")
                q_marks[tc] = q_mark
                sp_mark = S.mark("scalar")
                if tc == 0 and i == 0 and "sp" in DBG2:
                    S.emit("sync", lambda e: e.dma_start(out=DBG2["sp"], in_=sp_all[:, :]),
                           dma="dbgsp", waits=[sp_mark, q_mark])
                    S.emit("sync", lambda e: e.dma_start(out=DBG2["q"], in_=q_all[:, :]),
                           dma="dbgsp", waits=[sp_mark, q_mark])
                    S.emit("sync", lambda e: e.dma_start(out=DBG2["an"], in_=Aneg[:, :]),
                           dma="dbgsp", waits=[sp_mark, q_mark])

                import os as _os2
                NGl = int(_os2.environ.get("DBGNG", str(NG)))
                for g in range(NGl):
                    pg = (tc * NG + g) % 2
                    gi = tc * NG + g
                    wB = [q_mark, bc_done]
                    if gi >= 2:
                        wB.append(btt_marks.get(gi - 2, S.mark("vector")))
                    for seg in range(GS):
                        srow = g * GS + seg
                        S.emit("sync", lambda e, srow=srow, seg=seg, pg=pg, tc=tc: e.dma_start(
                            out=Bbc[pg][:, seg * SEG + 2:seg * SEG + 2 + TC],
                            in_=t_Bc[srow:srow + 1, tc * TC:(tc + 1) * TC]
                            .unsqueeze(1).broadcast_to([1, 128, TC])),
                            dma=f"bb{pg}", waits=wB if seg == 0 else [])
                    wC = [bc_done]
                    if gi >= 2:
                        wC.append(v_marks.get((gi - 2, NT - 1), S.mark("vector")))
                    for seg in range(GS):
                        srow = g * GS + seg
                        S.emit("gpsimd", lambda e, srow=srow, seg=seg, pg=pg, tc=tc: e.dma_start(
                            out=Cbc[pg][:, seg * SEG + 2:seg * SEG + 2 + TC],
                            in_=t_Cc[srow:srow + 1, tc * TC:(tc + 1) * TC]
                            .unsqueeze(1).broadcast_to([1, 128, TC])),
                            dma=f"cb{pg}", waits=wC if seg == 0 else [])
                    bbc_mark = S.dmark(f"bb{pg}")
                    cbc_mark = S.dmark(f"cb{pg}")
                    if tc == 0 and g == 0 and i == 0 and "sp" in DBG2:
                        S.emit("sync", lambda e: e.dma_start(out=DBG2["bb"], in_=Bbc[0][:, :]),
                               dma="dbgsp", waits=[bbc_mark, cbc_mark])
                        S.emit("sync", lambda e: e.dma_start(out=DBG2["cb"], in_=Cbc[0][:, :]),
                               dma="dbgsp", waits=[bbc_mark, cbc_mark])

                    for dt_i in range(NT):
                        k = k_idx[0]
                        pk = k % 2
                        import os as _os
                        _ser = bool(_os.environ.get("DBGSER"))
                        wA = [sp_mark]
                        if _ser and k >= 1:
                            wA.append(scan_marks[k - 1])
                        if k >= 2:
                            wA.append(scan_marks[k - 2])
                        for seg in range(GS):
                            s_idx = g * GS + seg
                            S.emit("scalar", lambda e, dt_i=dt_i, seg=seg, s_idx=s_idx, pk=pk:
                                   e.activation(
                                       dA[pk][:, seg * SEG + 2:seg * SEG + 2 + TC],
                                       sp_all[:, dt_i * TC:(dt_i + 1) * TC], AF.Exp,
                                       scale=Aneg[:, dt_i * 64 + s_idx:dt_i * 64 + s_idx + 1]),
                                   waits=wA if seg == 0 else [], inc=(seg == GS - 1))
                        dA_mark = S.mark("scalar")
                        wD = [bbc_mark, dA_mark]
                        S.emit("vector", lambda e, dt_i=dt_i, g=g: e.tensor_copy(
                            seg3(bcat, 1, 2),
                            carry[:, dt_i * 64 + g * GS:dt_i * 64 + (g + 1) * GS].unsqueeze(2)),
                            waits=wD)
                        S.emit("vector", lambda e, dt_i=dt_i, pg=pg: e.tensor_tensor(
                            seg3(bcat, 2, SEG),
                            q_all[:, dt_i * TC:(dt_i + 1) * TC].unsqueeze(1).broadcast_to([128, GS, TC]),
                            seg3(Bbc[pg], 2, SEG), ALU.mult), inc=True)
                        btt_marks[gi] = S.mark("vector")
                        S.emit("vector", lambda e, pk=pk: e.tensor_tensor_scan(
                            hcat[:, :], dA[pk][:, :], bcat[:, :], 0.0, ALU.mult, ALU.add),
                            inc=True)
                        scan_marks[k] = S.mark("vector")
                        wV = [cbc_mark]
                        if k >= 1:
                            wV.append(pe_marks[k - 1])
                        S.emit("vector", lambda e, pg=pg: e.tensor_tensor(
                            seg3(vcat, 2, SEG), seg3(hcat, 2, SEG),
                            seg3(Cbc[pg], 2, SEG), ALU.mult), waits=wV)
                        S.emit("vector", lambda e, dt_i=dt_i, g=g: e.tensor_copy(
                            carry[:, dt_i * 64 + g * GS:dt_i * 64 + (g + 1) * GS].unsqueeze(2),
                            seg3(hcat, SEG - 1, SEG)), inc=True)
                        v_marks[(gi, dt_i)] = S.mark("vector")
                        import os
                        _dbgg = int(os.environ.get("DBGG", "0"))
                        if tc == 0 and g == _dbgg and dt_i == 0 and i == 0 and "hc" in DBG2:
                            S.emit("sync", lambda e: e.dma_start(out=DBG2["hc"], in_=hcat[:, :]),
                                   dma="dbgh", waits=[v_marks[(gi, dt_i)]])
                            S.emit("sync", lambda e: e.dma_start(out=DBG2["vc"], in_=vcat[:, :]),
                                   dma="dbgh", waits=[v_marks[(gi, dt_i)]])
                            S.emit("sync", lambda e: e.dma_start(out=DBG2["bc2"], in_=bcat[:, :]),
                                   dma="dbgh", waits=[v_marks[(gi, dt_i)]])
                            S.emit("sync", lambda e, pk=pk: e.dma_start(out=DBG2["da"], in_=dA[pk][:, :]),
                                   dma="dbgh", waits=[v_marks[(gi, dt_i)]])
                            S.emit("vector", lambda e: e.engine_nop(),
                                   waits=[S.dmark("dbgh")], inc=True)
                        for seg in range(GS):
                            S.emit("tensor", lambda e, seg=seg, dt_i=dt_i, g=g, NGl=NGl:
                                   e.matmul(
                                       p_mm[dt_i][:, :], eye_b[:, :],
                                       vcat[:, seg * SEG + 2:seg * SEG + 2 + TC],
                                       start=(g == 0 and seg == 0),
                                       stop=(g == NGl - 1 and seg == GS - 1)),
                                   waits=[v_marks[(gi, dt_i)]] if seg == 0 else [],
                                   inc=(seg == GS - 1))
                        pe_marks[k] = S.mark("tensor")
                        if tc == 0 and g == int(_os2.environ.get("DBGPP", "0")) and dt_i == 0 and i == 0 and "pp" in DBG2:
                            S.emit("vector", lambda e: e.tensor_copy(
                                escr[:, :], p_mm[0][:, :]), waits=[pe_marks[k]], inc=True)
                            S.emit("sync", lambda e: e.dma_start(out=DBG2["pp"], in_=escr[:, :]),
                                   dma="dbgp", waits=[S.mark("vector")])
                            S.emit("tensor", lambda e: e.nop(), waits=[S.dmark("dbgp")])
                            S.emit("vector", lambda e: e.engine_nop(), waits=[S.dmark("dbgp")])
                        k_idx[0] += 1

                for dt_i in range(NT):
                    p = dt_i % 2
                    wE = [pe_marks[k_idx[0] - NT + dt_i]]
                    if tc * NT + dt_i >= 2:
                        wE.append(S.dmark(dma_y_name))
                    S.emit("vector", lambda e, dt_i=dt_i, p=p, ptc=ptc: e.scalar_tensor_tensor(
                        ystg[p][:, :], ucc[ptc][:, dt_i * TC:(dt_i + 1) * TC],
                        dskip[:, dt_i:dt_i + 1], p_mm[dt_i][:, :], ALU.mult, ALU.add),
                        waits=wE, inc=True)
                    evac_marks[(tc, dt_i)] = S.mark("vector")
                    S.emit("sync", lambda e, dt_i=dt_i, tc=tc, p=p: e.dma_start(
                        out=t_y[dt_i * 128:(dt_i + 1) * 128, tc * TC:(tc + 1) * TC],
                        in_=ystg[p][:, :]), dma=dma_y_name, waits=[evac_marks[(tc, dt_i)]])
                    if tc == 0 and dt_i == 0 and i == 0 and "ys" in DBG2:
                        S.emit("sync", lambda e, p=p: e.dma_start(
                            out=DBG2["ys"], in_=ystg[p][:, :]),
                            dma="dbgy", waits=[evac_marks[(tc, dt_i)]])
                        S.emit("sync", lambda e, ptc=ptc: e.dma_start(
                            out=DBG2["uc0"], in_=ucc[ptc][:, 0:TC]),
                            dma="dbgy", waits=[evac_marks[(tc, dt_i)]])
            return S.dmark(dma_y_name)

        def mamba_layer(i, rhs_in, rhs_w_of=None, mid_emit=None):
            tag = f"L{i}"
            for dst, src in [(convw, d_convw[i]), (convb, d_convb[i]), (dtb, d_dtb[i]),
                             (dskip, d_dskip[i])]:
                load_const(dst, src, f"cv{tag}",
                           waits=[S.mark("vector"), S.mark("scalar")] if i > 0 else ())
            S.emit("sync", lambda e: e.dma_start(out=escr[:, 0:NT * 64], in_=d_alog[i]),
                   dma=f"cv{tag}")
            S.emit("scalar", lambda e: e.activation(Aneg[:, :], escr[:, 0:NT * 64], AF.Exp),
                   waits=[S.dmark(f"cv{tag}")], inc=True)
            S.emit("vector", lambda e: e.tensor_scalar(
                Aneg[:, :], Aneg[:, :], -1.0, None, ALU.mult),
                waits=[S.mark("scalar")], inc=True)
            S.emit("vector", lambda e: e.memset(carry[:, :], 0.0), inc=True)
            aneg_mark = S.mark("scalar")

            wu_of = prep_lhsT(d_inw[i][0:DH, :], DH, f"{tag}u", eng="scalar")
            wt_ready = S.mark("scalar")
            ur_marks = {}

            def evac_u(m, n, bank, pe_mark):
                p = (n * 8 + m) % 2
                wv = [pe_mark]
                if n * 8 + m >= 2:
                    wv.append(S.dmark(f"ur{tag}"))
                S.emit("scalar", lambda e, bank=bank, p=p: e.activation(
                    stg[p][:, :], p_mm[bank][:, :], AF.Copy), waits=wv, inc=True)
                S.emit("sync", lambda e, m=m, n=n, p=p: e.dma_start(
                    out=t_uraw[m * 128:(m + 1) * 128, n * TC:(n + 1) * TC],
                    in_=stg[p][:, :]), dma=f"ur{tag}", waits=[S.mark("scalar")])
                ur_marks[(m, n)] = S.dmark(f"ur{tag}")
            mm_phase(rhs_in, 8, 8, wu_of, evac_u, f"{tag}u", extra_w=[wt_ready],
                     evac_eng="scalar", rhs_w=rhs_w_of,
                     first_w=[maxred_mark] if i == 0 else [S.mark("vector")])
            mu_done = S.mark("scalar")

            CC = 512
            ub = [sp_all.ap()[:, j * 520:j * 520 + CC + 4] for j in range(4)]
            ucb = [oq_alias[:, 4160 + p * CC:4160 + (p + 1) * CC] for p in range(2)]
            cacc = [escr, aux[:, 0:TC]]
            for j in range(4):
                S.emit("vector", lambda e, j=j: e.memset(ub[j][:, 0:3], 0.0), inc=True)
            silu_marks = {}
            uc_marks = {}
            nch = L // CC
            for ci in range(NT * nch):
                dt_i, ch = divmod(ci, nch)
                p = ci % 4
                pc = ci % 2
                wl = [ur_marks[(dt_i, ch)]]
                if ch > 0:
                    wl.append(ur_marks[(dt_i, ch - 1)])
                if ci >= 4:
                    wl.append(silu_marks[ci - 4])
                lo = ch * CC - 3
                if ch == 0:
                    S.emit("sync", lambda e, dt_i=dt_i, ch=ch, p=p: e.dma_start(
                        out=ub[p][:, 3:CC + 3],
                        in_=t_uraw[dt_i * 128:(dt_i + 1) * 128, ch * CC:(ch + 1) * CC]),
                        dma=f"ul{p}", waits=wl)
                    if ci > 0:
                        S.emit("vector", lambda e, p=p: e.memset(ub[p][:, 0:3], 0.0),
                               waits=[silu_marks[ci - 4]] if ci >= 4 else [])
                else:
                    S.emit("sync", lambda e, dt_i=dt_i, lo=lo, p=p: e.dma_start(
                        out=ub[p][:, 0:CC + 3],
                        in_=t_uraw[dt_i * 128:(dt_i + 1) * 128, lo:lo + CC + 3]),
                        dma=f"ul{p}", waits=wl)
                wts = [S.dmark(f"ul{p}")]
                if ci < 2:
                    wts.append(aneg_mark)
                else:
                    wts.append(silu_marks[ci - 2])
                S.emit("vector", lambda e, dt_i=dt_i, p=p, pc=pc: e.tensor_scalar(
                    cacc[pc][:, :], ub[p][:, 0:CC], convw[:, dt_i * 4:dt_i * 4 + 1], None,
                    ALU.mult), waits=wts, inc=True)
                for kk in range(1, 4):
                    S.emit("vector", lambda e, dt_i=dt_i, p=p, kk=kk, pc=pc: e.scalar_tensor_tensor(
                        cacc[pc][:, :], ub[p][:, kk:CC + kk],
                        convw[:, dt_i * 4 + kk:dt_i * 4 + kk + 1],
                        cacc[pc][:, :], ALU.mult, ALU.add), inc=(kk == 3))
                wsl = [S.mark("vector")]
                if ci >= 2:
                    wsl.append(uc_marks[ci - 2])
                S.emit("scalar", lambda e, dt_i=dt_i, pc=pc: e.activation(
                    ucb[pc][:, :], cacc[pc][:, :], AF.Silu, bias=convb[:, dt_i:dt_i + 1]),
                    waits=wsl, inc=True)
                silu_marks[ci] = S.mark("scalar")
                S.emit("gpsimd", lambda e, dt_i=dt_i, ch=ch, pc=pc: e.dma_start(
                    out=t_uc[dt_i * 128:(dt_i + 1) * 128, ch * CC:(ch + 1) * CC],
                    in_=ucb[pc][:, :]), dma=f"uc{tag}", waits=[silu_marks[ci]])
                uc_marks[ci] = S.dmark(f"uc{tag}")

            if mid_emit is not None:
                mid_emit(mu_done)

            wz_of = prep_lhsT(d_inw[i][DH:2 * DH, :], DH, f"{tag}z", eng="scalar")
            wt_ready = S.mark("scalar")

            def evac_z(m, n, bank, pe_mark):
                p = (n * 8 + m) % 2
                wv = [pe_mark]
                if n * 8 + m >= 2:
                    wv.append(S.dmark(f"zr{tag}"))
                S.emit("scalar", lambda e, bank=bank, p=p: e.activation(
                    stgb[p][:, :], p_mm[bank][:, :], AF.Copy), waits=wv, inc=True)
                S.emit("sync", lambda e, m=m, n=n, p=p: e.dma_start(
                    out=t_z[m * 128:(m + 1) * 128, n * TC:(n + 1) * TC],
                    in_=stgb[p][:, :]), dma=f"zr{tag}", waits=[S.mark("scalar")])
            mm_phase(rhs_in, 8, 8, wz_of, evac_z, f"{tag}z", extra_w=[wt_ready],
                     evac_eng="scalar", rhs_w=rhs_w_of)
            mz_done = S.mark("scalar")

            nmx = 2
            for m in range(nmx):
                rows = 128 if m == 0 else 64
                wv = [S.mark("tensor")] if m > 0 else [S.mark("vector"), S.mark("scalar")]
                S.emit("sync", lambda e, m=m, rows=rows: e.dma_start(
                    out=wld[0][0:rows, 0:DM], in_=d_xpw[i][m * 128:m * 128 + rows, :]),
                    dma=f"wx{tag}", waits=wv)
                S.emit("vector", lambda e, rows=rows: e.tensor_copy(
                    wldb[0:rows, 0:DM], wld[0][0:rows, 0:DM]),
                    waits=[S.dmark(f"wx{tag}")], inc=True)
                cast_mark = S.mark("vector")
                for k in range(8):
                    pt = p_tr[k % 2]
                    wtp = [cast_mark]
                    if m * 8 + k < 2:
                        wtp.append(mz_done)
                    if m * 8 + k >= 2:
                        wtp.append(tr_evac_marks[-2])
                    S.emit("tensor", lambda e, k=k, rows=rows, pt=pt: e.transpose(
                        pt[0:128, 0:rows], wldb[0:rows, k * 128:(k + 1) * 128],
                        eye_b[0:rows, 0:rows]), waits=wtp, inc=True)
                    col = (k * nmx + m) * 128
                    S.emit("vector", lambda e, col=col, rows=rows, pt=pt: e.tensor_copy(
                        wT[:, col:col + rows], pt[0:128, 0:rows]),
                        waits=[S.mark("tensor")], inc=True)
                    tr_evac_marks.append(S.mark("vector"))
            wt_ready = S.mark("vector")

            def evac_x(m, n, bank, pe_mark):
                rows = 128 if m == 0 else 64
                p = (n * 2 + m) % 2
                wv = [pe_mark]
                if n * 2 + m >= 2:
                    wv.append(S.dmark(f"xd{tag}"))
                S.emit("vector", lambda e, rows=rows, bank=bank, p=p: e.tensor_copy(
                    stg[p][0:rows, :], p_mm[bank][0:rows, :]), waits=wv, inc=True)
                S.emit("sync", lambda e, m=m, n=n, rows=rows, p=p: e.dma_start(
                    out=t_smp[m * 128:m * 128 + rows, n * TC:(n + 1) * TC],
                    in_=stg[p][0:rows, :]), dma=f"xd{tag}", waits=[S.mark("vector")])
            mm_phase(t_uc, 8, 2, lambda k, m: (k * nmx + m) * 128, evac_x, f"{tag}x",
                     extra_w=[wt_ready, S.dmark(f"uc{tag}")], first_w=[mz_done])
            slot_copy("gpsimd", "sci",
                      lambda p: t_ccsi[p * 192:(p + 1) * 192, :],
                      lambda p: t_smp[:, :], waits=[S.dmark(f"xd{tag}"), zf_done])
            S.emit("gpsimd", lambda e: e.collective_compute(
                "AllReduce", ALU.add, replica_groups=ALL8,
                ins=[t_ccsi.opt()], outs=[t_ccso.opt()]),
                waits=[S.dmark("sci")], dma="ccc", dmainc=1)
            slot_copy("scalar", "sco",
                      lambda p: t_smf[:, :],
                      lambda p: t_ccso[p * 192:(p + 1) * 192, :],
                      waits=[S.dmark("ccc")])
            ccc_mark = S.dmark("sco")
            for gi_, (dst, r0) in enumerate([(None, 0), (t_Bc, 64), (t_Cc, 128)]):
                for ch in range(4):
                    p = ch % 2
                    wv = [ccc_mark]
                    if gi_ * 4 + ch >= 2:
                        wv.append(S.mark("vector"))
                    S.emit("sync", lambda e, r0=r0, ch=ch, p=p: e.dma_start(
                        out=wld[p][0:64, 0:1024],
                        in_=t_smf[r0:r0 + 64, ch * 1024:(ch + 1) * 1024]),
                        dma=f"xu{tag}", waits=wv)
                    if dst is None:
                        S.emit("vector", lambda e, ch=ch, p=p: e.tensor_copy(
                            dtraw[:, ch * 1024:(ch + 1) * 1024], wld[p][0:64, 0:1024]),
                            waits=[S.dmark(f"xu{tag}")], inc=True)
                    else:
                        wbk = [S.dmark(f"xu{tag}")]
                        if gi_ * 4 + ch < 2:
                            wbk += [S.dmark("xqfm"), S.dmark("op")]
                        if gi_ * 4 + ch >= 2:
                            wbk.append(S.dmark(f"bc{tag}"))
                        S.emit("vector", lambda e, ch=ch, p=p: e.tensor_copy(
                            auxb[0:64, 0:1024], wld[p][0:64, 0:1024]), waits=wbk, inc=True)
                        S.emit("sync", lambda e, dst=dst, ch=ch: e.dma_start(
                            out=dst[0:64, ch * 1024:(ch + 1) * 1024], in_=auxb[0:64, 0:1024]),
                            dma=f"bc{tag}", waits=[S.mark("vector")])
            dtraw_mark = S.mark("vector")
            bc_done = S.dmark(f"bc{tag}")

            S.emit("sync", lambda e: e.dma_start(out=wld[1][:, 0:NT * 64], in_=d_dtw[i]),
                   dma=f"dw{tag}", waits=[S.mark("vector"), S.mark("scalar")])
            S.emit("vector", lambda e: e.tensor_copy(wldb[:, 0:NT * 64], wld[1][:, 0:NT * 64]),
                   waits=[S.dmark(f"dw{tag}")], inc=True)
            cast_mark = S.mark("vector")
            for k in range(NT):
                pt = p_tr[k % 2]
                wtp = [cast_mark]
                if k >= 2:
                    wtp.append(tr_evac_marks[-2])
                S.emit("tensor", lambda e, k=k, pt=pt: e.transpose(
                    pt[0:64, 0:128], wldb[:, k * 64:(k + 1) * 64], eye_b[:, :]),
                    waits=wtp, inc=True)
                S.emit("vector", lambda e, k=k, pt=pt: e.tensor_copy(
                    dtwT[0:64, k * 128:(k + 1) * 128], pt[0:64, 0:128]),
                    waits=[S.mark("tensor")], inc=True)
                tr_evac_marks.append(S.mark("vector"))
            dtwT_mark = S.mark("vector")

            wo_of = prep_lhsT(d_outw[i], DM, f"{tag}o", eng="scalar")
            wt_ready_o = S.mark("scalar")

            y_done = scan_phase(i, tag, dtraw_mark, dtwT_mark, bc_done)

            gt = gt_alias
            zl = stgb
            zs = escr
            yl = [sp_all.ap()[:, p * TC:(p + 1) * TC] for p in range(2)]
            g_marks = {}
            mm_done = {}
            bank_evac = {}
            for n in range(NTC):
                for k in range(NT):
                    p = k % 2
                    wv = [S.dmark(f"zr{tag}"), y_done]
                    if (n, k) != (0, 0):
                        if n * NT + k >= 2:
                            wv.append(g_marks[n * NT + k - 2])
                    S.emit("sync", lambda e, n=n, k=k, p=p: e.dma_start(
                        out=zl[p][:, :], in_=t_z[k * 128:(k + 1) * 128, n * TC:(n + 1) * TC]),
                        dma=f"zl{p}", waits=wv)
                    S.emit("sync", lambda e, n=n, k=k, p=p: e.dma_start(
                        out=yl[p][:, :], in_=t_y[k * 128:(k + 1) * 128, n * TC:(n + 1) * TC]),
                        dma=f"yl{p}")
                    S.emit("scalar", lambda e, p=p: e.activation(
                        zs[:, :], zl[p][:, :], AF.Silu),
                        waits=[S.dmark(f"zl{p}"), S.mark("vector")], inc=True)
                    wg_ = [S.dmark(f"yl{p}"), S.mark("scalar")]
                    if n > 0:
                        wg_.append(mm_done.get((n - 1, NT - 1), (None, 0)))
                    S.emit("vector", lambda e, k=k, p=p: e.tensor_tensor(
                        gt[:, k * TC:(k + 1) * TC], yl[p][:, :], zs[:, :], ALU.mult),
                        waits=[x for x in wg_ if x[0] is not None], inc=True)
                    g_marks[n * NT + k] = S.mark("vector")
                gm = S.mark("vector")
                for m in range(8):
                    bank = m
                    wm = [gm]
                    prev = bank_evac.get(bank)
                    if prev is not None:
                        wm.append(prev)
                    for k in range(NT):
                        S.emit("tensor", lambda e, m=m, k=k, bank=bank: e.matmul(
                            p_mm[bank][:, :], wT[:, wo_of(k, m):wo_of(k, m) + 128],
                            gt[:, k * TC:(k + 1) * TC],
                            start=(k == 0), stop=(k == NT - 1)),
                            waits=(wm + [wt_ready_o]) if k == 0 else [], inc=(k == NT - 1))
                    mm_done[(n, m)] = S.mark("tensor")
                    p = (n * 8 + m) % 2
                    wv = [mm_done[(n, m)]]
                    if n * 8 + m >= 2:
                        wv.append(S.dmark(f"op{tag}"))
                    if i == 0:
                        S.emit("vector", lambda e, bank=bank, p=p: e.tensor_copy(
                            auxb[:, p * TC:(p + 1) * TC], p_mm[bank][:, :]),
                            waits=wv, inc=True)
                        bank_evac[bank] = S.mark("vector")
                        S.emit("sync", lambda e, m=m, n=n, p=p: e.dma_start(
                            out=t_bg1[m * 128:(m + 1) * 128, n * TC:(n + 1) * TC],
                            in_=auxb[:, p * TC:(p + 1) * TC]),
                            dma=f"op{tag}", waits=[bank_evac[bank]])
                    else:
                        S.emit("vector", lambda e, bank=bank, p=p: e.tensor_copy(
                            stg[p][:, :], p_mm[bank][:, :]), waits=wv, inc=True)
                        bank_evac[bank] = S.mark("vector")
                        S.emit("sync", lambda e, m=m, n=n, p=p: e.dma_start(
                            out=t_m2p[m * 128:(m + 1) * 128, n * TC:(n + 1) * TC],
                            in_=stg[p][:, :]), dma=f"op{tag}", waits=[bank_evac[bank]])
                if i == 0 and n == 3:
                    opA = S.dmark(f"op{tag}")
                    slot_copy("scalar", "bcia",
                              lambda p: t_ccli_a[p * DM:(p + 1) * DM, :],
                              lambda p: t_bg1[:, 0:L // 2], waits=[opA, zf_done])
                    S.emit("gpsimd", lambda e: e.collective_compute(
                        "AllReduce", ALU.add, replica_groups=ALL8,
                        ins=[t_ccli_a.opt()], outs=[t_cclo_a.opt()]),
                        waits=[S.dmark("bcia")], dma="cca", dmainc=1)
            if i == 0:
                opB = S.dmark(f"op{tag}")
                slot_copy("scalar", "bcib",
                          lambda p: t_ccli_b[p * DM:(p + 1) * DM, :],
                          lambda p: t_bg1[:, L // 2:L], waits=[opB, zf_done])
                S.emit("gpsimd", lambda e: e.collective_compute(
                    "AllReduce", ALU.add, replica_groups=ALL8,
                    ins=[t_ccli_b.opt()], outs=[t_cclo_b.opt()]),
                    waits=[S.dmark("bcib")], dma="ccb", dmainc=1)
                slot_copy("scalar", "bcoa",
                          lambda p: t_l2in[:, 0:L // 2],
                          lambda p: t_cclo_a[p * DM:(p + 1) * DM, :],
                          waits=[S.dmark("cca")])
                bcoa_m = S.dmark("bcoa")
                slot_copy("gpsimd", "bcob",
                          lambda p: t_l2in[:, L // 2:L],
                          lambda p: t_cclo_b[p * DM:(p + 1) * DM, :],
                          waits=[S.dmark("ccb")])
                bcob_m = S.dmark("bcob")
                return bcoa_m, bcob_m
            return None

        def qdq3(dstb, src, inv_col, s_col, lo, hi, out_bf=False):
            S.emit("vector", lambda e: e.tensor_scalar(
                aux[:, 0:TC], src, inv_col, RND, ALU.mult, ALU.add), inc=True)
            S.emit("vector", lambda e: e.tensor_scalar(
                aux[:, 0:TC], aux[:, 0:TC], RND, hi, ALU.subtract, ALU.min))
            S.emit("vector", lambda e, dstb=dstb: e.tensor_scalar(
                dstb, aux[:, 0:TC], lo, s_col, ALU.max, ALU.mult), inc=True)

        def dequant_prep(w_dram, nrow_tiles, ncols, name, out_base=0):
            CW = 512
            ncc = ncols // CW
            nbc = CW // 32
            for m in range(nrow_tiles):
                for cch in range(ncc):
                    pw = (m * ncc + cch) % 2
                    wv = [S.mark("vector")] if m * ncc + cch == 0 else [S.mark("vector")]
                    S.emit("sync", lambda e, m=m, cch=cch, pw=pw: e.dma_start(
                        out=wld[pw][:, 0:CW],
                        in_=w_dram[m * 128:(m + 1) * 128, cch * CW:(cch + 1) * CW]),
                        dma="wp", waits=wv)
                    w3 = wld[pw][:, 0:CW].rearrange("p (nb b) -> p nb b", b=32)
                    S.emit("vector", lambda e, w3=w3: e.tensor_reduce(
                        aux[:, 0:nbc], w3, AX, ALU.max, apply_absolute_value=True),
                        waits=[S.dmark("wp")], inc=True)
                    S.gap("vector")
                    S.emit("vector", lambda e: e.tensor_scalar(
                        aux[:, 0:nbc], aux[:, 0:nbc], 1.0 / 7.0, 1e-12, ALU.mult, ALU.max))
                    S.gap("vector")
                    S.emit("vector", lambda e: e.reciprocal(
                        aux[:, nbc:2 * nbc], aux[:, 0:nbc]), inc=True)
                    S.gap("vector")
                    rq3 = aux[:, 512:512 + CW].rearrange("p (nb b) -> p nb b", b=32)
                    S.emit("vector", lambda e, w3=w3, rq3=rq3: e.tensor_tensor(
                        rq3, w3, aux[:, nbc:2 * nbc].unsqueeze(2).broadcast_to([128, nbc, 32]),
                        ALU.mult), waits=[S.mark("tensor")] if cch == 0 else [], inc=True)
                    S.emit("vector", lambda e, pw=pw: e.tensor_scalar(
                        wld[pw][:, 0:CW], aux[:, 512:512 + CW],
                        RND, RND, ALU.add, ALU.subtract), inc=True)
                    S.emit("vector", lambda e, pw=pw: e.tensor_scalar(
                        wld[pw][:, 0:CW], wld[pw][:, 0:CW], 7.0, -8.0, ALU.min, ALU.max))
                    wq3 = wldb[:, cch * CW:(cch + 1) * CW].rearrange("p (nb b) -> p nb b", b=32)
                    S.emit("vector", lambda e, pw=pw, wq3=wq3: e.tensor_tensor(
                        wq3, wld[pw][:, 0:CW].rearrange("p (nb b) -> p nb b", b=32),
                        aux[:, 0:nbc].unsqueeze(2).broadcast_to([128, nbc, 32]),
                        ALU.mult), inc=True)
                cast_mark = S.mark("vector")
                kt = ncols // 128
                for k in range(kt):
                    pt = p_tr[k % 2]
                    wtp = [cast_mark]
                    if m * kt + k >= 2:
                        wtp.append(tr_evac_marks[-2])
                    S.emit("tensor", lambda e, k=k, pt=pt: e.transpose(
                        pt[:, :], wldb[:, k * 128:(k + 1) * 128], eye_b[:, :]),
                        waits=wtp, inc=True)
                    col = out_base + (k * nrow_tiles + m) * 128
                    S.emit("vector", lambda e, col=col, pt=pt: e.tensor_copy(
                        wT[:, col:col + 128], pt[:, :]), waits=[S.mark("tensor")], inc=True)
                    tr_evac_marks.append(S.mark("vector"))
                prep_marks.append(S.mark("vector"))
            return lambda k, m: out_base + (k * nrow_tiles + m) * 128

        def collective_scalar_max(src_colidx, dst_colidx, ccname):
            S.emit("tensor", lambda e, src_colidx=src_colidx: e.transpose(
                p_mm[0][0:1, 0:128], acc1[:, src_colidx:src_colidx + 1], eye_f[:, :]),
                waits=[S.mark("vector")], inc=True)
            S.emit("vector", lambda e, dst_colidx=dst_colidx: e.reduce_max(
                sc11[0:1, dst_colidx:dst_colidx + 1], p_mm[0][0:1, 0:128], AX),
                waits=[S.mark("tensor")], inc=True)
            S.emit("sync", lambda e, dst_colidx=dst_colidx: e.dma_start(
                out=t_ccx1[0:1, 0:1], in_=sc11[0:1, dst_colidx:dst_colidx + 1]),
                dma="ccs", waits=[S.mark("vector")])
            S.emit("gpsimd", lambda e: e.collective_compute(
                "AllReduce", ALU.max, replica_groups=ALL8,
                ins=[t_ccx1.opt()], outs=[t_ccx2.opt()]),
                waits=[S.dmark("ccs")], dma="ccse", dmainc=1)
            S.emit("sync", lambda e, dst_colidx=dst_colidx: e.dma_start(
                out=sc11[0:1, dst_colidx:dst_colidx + 1], in_=t_ccx2[0:1, 0:1]),
                dma="ccs", waits=[S.dmark("ccse")])

        def ffn_phase():
            for which, w_dram, t_raw, acc_col in [("g", d_wg, t_grw, 1), ("u", d_wu, t_upw, 2)]:
                wf_of = dequant_prep(w_dram, NKI, DM, f"f{which}")
                wt_ready = S.mark("vector")
                S.emit("vector", lambda e: e.memset(stmaxf[:, :], 0.0), inc=True)

                def evac_f(m, n, bank, pe_mark, t_raw=t_raw, acc_col=acc_col, which=which):
                    p = (n * NKI + m) % 2
                    wv = [pe_mark]
                    if n * NKI + m >= 2:
                        wv.append(S.dmark("fr"))
                    S.emit("vector", lambda e, bank=bank, p=p: e.tensor_copy(
                        stg[p][:, :], p_mm[bank][:, :]), waits=wv, inc=True)
                    S.emit("vector", lambda e, p=p, m=m, n=n: e.reduce_max(
                        stmaxf[:, (m * NTC + n) % 128:(m * NTC + n) % 128 + 1],
                        stg[p][:, :], AX, apply_absolute_value=True), inc=True)
                    S.emit("sync", lambda e, m=m, n=n, p=p, t_raw=t_raw: e.dma_start(
                        out=t_raw[m * 128:(m + 1) * 128, n * TC:(n + 1) * TC],
                        in_=stg[p][:, :]), dma="fr", waits=[S.mark("vector")])
                mm_phase(t_xq, 8, NKI, wf_of, evac_f, f"f{which}",
                         extra_w=[wt_ready, S.dmark("xqfm")])
                S.emit("vector", lambda e, acc_col=acc_col: e.reduce_max(
                    acc1[:, acc_col:acc_col + 1], stmaxf[:, :], AX),
                    waits=[S.mark("vector")], inc=True)
                S.gap("vector")
                collective_scalar_max(acc_col, 4 + (0 if which == "g" else 1), f"m{which}")
            for src, dst in [(4, (2, 3)), (5, (4, 5))]:
                S.emit("vector", lambda e, src=src: e.tensor_scalar(
                    sc11[0:1, 6:7], sc11[0:1, src:src + 1], 1.0 / QMAX, 1e-12, ALU.mult, ALU.max),
                    waits=[S.dmark("ccs"), S.dmark("qb")], inc=True)
                S.gap("vector")
                S.emit("vector", lambda e: e.reciprocal(sc11[0:1, 7:8], sc11[0:1, 6:7]), inc=True)
                S.gap("vector")
                bcast_scalar(qsc[:, dst[1]:dst[1] + 1], sc11[0:1, 6:7], "qb",
                             waits=[S.mark("vector")])
                bcast_scalar(qsc[:, dst[0]:dst[0] + 1], sc11[0:1, 7:8], "qb")

            with ExitStack() as c6:
                fl = [dA[0].ap()[:, p * TC:(p + 1) * TC] for p in range(2)]
                gq = [escr, stg[1]]
                sg = [stg[0], auxb.ap().bitcast(F32)[:, 0:TC]]
                S.emit("vector", lambda e: e.memset(stmaxf[:, :], 0.0), inc=True)
                nchunks = NKI * NTC
                st_marks = {}
                sig_marks = {}

                def p2_sa(ci):
                    m, n = divmod(ci, NTC)
                    p = ci % 2
                    wv = [S.dmark("qb"), S.dmark("fr")]
                    if ci >= 2:
                        wv += [st_marks[ci - 2], S.dmark("g2w")]
                    S.emit("sync", lambda e, m=m, n=n, p=p: e.dma_start(
                        out=fl[p][:, :], in_=t_grw[m * 128:(m + 1) * 128, n * TC:(n + 1) * TC]),
                        dma=f"g2l{p}", waits=wv)
                    wts = [S.dmark(f"g2l{p}")]
                    if ci >= 2:
                        wts.append(st_marks[ci - 2])
                    S.emit("vector", lambda e, p=p: e.tensor_scalar(
                        aux[:, 0:TC], fl[p][:, :], qsc[:, 2:3], RND, ALU.mult, ALU.add),
                        waits=wts, inc=True)
                    S.emit("vector", lambda e: e.tensor_scalar(
                        aux[:, 0:TC], aux[:, 0:TC], RND, QMAX, ALU.subtract, ALU.min))
                    S.emit("vector", lambda e, p=p: e.tensor_scalar(
                        gq[p][:, :], aux[:, 0:TC], -QMAX - 1.0, qsc[:, 3:4], ALU.max, ALU.mult),
                        inc=True)
                    wsg = [S.mark("vector")]
                    if ci >= 2:
                        wsg.append(st_marks[ci - 2])
                    S.emit("scalar", lambda e, p=p: e.activation(
                        sg[p][:, :], gq[p][:, :], AF.Sigmoid), waits=wsg, inc=True)
                    sig_marks[ci] = S.mark("scalar")

                def p2_sb(ci):
                    m, n = divmod(ci, NTC)
                    p = ci % 2
                    S.emit("vector", lambda e, p=p: e.tensor_scalar(
                        aux[:, TC:2 * TC], sg[p][:, :], 65536.0, RND, ALU.mult, ALU.add),
                        waits=[sig_marks[ci]], inc=True)
                    S.emit("vector", lambda e: e.tensor_scalar(
                        aux[:, TC:2 * TC], aux[:, TC:2 * TC], RND, 65535.0, ALU.subtract, ALU.min))
                    S.emit("vector", lambda e, p=p: e.tensor_scalar(
                        sg[p][:, :], aux[:, TC:2 * TC], 0.0, 1.0 / 65536.0, ALU.max, ALU.mult))
                    S.emit("vector", lambda e, p=p: e.tensor_tensor(
                        fl[p][:, :], gq[p][:, :], sg[p][:, :], ALU.mult), inc=True)
                    st_marks[ci] = S.mark("vector")
                    S.emit("vector", lambda e, p=p, ci=ci: e.reduce_max(
                        stmaxf[:, ci % 128:ci % 128 + 1], fl[p][:, :], AX,
                        apply_absolute_value=True))
                    S.emit("sync", lambda e, m=m, n=n, p=p: e.dma_start(
                        out=t_g2w[m * 128:(m + 1) * 128, n * TC:(n + 1) * TC],
                        in_=fl[p][:, :]), dma="g2w", waits=[S.mark("vector")])

                for ci in range(nchunks):
                    p2_sa(ci)
                    if ci >= 1:
                        p2_sb(ci - 1)
                p2_sb(nchunks - 1)
                S.emit("vector", lambda e: e.reduce_max(acc1[:, 3:4], stmaxf[:, :], AX),
                       waits=[S.mark("vector")], inc=True)
                S.gap("vector")
                collective_scalar_max(3, 6, "m2")
                S.emit("vector", lambda e: e.tensor_scalar(
                    sc11[0:1, 6:7], sc11[0:1, 6:7], 1.0 / QMAX, 1e-12, ALU.mult, ALU.max),
                    waits=[S.dmark("ccs"), S.dmark("qb")], inc=True)
                S.gap("vector")
                S.emit("vector", lambda e: e.reciprocal(sc11[0:1, 7:8], sc11[0:1, 6:7]), inc=True)
                S.gap("vector")
                bcast_scalar(qsc[:, 3:4], sc11[0:1, 6:7], "qb", waits=[S.mark("vector")])
                bcast_scalar(qsc[:, 2:3], sc11[0:1, 7:8], "qb")

                S.emit("vector", lambda e: e.memset(stmaxf[:, :], 0.0),
                       waits=[S.dmark("qb")], inc=True)
                st_marks = {}
                for ci in range(nchunks):
                    m, n = divmod(ci, NTC)
                    p = ci % 2
                    wv = [S.dmark("g2w"), S.dmark("fr")]
                    if ci >= 2:
                        wv.append(st_marks[ci - 2])
                    S.emit("sync", lambda e, m=m, n=n, p=p: e.dma_start(
                        out=fl[p][:, :], in_=t_g2w[m * 128:(m + 1) * 128, n * TC:(n + 1) * TC]),
                        dma=f"o3a{p}", waits=wv)
                    S.emit("sync", lambda e, m=m, n=n, p=p: e.dma_start(
                        out=gq[p][:, :],
                        in_=t_upw[m * 128:(m + 1) * 128, n * TC:(n + 1) * TC]),
                        dma=f"o3b{p}")
                    upbuf = lambda p: gq[p]
                    S.emit("vector", lambda e, p=p: e.tensor_scalar(
                        aux[:, 0:TC], fl[p][:, :], qsc[:, 2:3], RND, ALU.mult, ALU.add),
                        waits=[S.dmark(f"o3a{p}"), S.dmark(f"o3b{p}")], inc=True)
                    S.emit("vector", lambda e: e.tensor_scalar(
                        aux[:, 0:TC], aux[:, 0:TC], RND, QMAX, ALU.subtract, ALU.min))
                    S.emit("vector", lambda e, p=p: e.tensor_scalar(
                        fl[p][:, :], aux[:, 0:TC], -QMAX - 1.0, qsc[:, 3:4], ALU.max, ALU.mult))
                    S.emit("vector", lambda e, p=p: e.tensor_scalar(
                        aux[:, 0:TC], upbuf(p)[:, :], qsc[:, 4:5], RND, ALU.mult, ALU.add))
                    S.emit("vector", lambda e: e.tensor_scalar(
                        aux[:, 0:TC], aux[:, 0:TC], RND, QMAX, ALU.subtract, ALU.min))
                    S.emit("vector", lambda e, p=p: e.tensor_scalar(
                        upbuf(p)[:, :], aux[:, 0:TC], -QMAX - 1.0, qsc[:, 5:6], ALU.max, ALU.mult))
                    S.emit("vector", lambda e, p=p: e.tensor_tensor(
                        fl[p][:, :], fl[p][:, :], upbuf(p)[:, :], ALU.mult), inc=True)
                    st_marks[ci] = S.mark("vector")
                    S.emit("vector", lambda e, p=p, ci=ci: e.reduce_max(
                        stmaxf[:, ci % 128:ci % 128 + 1], fl[p][:, :], AX,
                        apply_absolute_value=True))
                    S.emit("sync", lambda e, m=m, n=n, p=p: e.dma_start(
                        out=t_ow[m * 128:(m + 1) * 128, n * TC:(n + 1) * TC],
                        in_=fl[p][:, :]), dma="o3w", waits=[S.mark("vector")])
                S.emit("vector", lambda e: e.reduce_max(acc1[:, 3:4], stmaxf[:, :], AX),
                       waits=[S.mark("vector")], inc=True)
                S.gap("vector")
                collective_scalar_max(3, 6, "m3")
                S.emit("vector", lambda e: e.tensor_scalar(
                    sc11[0:1, 6:7], sc11[0:1, 6:7], 1.0 / QMAX, 1e-12, ALU.mult, ALU.max),
                    waits=[S.dmark("ccs"), S.dmark("qb")], inc=True)
                S.gap("vector")
                S.emit("vector", lambda e: e.reciprocal(sc11[0:1, 7:8], sc11[0:1, 6:7]), inc=True)
                S.gap("vector")
                bcast_scalar(qsc[:, 7:8], sc11[0:1, 6:7], "qb", waits=[S.mark("vector")])
                bcast_scalar(qsc[:, 6:7], sc11[0:1, 7:8], "qb")

            wd_of = dequant_prep(d_wd, 8, IH, "fd")
            wt_ready = S.mark("vector")
            with ExitStack() as c7:
                oq = oq_alias
                ol = [dA[1].ap()[:, p * TC:(p + 1) * TC] for p in range(2)]
                m2l = [escr, ystg[0]]
                oq_marks = {}
                mm_done = {}
                bank_evac = {}
                for n in range(NTC):
                    for k in range(NKI):
                        p = k % 2
                        wv = [S.dmark("o3w"), S.dmark("qb")]
                        if n * NKI + k >= 2:
                            wv.append(oq_marks[n * NKI + k - 2])
                        if n > 0:
                            wv.append(mm_done.get((n - 1, 7), (None, 0)))
                        S.emit("sync", lambda e, n=n, k=k, p=p: e.dma_start(
                            out=ol[p][:, :], in_=t_ow[k * 128:(k + 1) * 128, n * TC:(n + 1) * TC]),
                            dma=f"odl{p}", waits=[x for x in wv if x[0] is not None])
                        S.emit("vector", lambda e, p=p: e.tensor_scalar(
                            aux[:, 0:TC], ol[p][:, :], qsc[:, 6:7], RND, ALU.mult, ALU.add),
                            waits=[S.dmark(f"odl{p}")], inc=True)
                        S.emit("vector", lambda e: e.tensor_scalar(
                            aux[:, 0:TC], aux[:, 0:TC], RND, QMAX, ALU.subtract, ALU.min))
                        S.emit("vector", lambda e, k=k: e.tensor_scalar(
                            oq[:, k * TC:(k + 1) * TC], aux[:, 0:TC], -QMAX - 1.0, qsc[:, 7:8],
                            ALU.max, ALU.mult), inc=True)
                        oq_marks[n * NKI + k] = S.mark("vector")
                    oqm = S.mark("vector")
                    for m in range(8):
                        bank = m
                        wm = [oqm, wt_ready]
                        prev = bank_evac.get(bank)
                        if prev is not None:
                            wm.append(prev)
                        for k in range(NKI):
                            S.emit("tensor", lambda e, m=m, k=k, bank=bank: e.matmul(
                                p_mm[bank][:, :], wT[:, wd_of(k, m):wd_of(k, m) + 128],
                                oq[:, k * TC:(k + 1) * TC],
                                start=(k == 0), stop=(k == NKI - 1)),
                                waits=wm if k == 0 else [], inc=(k == NKI - 1))
                        mm_done[(n, m)] = S.mark("tensor")
                        p = (n * 8 + m) % 2
                        S.emit("sync", lambda e, m=m, n=n, p=p: e.dma_start(
                            out=m2l[p][:, :], in_=t_m2p[m * 128:(m + 1) * 128, n * TC:(n + 1) * TC]),
                            dma=f"m2l{p}", waits=[S.dmark("op")])
                        wv = [mm_done[(n, m)], S.dmark(f"m2l{p}")]
                        if n * 8 + m >= 2:
                            wv.append(S.dmark("sumw"))
                        S.emit("vector", lambda e, bank=bank, p=p: e.tensor_tensor(
                            stg[p][:, :], p_mm[bank][:, :], m2l[p][:, :], ALU.add),
                            waits=wv, inc=True)
                        bank_evac[bank] = S.mark("vector")
                        S.emit("sync", lambda e, m=m, n=n, p=p: e.dma_start(
                            out=t_bgp[m * 128:(m + 1) * 128, n * TC:(n + 1) * TC],
                            in_=stg[p][:, :]), dma="sumw", waits=[bank_evac[bank]])
                    if n == 3:
                        sumA = S.dmark("sumw")
                        slot_copy("scalar", "bfia",
                                  lambda p: t_ccbi_a[p * DM:(p + 1) * DM, :],
                                  lambda p: t_bgp[:, 0:L // 2], waits=[sumA, zf_done])
                        S.emit("gpsimd", lambda e: e.collective_compute(
                            "AllReduce", ALU.add, replica_groups=ALL8,
                            ins=[t_ccbi_a.opt()], outs=[t_ccbo_a.opt()]),
                            waits=[S.dmark("bfia")], dma="cfa", dmainc=1)
            sumB = S.dmark("sumw")
            slot_copy("scalar", "bfib",
                      lambda p: t_ccbi_b[p * DM:(p + 1) * DM, :],
                      lambda p: t_bgp[:, L // 2:L], waits=[sumB, zf_done])
            S.emit("gpsimd", lambda e: e.collective_compute(
                "AllReduce", ALU.add, replica_groups=ALL8,
                ins=[t_ccbi_b.opt()], outs=[t_ccbo_b.opt()]),
                waits=[S.dmark("bfib")], dma="cfb", dmainc=1)
            slot_copy("sync", "bfoa",
                      lambda p: t_bgf[:, 0:L // 2],
                      lambda p: t_ccbo_a[p * DM:(p + 1) * DM, :],
                      waits=[S.dmark("cfa")])
            bfoa_m = S.dmark("bfoa")
            slot_copy("gpsimd", "bfob",
                      lambda p: t_bgf[:, L // 2:L],
                      lambda p: t_ccbo_b[p * DM:(p + 1) * DM, :],
                      waits=[S.dmark("cfb")])
            bfob_m = S.dmark("bfob")
            return bfoa_m, bfob_m

        def final_phase(bfoa_m, bfob_m):
            fstart = S.mark("vector")
            with ExitStack() as c8:
                sl = [aux[:, i * 128:(i + 1) * 128] for i in range(4)]
                xl = [rhsb[p].ap().bitcast(F32)[:, 0:DM] for p in range(2)]
                ob = [dA[p].ap()[:, 0:DM] for p in range(2)]
                ntok = L // 128
                add_marks = {}
                pe_tr = {}
                for tt in range(ntok):
                    p = tt % 2
                    half_m = bfoa_m if tt < ntok // 2 else bfob_m
                    wv = [half_m]
                    if tt >= 2:
                        wv.append(add_marks[tt - 2])
                    S.emit("sync", lambda e, tt=tt, p=p: e.dma_start(
                        out=xl[p][:, :], in_=d_x[tt * 128:(tt + 1) * 128, :]), dma=f"xf{p}", waits=wv)
                    for c in range(8):
                        psl = sl[c % 4]
                        wv2 = [half_m]
                        if (tt * 8 + c) >= 4:
                            wv2.append(pe_tr[tt * 8 + c - 4])
                        S.emit("sync", lambda e, tt=tt, c=c, psl=psl: e.dma_start(
                            out=psl[:, :],
                            in_=t_bgf[c * 128:(c + 1) * 128, tt * 128:(tt + 1) * 128]),
                            dma=f"sf{c % 4}", waits=wv2)
                        ptp = p_tr[c % 2]
                        wv3 = [S.dmark(f"sf{c % 4}"), fstart]
                        if (tt * 8 + c) >= 2:
                            wv3.append(add_marks.get(tt - 2, (None, 0)))
                        S.emit("tensor", lambda e, psl=psl, c=c: e.transpose(
                            p_mm[c % 4][:, 0:128], psl[:, :], eye_f[:, :]),
                            waits=[x for x in wv3 if x[0] is not None], inc=True)
                        pe_tr[tt * 8 + c] = S.mark("tensor")
                        S.emit("vector", lambda e, c=c, p=p: e.tensor_tensor(
                            ob[p][:, c * 128:(c + 1) * 128], p_mm[c % 4][0:128, 0:128],
                            xl[p][:, c * 128:(c + 1) * 128], ALU.add),
                            waits=[pe_tr[tt * 8 + c], S.dmark(f"xf{p}")], inc=True)
                    add_marks[tt] = S.mark("vector")
                    S.emit("sync", lambda e, tt=tt, p=p: e.dma_start(
                        out=d_out[tt * 128:(tt + 1) * 128, :], in_=ob[p][:, :]),
                        dma="outw", waits=[add_marks[tt]])
                S.emit("sync", lambda e: e.dma_start(out=sc11[0:1, 9:10], in_=t_ccx2[0:1, 0:1]),
                       dma="fin", waits=[S.dmark("outw")])

        if STAGE >= 1:
            pass
        if STAGE >= 2:
            l0_marks = mamba_layer(0, t_xn,
                                   rhs_w_of=lambda n: [xn_tile[4 * n + 3]],
                                   mid_emit=emit_quant)
        if STAGE >= 4:
            bcoa_m, bcob_m = l0_marks
            mamba_layer(1, t_l2in,
                        rhs_w_of=lambda n, a=bcoa_m, b=bcob_m: [a if n < 4 else b])
        if STAGE >= 5:
            ff_marks = ffn_phase()
        if STAGE >= 6:
            final_phase(*ff_marks)
        for tag, (dap, src, shp, dt) in dbg_outs.items():
            name = f"dbg{tag}"
            wv = [S.mark(e) for e in ENGINES] + [S.dmark(n) for n in list(S.dsems)]
            first = True
            for r in range(0, shp[0], 128):
                rows = min(128, shp[0] - r)
                S.emit("sync", lambda e, r=r, rows=rows, dap=dap, src=src: e.dma_start(
                    out=dap[r:r + rows, :], in_=src[r:r + rows, :]),
                    dma=name, waits=wv if first else [])
                first = False
            S.emit("sync", lambda e: e.dma_start(out=sc11[0:1, 10:11], in_=t_ccx2[0:1, 0:1]),
                   dma="fin2", waits=[S.dmark(name)])
        if STAGE < 6:
            wv = [S.mark(e) for e in ENGINES] + [S.dmark("xntok")]
            for tt in range(L // 128):
                S.emit("sync", lambda e, tt=tt: e.dma_start(
                    out=d_out[tt * 128:(tt + 1) * 128, :],
                    in_=t_xn_tok[tt * 128:(tt + 1) * 128, :]),
                    dma="outw", waits=wv if tt == 0 else [])
            S.emit("sync", lambda e: e.dma_start(out=sc11[0:1, 9:10], in_=t_ccx2[0:1, 0:1]),
                   dma="fin", waits=[S.dmark("outw")])

        blk = ctx.enter_context(nc.Block())
        S.finalize(blk)
    return nc, dbg_outs


def _slice_inputs(inputs, core):
    b, j = divmod(core, 2)
    f32 = np.float32

    def resh(v):
        v = np.ascontiguousarray(v)
        if v.ndim == 1:
            return np.ascontiguousarray(v.reshape(8, 128).T)
        k = v.shape[1]
        return np.ascontiguousarray(v.reshape(8, 128, k).transpose(1, 0, 2).reshape(128, 8 * k))

    m = {}
    x = np.asarray(inputs["L0_attn_x"], f32)
    m["x_tok"] = np.ascontiguousarray(x[b])
    for i in range(2):
        inw = np.asarray(inputs["L0_attn_in_proj_w"], f32)[i]
        m[f"inw{i}"] = np.ascontiguousarray(
            np.concatenate([inw[j * DH:(j + 1) * DH], inw[2 * DH + j * DH:2 * DH + (j + 1) * DH]], 0))
        m[f"convw{i}"] = resh(np.asarray(inputs["L0_attn_conv_w"], f32)[i][j * DH:(j + 1) * DH])
        m[f"convb{i}"] = resh(np.asarray(inputs["L0_attn_conv_b"], f32)[i][j * DH:(j + 1) * DH])
        m[f"xpw{i}"] = np.ascontiguousarray(
            np.asarray(inputs["L0_attn_x_proj_w"], f32)[i][:, j * DH:(j + 1) * DH])
        m[f"dtw{i}"] = resh(np.asarray(inputs["L0_attn_dt_proj_w"], f32)[i][j * DH:(j + 1) * DH])
        m[f"dtb{i}"] = resh(np.asarray(inputs["L0_attn_dt_proj_b"], f32)[i][j * DH:(j + 1) * DH])
        m[f"alog{i}"] = resh(np.asarray(inputs["L0_attn_A_log"], f32)[i][j * DH:(j + 1) * DH])
        m[f"dskip{i}"] = resh(np.asarray(inputs["L0_attn_D_skip"], f32)[i][j * DH:(j + 1) * DH])
        m[f"outw{i}"] = np.ascontiguousarray(
            np.asarray(inputs["L0_attn_out_proj_w"], f32)[i][:, j * DH:(j + 1) * DH])
    m["wg"] = np.ascontiguousarray(np.asarray(inputs["L0_ffn_w_gate"], f32)[j * IH:(j + 1) * IH])
    m["wu"] = np.ascontiguousarray(np.asarray(inputs["L0_ffn_w_up"], f32)[j * IH:(j + 1) * IH])
    m["wd"] = np.ascontiguousarray(np.asarray(inputs["L0_ffn_w_down"], f32)[:, j * IH:(j + 1) * IH])
    m["eye_b_in"] = np.eye(128, dtype=ml_dtypes.bfloat16)
    m["eye_f_in"] = np.eye(128, dtype=np.float32)
    return m


def kernel(**inputs):
    nc, _ = build_program(dbg=DBG)
    in_maps = [_slice_inputs(inputs, c) for c in range(8)]
    res = run_bass_kernel_spmd(nc, in_maps, core_ids=list(range(8)))
    out = np.stack([np.asarray(res.results[2 * b]["out"], np.float32) for b in range(4)], 0)
    return out


# version 07 (final)
# speedup vs baseline: 1.0600x; 1.0147x over previous
import numpy as np
import ml_dtypes
from contextlib import ExitStack

import concourse.bass as bass
import concourse.mybir as mybir
from concourse.bass_utils import run_bass_kernel_spmd

F32 = mybir.dt.float32
BF16 = mybir.dt.bfloat16
AF = mybir.ActivationFunctionType
ALU = mybir.AluOpType
AX = mybir.AxisListType.X

L = 4096
DM = 1024
DH = 1024
NT = 8
NG = 8
GS = 8
TC = 512
NTC = L // TC
SEG = TC + 2
CAT = GS * SEG
IH = 1536
NKI = IH // 128
RND = 12582912.0
QMAX = 32767.0

STAGE = 6
DBG = ()

ENGINES = ("sync", "vector", "scalar", "tensor", "gpsimd")
PAIRS = [[0, 1], [2, 3], [4, 5], [6, 7]]
ALL8 = [list(range(8))]


class Sched:
    def __init__(self, nc, ctx):
        self.nc = nc
        self.ops = {e: [] for e in ENGINES}
        self.psem = {e: ctx.enter_context(nc.semaphore(f"sp_{e}")) for e in ENGINES}
        self.pcnt = {e: 0 for e in ENGINES}
        self.dsems = {}
        self.ctx = ctx

    def _dsem(self, name):
        name = name.replace("L0", "").replace("L1", "")
        if name not in self.dsems:
            h = self.ctx.enter_context(self.nc.semaphore(f"sd_{name}"))
            self.dsems[name] = [h, 0]
        return self.dsems[name]

    def emit(self, eng, fn, waits=(), inc=False, dma=None, dmainc=16):
        incs = []
        if inc:
            self.pcnt[eng] += 1
            incs.append((self.psem[eng], 1))
        if dma is not None:
            s = self._dsem(dma)
            s[1] += dmainc
            incs.append((s[0], dmainc))
        ws = tuple((h, v) for (h, v) in waits if v > 0)
        self.ops[eng].append((fn, ws, tuple(incs)))

    def mark(self, eng):
        return (self.psem[eng], self.pcnt[eng])

    def gap(self, eng, n=6):
        for _ in range(n):
            self.emit(eng, lambda e: e.engine_nop())

    def dmark(self, name):
        s = self._dsem(name)
        return (s[0], s[1])

    def finalize(self, block):
        def mk(name):
            def body(eng):
                for fn, waits, incs in self.ops[name]:
                    for (h, v) in waits:
                        eng.wait_ge(h, v)
                    ins = fn(eng)
                    for (h, v) in incs:
                        ins = ins.then_inc(h, v)
            return body
        block.sync(mk("sync"))
        block.vector(mk("vector"))
        block.scalar(mk("scalar"))
        block.tensor(mk("tensor"))
        block.gpsimd(mk("gpsimd"))


def seg3(t, c0, c1):
    return t.ap().rearrange("p (s c) -> p s c", c=SEG)[:, :, c0:c1]


def build_program(dbg=()):
    nc = bass.Bass("TRN2", target_bir_lowering=False, debug=False)

    d_x = nc.dram_tensor("x_tok", [L, DM], F32, kind="ExternalInput").ap()
    d_inw = [nc.dram_tensor(f"inw{i}", [2 * DH, DM], F32, kind="ExternalInput").ap() for i in range(2)]
    d_convw = [nc.dram_tensor(f"convw{i}", [128, NT * 4], F32, kind="ExternalInput").ap() for i in range(2)]
    d_convb = [nc.dram_tensor(f"convb{i}", [128, NT], F32, kind="ExternalInput").ap() for i in range(2)]
    d_xpw = [nc.dram_tensor(f"xpw{i}", [192, DH], F32, kind="ExternalInput").ap() for i in range(2)]
    d_dtw = [nc.dram_tensor(f"dtw{i}", [128, NT * 64], F32, kind="ExternalInput").ap() for i in range(2)]
    d_dtb = [nc.dram_tensor(f"dtb{i}", [128, NT], F32, kind="ExternalInput").ap() for i in range(2)]
    d_alog = [nc.dram_tensor(f"alog{i}", [128, NT * 64], F32, kind="ExternalInput").ap() for i in range(2)]
    d_dskip = [nc.dram_tensor(f"dskip{i}", [128, NT], F32, kind="ExternalInput").ap() for i in range(2)]
    d_outw = [nc.dram_tensor(f"outw{i}", [DM, DH], F32, kind="ExternalInput").ap() for i in range(2)]
    d_wg = nc.dram_tensor("wg", [IH, DM], F32, kind="ExternalInput").ap()
    d_wu = nc.dram_tensor("wu", [IH, DM], F32, kind="ExternalInput").ap()
    d_wd = nc.dram_tensor("wd", [DM, IH], F32, kind="ExternalInput").ap()
    d_eyeb = nc.dram_tensor("eye_b_in", [128, 128], BF16, kind="ExternalInput").ap()
    d_eyef = nc.dram_tensor("eye_f_in", [128, 128], F32, kind="ExternalInput").ap()
    d_out = nc.dram_tensor("out", [L, DM], F32, kind="ExternalOutput").ap()

    ds = lambda n, shp, dt=F32: nc.dram_tensor(n, shp, dt).ap()
    t_xn_tok = ds("s_xn_tok", [L, DM])
    t_xn = ds("s_xn", [DM, L], BF16)
    t_xq = ds("s_xq", [DM, L], BF16)
    t_uraw = ds("s_uraw", [DH, L])
    t_uc = ds("s_uc", [DH, L], BF16)
    t_z = ds("s_z", [DH, L], BF16)
    t_y = ds("s_y", [DH, L])
    t_m2p = ds("s_m2p", [DM, L])
    t_l2in = ds("s_l2in", [DM, L], BF16)
    t_ccsi = nc.dram_tensor("cc_sm_in", [4 * 192, L], F32).ap()
    t_ccso = nc.dram_tensor("cc_sm_out", [4 * 192, L], F32, addr_space="Shared").ap()
    LQ = L // 4
    t_ccbi_q = [nc.dram_tensor(f"cc_big_in_{q}", [4 * DM, LQ], F32).ap() for q in range(4)]
    t_ccbo_q = [nc.dram_tensor(f"cc_big_out_{q}", [4 * DM, LQ], F32, addr_space="Shared").ap()
                for q in range(4)]
    t_ccli_q = [nc.dram_tensor(f"cc_l2_in_{q}", [4 * DM, LQ], BF16).ap() for q in range(4)]
    t_cclo_q = [nc.dram_tensor(f"cc_l2_out_{q}", [4 * DM, LQ], BF16, addr_space="Shared").ap()
                for q in range(4)]
    t_bg1 = ds("s_bg1", [DM, L], BF16)
    t_ccx1 = ds("cc_x_in", [1, 4])
    t_ccx2 = nc.dram_tensor("cc_x_out", [1, 4], F32, addr_space="Shared").ap()
    t_grw = ds("s_gate", [IH, L])
    t_upw = ds("s_up", [IH, L])
    t_g2w = ds("s_gate2", [IH, L])
    t_ow = ds("s_o", [IH, L])
    t_Bc = ds("s_Bc", [64, L], BF16)
    t_Cc = ds("s_Cc", [64, L], BF16)
    t_smp = ds("s_smp", [192, L])
    t_smf = ds("s_smf", [192, L])
    t_bgp = ds("s_bgp", [DM, L])
    t_bgf = ds("s_bgf", [DM, L])

    dbg_specs = {"xn": (t_xn, [DM, L], BF16), "uraw": (t_uraw, [DH, L], F32),
                 "uc": (t_uc, [DH, L], BF16), "z": (t_z, [DH, L], BF16),
                 "xdbl": (t_ccso, [192, L], F32), "y": (t_y, [DH, L], F32),
                 "m1": (t_bgf, [DM, L], F32), "xq": (t_xq, [DM, L], BF16),
                 "gate": (t_grw, [IH, L], F32), "up": (t_upw, [IH, L], F32),
                 "g2": (t_g2w, [IH, L], F32), "o": (t_ow, [IH, L], F32),
                 "m2p": (t_m2p, [DM, L], F32), "xntok": (t_xn_tok, [L, DM], F32)}
    DBG2 = {}
    if "sp" in dbg:
        DBG2["sp"] = nc.dram_tensor("dbg2_sp", [128, NT * TC], F32, kind="ExternalOutput").ap()
        DBG2["q"] = nc.dram_tensor("dbg2_q", [128, NT * TC], BF16, kind="ExternalOutput").ap()
        DBG2["an"] = nc.dram_tensor("dbg2_an", [128, NT * 64], F32, kind="ExternalOutput").ap()
        DBG2["bb"] = nc.dram_tensor("dbg2_bb", [128, CAT], BF16, kind="ExternalOutput").ap()
        DBG2["cb"] = nc.dram_tensor("dbg2_cb", [128, CAT], BF16, kind="ExternalOutput").ap()
    if "ys" in dbg:
        DBG2["ys"] = nc.dram_tensor("dbg2_ys", [128, TC], F32, kind="ExternalOutput").ap()
        DBG2["uc0"] = nc.dram_tensor("dbg2_uc0", [128, TC], BF16, kind="ExternalOutput").ap()
    if "pp" in dbg:
        DBG2["pp"] = nc.dram_tensor("dbg2_pp", [128, TC], F32, kind="ExternalOutput").ap()
    if "hc" in dbg:
        DBG2["hc"] = nc.dram_tensor("dbg2_hc", [128, CAT], BF16, kind="ExternalOutput").ap()
        DBG2["vc"] = nc.dram_tensor("dbg2_vc", [128, CAT], BF16, kind="ExternalOutput").ap()
        DBG2["bc2"] = nc.dram_tensor("dbg2_bc2", [128, CAT], BF16, kind="ExternalOutput").ap()
        DBG2["da"] = nc.dram_tensor("dbg2_da", [128, CAT], F32, kind="ExternalOutput").ap()
    dbg_outs = {}
    for tag in dbg:
        if tag in ("sp", "hc", "ys", "pp"):
            continue
        src, shp, dt = dbg_specs[tag]
        dbg_outs[tag] = (nc.dram_tensor("dbg_" + tag, shp, dt, kind="ExternalOutput").ap(), src, shp, dt)

    with ExitStack() as ctx:
        sbt = lambda n, shp, dt=F32: ctx.enter_context(nc.sbuf_tensor(n, shp, dt))
        pst = lambda n, shp, dt=F32: ctx.enter_context(nc.psum_tensor(n, shp, dt))
        S = Sched(nc, ctx)

        eye_b = sbt("eye_b", [128, 128], BF16)
        eye_f = sbt("eye_f", [128, 128], F32)
        convw = sbt("convw", [128, NT * 4])
        convb = sbt("convb", [128, NT])
        dtb = sbt("dtb", [128, NT])
        dskip = sbt("dskip", [128, NT])
        Aneg = sbt("Aneg", [128, NT * 64])
        carry = sbt("carry", [128, NT * 64], BF16)
        qsc = sbt("qsc", [128, 8])
        sc11 = sbt("sc11", [1, 16])
        cst = sbt("cstc", [128, 4])

        aux = sbt("aux", [128, 1024])
        auxb = sbt("auxb", [128, 1024], BF16)
        acc1 = sbt("acc1", [128, 4])
        st1 = sbt("st1", [128, 4])
        stmaxf = sbt("stmaxf", [128, 128])

        p_mm = [pst(f"p_mm{i}", [128, 512]) for i in range(8)]
        p_tr = [p_mm[6].ap().bitcast(BF16)[:, 0:128], p_mm[7].ap().bitcast(BF16)[:, 0:128]]
        p_tr4 = [p_mm[b].ap().bitcast(BF16)[:, 0:128] for b in (4, 5, 6, 7)]

        def load_const(dst, src, name, waits=()):
            S.emit("sync", lambda e, d=dst, s=src: e.dma_start(out=d[:, :], in_=s),
                   dma=name, waits=waits)

        def bcast_scalar(dst_col, src_row, name, waits=()):
            S.emit("sync", lambda e, d=dst_col, s=src_row: e.dma_start(
                out=d, in_=s.unsqueeze(1).broadcast_to([1, 128, 1])), dma=name, waits=waits)

        def prep_lhsT(w_dram, rows, name, out_base=0, ktiles=8, eng="vector"):
            nm = rows // 128

            def cp(e, dst, src):
                if eng == "scalar":
                    return e.activation(dst, src, AF.Copy)
                return e.tensor_copy(dst, src)
            for m in range(nm):
                pw = m % 2
                w = [S.mark("vector"), S.mark("scalar")] if m == 0 else [prep_marks[-1]]
                S.emit("sync", lambda e, m=m, pw=pw: e.dma_start(
                    out=wld[pw][:, 0:ktiles * 128], in_=w_dram[m * 128:(m + 1) * 128, :]),
                    dma="wp", waits=w)
                S.emit(eng, lambda e, pw=pw: cp(
                    e, wldb[:, 0:ktiles * 128], wld[pw][:, 0:ktiles * 128]),
                    waits=[S.dmark("wp")], inc=True)
                cast_mark = S.mark(eng)
                for k in range(ktiles):
                    pt = p_tr[k % 2]
                    wtp = [cast_mark]
                    if m * ktiles + k >= 2:
                        wtp.append(tr_evac_marks[-2])
                    S.emit("tensor", lambda e, k=k, pt=pt: e.transpose(
                        pt[:, :], wldb[:, k * 128:(k + 1) * 128], eye_b[:, :]),
                        waits=wtp, inc=True)
                    col = out_base + (k * nm + m) * 128
                    S.emit(eng, lambda e, col=col, pt=pt: cp(
                        e, wT[:, col:col + 128], pt[:, :]), waits=[S.mark("tensor")], inc=True)
                    tr_evac_marks.append(S.mark(eng))
                prep_marks.append(S.mark(eng))
            return lambda k, m: out_base + (k * nm + m) * 128

        tr_evac_marks = []
        prep_marks = []

        def mm_phase(rhs_dram, nk, nm, wt_of, evac, tag, extra_w=(), rhs_ktiles=None,
                     evac_eng="vector", rhs_w=None, first_w=()):
            kt = rhs_ktiles if rhs_ktiles is not None else nk
            rhs_marks = {}
            grp_marks = {}
            bank_evac = {}
            for n in range(NTC):
                p = n % 2
                w = list(extra_w) if n == 0 else []
                if rhs_w is not None:
                    w += list(rhs_w(n))
                if n >= 2:
                    w.append(grp_marks[(n - 2, nm - 1)])
                for k in range(kt):
                    S.emit("sync", lambda e, n=n, k=k, p=p: e.dma_start(
                        out=rhsb[p][:, k * TC:(k + 1) * TC],
                        in_=rhs_dram[k * 128:(k + 1) * 128, n * TC:(n + 1) * TC]),
                        dma=f"rhs{p}", waits=w if k == 0 else [])
                rhs_marks[n] = S.dmark(f"rhs{p}")
                for m in range(nm):
                    bank = m % 8
                    wm = [rhs_marks[n]]
                    if (n, m) == (0, 0):
                        wm += list(first_w)
                    if (n, m) != (0, 0) and (n * nm + m) >= (8 if nm >= 8 else nm):
                        prev = bank_evac.get(bank)
                        if prev is not None:
                            wm.append(prev)
                    for k in range(nk):
                        S.emit("tensor", lambda e, m=m, k=k, bank=bank, p=p: e.matmul(
                            p_mm[bank][:, :], wT[:, wt_of(k, m):wt_of(k, m) + 128],
                            rhsb[p][:, (k % kt) * TC:((k % kt) + 1) * TC],
                            start=(k == 0), stop=(k == nk - 1)),
                            waits=wm if k == 0 else [], inc=(k == nk - 1))
                    grp_marks[(n, m)] = S.mark("tensor")
                    evac(m, n, bank, grp_marks[(n, m)])
                    bank_evac[bank] = S.mark(evac_eng)

        dyn = {}

        def _conds(e, queue):
            key = f"conds_{queue}"
            if key not in dyn:
                b = e.compute_val(e.partition_id() // 2)
                dyn[key] = [e.compute_val(b == p) for p in range(4)]
            return dyn[key]

        def slot_copy(queue, name, dst_of_p, src_of_p, waits=()):
            for p in range(4):
                S.emit(queue, lambda e, p=p, queue=queue: e.dma_start(
                    out=dst_of_p(p), in_=src_of_p(p), cond=_conds(e, queue)[p]),
                    dma=name, waits=waits if p == 0 else [])

        def dyn_slot(queue, name, dst_fn, src_fn, waits=()):
            def emitfn(e, queue=queue):
                key = f"bv_{queue}"
                if key not in dyn:
                    dyn[key] = e.compute_val(e.partition_id() // 2)
                return e.dma_start(out=dst_fn(dyn[key]), in_=src_fn(dyn[key]))
            S.emit(queue, emitfn, dma=name, waits=waits)

        load_const(eye_b, d_eyeb, "cst")
        load_const(eye_f, d_eyef, "cst")
        cst_done = S.dmark("cst")
        S.emit("vector", lambda e: e.memset(acc1[:, :], 0.0), inc=True)
        S.emit("vector", lambda e: e.memset(cst[:, 0:1], 1e-5), inc=True)
        S.emit("vector", lambda e: e.memset(cst[:, 1:2], 1.0 / DM), inc=True)
        S.emit("vector", lambda e: e.memset(cst[:, 2:3], 1.0), inc=True)

        dA = [sbt(f"dA{p}", [128, CAT]) for p in range(2)]
        bcat = sbt("bcat", [128, CAT], BF16)
        hcat = sbt("hcat", [128, CAT], BF16)
        vcat = sbt("vcat", [128, CAT], BF16)
        Bbc = [sbt(f"Bbc{p}", [128, CAT], BF16) for p in range(2)]
        Cbc = [sbt(f"Cbc{p}", [128, CAT], BF16) for p in range(2)]
        sp_all = sbt("sp_all", [128, NT * TC])
        q_all = sbt("q_all", [128, NT * TC], BF16)
        dtraw = sbt("dtraw", [64, L], BF16)
        escr = sbt("escr", [128, TC])
        dtwT = sbt("dtwT", [64, NT * 128], BF16)

        wld = [sbt(f"wld{p}", [128, 1024]) for p in range(2)]
        wldb = sbt("wldb", [128, 2048], BF16)
        wT = sbt("wT", [128, 12288], BF16)
        rhsb = [sbt(f"rhsb{p}", [128, NT * TC], BF16) for p in range(2)]
        stg = [sbt(f"stg{p}", [128, TC]) for p in range(2)]
        stgb = [sbt(f"stgb{p}", [128, TC], BF16) for p in range(2)]
        zb = sbt("zb", [128, TC], BF16)
        for p in range(2):
            S.emit("vector", lambda e, p=p: e.memset(dA[p][:, :], 0.0), inc=True)
        S.emit("vector", lambda e: e.memset(bcat[:, :], 0.0), inc=True)
        ucc = rhsb
        ystg = stg
        S.emit("vector", lambda e: e.memset(zb[:, :], 0.0), inc=True)
        zf_mark = S.mark("vector")
        zbf = zb.ap().bitcast(F32)[:, 0:256]
        for ti, t4 in enumerate(t_ccbi_q):
            for rg in range(4 * DM // 128):
                S.emit("gpsimd", lambda e, rg=rg, t4=t4: e.dma_start(
                    out=t4[rg * 128:(rg + 1) * 128, :],
                    in_=zbf.unsqueeze(1).broadcast_to([128, 4, 256])),
                    dma="zf", waits=[zf_mark] if (ti == 0 and rg == 0) else [])
        for t4 in t_ccli_q:
            for rg in range(4 * DM // 128):
                S.emit("gpsimd", lambda e, rg=rg, t4=t4: e.dma_start(
                    out=t4[rg * 128:(rg + 1) * 128, :],
                    in_=zb[:, 0:LQ // 4].unsqueeze(1).broadcast_to([128, 4, LQ // 4])),
                    dma="zf")
        for rg in range(6):
            S.emit("gpsimd", lambda e, rg=rg: e.dma_start(
                out=t_ccsi[rg * 128:(rg + 1) * 128, :],
                in_=zbf.unsqueeze(1).broadcast_to([128, 16, 256])),
                dma="zf")
        zf_done = S.dmark("zf")
        gt_alias = q_all
        oq_alias = sp_all.ap().bitcast(BF16)

        NXB = 3
        xt = [Bbc[0].ap().bitcast(F32)[:, 0:DM], Bbc[1].ap().bitcast(F32)[:, 0:DM],
              vcat.ap().bitcast(F32)[:, 0:DM]]
        xc = [Cbc[p].ap().bitcast(F32)[:, 0:DM] for p in range(2)]
        sq = aux
        xnb = hcat.ap()[:, 0:DM]
        mstat = sbt("mstat", [128, 8])
        stmax = sbt("stmax", [128, 32])
        ntok = L // 128
        tile_done = {}
        xn_tile = {}

        def fm_pass(tt, out_dram, name, extra_first=()):
            for cblk in range(8):
                pt = p_tr4[cblk % 4]
                wtp = [S.mark("vector")]
                if cblk < 4:
                    wtp += list(extra_first)
                if tt == 0 and cblk == 0:
                    wtp.append(cst_done)
                if cblk >= 4 or (tt > 0 and cblk < 4):
                    wtp.append(tr_evac_marks[-4])
                S.emit("tensor", lambda e, cblk=cblk, pt=pt: e.transpose(
                    pt[:, :], xnb[:, cblk * 128:(cblk + 1) * 128], eye_b[:, :]),
                    waits=wtp, inc=True)
                wev = [S.mark("tensor")]
                if cblk == 0:
                    wev.append(S.dmark(name))
                S.emit("vector", lambda e, cblk=cblk, pt=pt: e.tensor_copy(
                    auxb[:, cblk * 128:(cblk + 1) * 128], pt[:, :]),
                    waits=wev, inc=True)
                tr_evac_marks.append(S.mark("vector"))
            evm = S.mark("vector")
            out3 = out_dram.rearrange("(c p) l -> p c l", p=128)[:, :, tt * 128:(tt + 1) * 128]
            in3 = auxb.ap().rearrange("p (c w) -> p c w", w=128)
            S.emit("sync", lambda e, out3=out3, in3=in3: e.dma_start(
                out=out3, in_=in3), dma=name, waits=[evm])

        sq2 = [sbt(f"sq{p}", [128, DM]) for p in range(2)]

        def ln_h1(tt):
            p = tt % NXB
            pc = tt % 2
            w = [tile_done[tt - NXB]] if tt >= NXB else []
            S.emit("sync", lambda e, tt=tt, p=p: e.dma_start(
                out=xt[p][:, :], in_=d_x[tt * 128:(tt + 1) * 128, :]), dma=f"xld{p}", waits=w)
            wldx = S.dmark(f"xld{p}")
            S.emit("vector", lambda e, p=p, pc=pc: e.reduce_sum(
                mstat[:, 4 * pc:4 * pc + 1], xt[p][:, :], AX), waits=[wldx], inc=True)
            S.emit("scalar", lambda e, pc=pc: e.activation(
                mstat[:, 4 * pc + 1:4 * pc + 2], mstat[:, 4 * pc:4 * pc + 1],
                AF.Copy, scale=cst[:, 1:2]), waits=[S.mark("vector")], inc=True)
            wsub = [S.mark("scalar")]
            if tt >= 1:
                wsub.append(S.dmark("xntok"))
            S.emit("vector", lambda e, p=p, pc=pc: e.tensor_scalar(
                xc[pc][:, :], xt[p][:, :], mstat[:, 4 * pc + 1:4 * pc + 2], None,
                ALU.subtract), waits=wsub, inc=True)
            S.emit("scalar", lambda e, pc=pc: e.activation(
                sq2[pc][:, :], xc[pc][:, :], AF.Square),
                waits=[S.mark("vector")], inc=True)

        def ln_h2(tt):
            pc = tt % 2
            S.emit("vector", lambda e, pc=pc: e.reduce_sum(
                mstat[:, 4 * pc + 2:4 * pc + 3], sq2[pc][:, :], AX),
                waits=[S.mark("scalar")], inc=True)
            S.emit("scalar", lambda e, pc=pc: e.activation(
                mstat[:, 4 * pc + 3:4 * pc + 4], mstat[:, 4 * pc + 2:4 * pc + 3],
                AF.Sqrt, bias=cst[:, 0:1], scale=cst[:, 1:2]),
                waits=[S.mark("vector")], inc=True)
            S.emit("vector", lambda e, pc=pc: e.reciprocal(
                mstat[:, 4 * pc:4 * pc + 1], mstat[:, 4 * pc + 3:4 * pc + 4]),
                waits=[S.mark("scalar")], inc=True)
            S.gap("vector")
            S.emit("vector", lambda e, pc=pc: e.tensor_scalar(
                xc[pc][:, :], xc[pc][:, :], mstat[:, 4 * pc:4 * pc + 1], None,
                ALU.mult), inc=True)
            xn_mark = S.mark("vector")
            S.emit("vector", lambda e, tt=tt, pc=pc: e.reduce_max(
                stmax[:, tt:tt + 1], xc[pc][:, :], AX, apply_absolute_value=True))
            S.emit("sync", lambda e, tt=tt, pc=pc: e.dma_start(
                out=t_xn_tok[tt * 128:(tt + 1) * 128, :], in_=xc[pc][:, :]),
                dma="xntok", waits=[xn_mark])
            S.emit("vector", lambda e, pc=pc: e.tensor_copy(xnb[:, :], xc[pc][:, :]),
                   waits=[S.mark("tensor")], inc=True)
            fm_pass(tt, t_xn, "xnfm")
            xn_tile[tt] = S.dmark("xnfm")
            tile_done[tt] = S.mark("vector")

        for tt in range(ntok):
            ln_h1(tt)
            if tt >= 1:
                ln_h2(tt - 1)
        ln_h2(ntok - 1)

        S.emit("vector", lambda e: e.reduce_max(acc1[:, 0:1], stmax[:, :], AX),
               waits=[S.mark("vector")], inc=True)
        S.gap("vector")
        S.emit("tensor", lambda e: e.transpose(p_mm[0][0:1, 0:128],
                                               acc1[:, 0:1], eye_f[:, :]),
               waits=[S.mark("vector")], inc=True)
        S.emit("vector", lambda e: e.reduce_max(sc11[0:1, 0:1], p_mm[0][0:1, 0:128], AX),
               waits=[S.mark("tensor")], inc=True)
        maxred_mark = S.mark("vector")
        S.emit("sync", lambda e: e.dma_start(out=t_ccx1[0:1, 0:1], in_=sc11[0:1, 0:1]),
               dma="ccs", waits=[maxred_mark])
        S.emit("gpsimd", lambda e: e.collective_compute(
            "AllReduce", ALU.max, replica_groups=ALL8,
            ins=[t_ccx1.opt()], outs=[t_ccx2.opt()]),
            waits=[S.dmark("ccs")], dma="ccse", dmainc=1)
        S.emit("sync", lambda e: e.dma_start(out=sc11[0:1, 1:2], in_=t_ccx2[0:1, 0:1]),
               dma="ccs", waits=[S.dmark("ccse")])
        S.emit("vector", lambda e: e.tensor_scalar(
            sc11[0:1, 2:3], sc11[0:1, 1:2], 1.0 / QMAX, 1e-12, ALU.mult, ALU.max),
            waits=[S.dmark("ccs")], inc=True)
        S.gap("vector")
        S.emit("vector", lambda e: e.reciprocal(sc11[0:1, 3:4], sc11[0:1, 2:3]), inc=True)
        S.gap("vector")
        bcast_scalar(qsc[:, 1:2], sc11[0:1, 2:3], "qb", waits=[S.mark("vector")])
        bcast_scalar(qsc[:, 0:1], sc11[0:1, 3:4], "qb")

        def emit_quant(mu_done):
            for tt in range(ntok):
                p = tt % NXB
                pc = tt % 2
                w = [S.dmark("qb"), S.dmark("xntok")] if tt == 0 else []
                if tt >= NXB:
                    w.append(tile_done[ntok + tt - NXB])
                S.emit("sync", lambda e, tt=tt, p=p: e.dma_start(
                    out=xt[p][:, :], in_=t_xn_tok[tt * 128:(tt + 1) * 128, :]),
                    dma=f"xld{p}", waits=w)
                S.emit("vector", lambda e, p=p, pc=pc: e.tensor_scalar(
                    xc[pc][:, :], xt[p][:, :], qsc[:, 0:1], RND, ALU.mult, ALU.add),
                    waits=[S.dmark(f"xld{p}")], inc=True)
                S.emit("vector", lambda e, pc=pc: e.tensor_scalar(
                    xc[pc][:, :], xc[pc][:, :], RND, QMAX, ALU.subtract, ALU.min))
                S.emit("vector", lambda e, pc=pc: e.tensor_scalar(
                    xc[pc][:, :], xc[pc][:, :], -QMAX - 1.0, qsc[:, 1:2], ALU.max, ALU.mult))
                S.emit("vector", lambda e, pc=pc: e.tensor_copy(xnb[:, :], xc[pc][:, :]),
                       waits=[S.mark("tensor")], inc=True)
                fm_pass(tt, t_xq, "xqfm", extra_first=[mu_done] if tt == 0 else ())
                tile_done[ntok + tt] = S.mark("vector")

        def scan_phase(i, tag, dtraw_mark, dtwT_mark, bc_done):
            k_idx = [0]
            q_marks = {}
            scan_marks = {}
            v_marks = {}
            pe_marks = {}
            btt_marks = {}
            evac_marks = {}
            dma_y_name = f"ydma{tag}"

            rhs_free = S.mark("tensor")
            for tc in range(NTC):
                ptc = tc % 2
                w = [S.dmark(f"uc{tag}")]
                if tc < 2:
                    w.append(rhs_free)
                if tc >= 2:
                    w.append(evac_marks[(tc - 2, NT - 1)])
                for dt_i in range(NT):
                    S.emit("sync", lambda e, tc=tc, dt_i=dt_i, ptc=ptc: e.dma_start(
                        out=ucc[ptc][:, dt_i * TC:(dt_i + 1) * TC],
                        in_=t_uc[dt_i * 128:(dt_i + 1) * 128, tc * TC:(tc + 1) * TC]),
                        dma=f"ucc{ptc}", waits=w if dt_i == 0 else [])
                ucc_mark = S.dmark(f"ucc{ptc}")
                prev_q_mark = q_marks.get(tc - 1)
                for dt_i in range(NT):
                    bank = dt_i
                    wm = [dtraw_mark, dtwT_mark]
                    if tc > 0:
                        wm.append(evac_marks[(tc - 1, dt_i)])
                    S.emit("tensor", lambda e, tc=tc, dt_i=dt_i, bank=bank: e.matmul(
                        p_mm[bank][:, :], dtwT[0:64, dt_i * 128:(dt_i + 1) * 128],
                        dtraw[0:64, tc * TC:(tc + 1) * TC], start=True, stop=True),
                        waits=wm, inc=True)
                    wsc = [S.mark("tensor")]
                    if dt_i == 0 and prev_q_mark is not None:
                        wsc.append(prev_q_mark)
                    S.emit("scalar", lambda e, dt_i=dt_i, bank=bank: e.activation(
                        escr[:, :], p_mm[bank][:, :], AF.Exp, bias=dtb[:, dt_i:dt_i + 1]),
                        waits=wsc, inc=True)
                    S.emit("scalar", lambda e, dt_i=dt_i: e.activation(
                        sp_all[:, dt_i * TC:(dt_i + 1) * TC], escr[:, :], AF.Ln,
                        bias=cst[:, 2:3]), inc=True)
                    S.emit("vector", lambda e, dt_i=dt_i, ptc=ptc: e.tensor_tensor(
                        q_all[:, dt_i * TC:(dt_i + 1) * TC],
                        sp_all[:, dt_i * TC:(dt_i + 1) * TC],
                        ucc[ptc][:, dt_i * TC:(dt_i + 1) * TC], ALU.mult),
                        waits=[S.mark("scalar"), ucc_mark], inc=True)
                q_mark = S.mark("vector")
                q_marks[tc] = q_mark
                sp_mark = S.mark("scalar")
                if tc == 0 and i == 0 and "sp" in DBG2:
                    S.emit("sync", lambda e: e.dma_start(out=DBG2["sp"], in_=sp_all[:, :]),
                           dma="dbgsp", waits=[sp_mark, q_mark])
                    S.emit("sync", lambda e: e.dma_start(out=DBG2["q"], in_=q_all[:, :]),
                           dma="dbgsp", waits=[sp_mark, q_mark])
                    S.emit("sync", lambda e: e.dma_start(out=DBG2["an"], in_=Aneg[:, :]),
                           dma="dbgsp", waits=[sp_mark, q_mark])

                import os as _os2
                NGl = int(_os2.environ.get("DBGNG", str(NG)))
                for g in range(NGl):
                    pg = (tc * NG + g) % 2
                    gi = tc * NG + g
                    wB = [q_mark, bc_done]
                    if gi >= 2:
                        wB.append(btt_marks.get(gi - 2, S.mark("vector")))
                    for seg in range(GS):
                        srow = g * GS + seg
                        S.emit("sync", lambda e, srow=srow, seg=seg, pg=pg, tc=tc: e.dma_start(
                            out=Bbc[pg][:, seg * SEG + 2:seg * SEG + 2 + TC],
                            in_=t_Bc[srow:srow + 1, tc * TC:(tc + 1) * TC]
                            .unsqueeze(1).broadcast_to([1, 128, TC])),
                            dma=f"bb{pg}", waits=wB if seg == 0 else [])
                    wC = [bc_done]
                    if gi >= 2:
                        wC.append(v_marks.get((gi - 2, NT - 1), S.mark("vector")))
                    for seg in range(GS):
                        srow = g * GS + seg
                        S.emit("gpsimd", lambda e, srow=srow, seg=seg, pg=pg, tc=tc: e.dma_start(
                            out=Cbc[pg][:, seg * SEG + 2:seg * SEG + 2 + TC],
                            in_=t_Cc[srow:srow + 1, tc * TC:(tc + 1) * TC]
                            .unsqueeze(1).broadcast_to([1, 128, TC])),
                            dma=f"cb{pg}", waits=wC if seg == 0 else [])
                    bbc_mark = S.dmark(f"bb{pg}")
                    cbc_mark = S.dmark(f"cb{pg}")
                    if tc == 0 and g == 0 and i == 0 and "sp" in DBG2:
                        S.emit("sync", lambda e: e.dma_start(out=DBG2["bb"], in_=Bbc[0][:, :]),
                               dma="dbgsp", waits=[bbc_mark, cbc_mark])
                        S.emit("sync", lambda e: e.dma_start(out=DBG2["cb"], in_=Cbc[0][:, :]),
                               dma="dbgsp", waits=[bbc_mark, cbc_mark])

                    for dt_i in range(NT):
                        k = k_idx[0]
                        pk = k % 2
                        import os as _os
                        _ser = bool(_os.environ.get("DBGSER"))
                        wA = [sp_mark]
                        if _ser and k >= 1:
                            wA.append(scan_marks[k - 1])
                        if k >= 2:
                            wA.append(scan_marks[k - 2])
                        for seg in range(GS):
                            s_idx = g * GS + seg
                            S.emit("scalar", lambda e, dt_i=dt_i, seg=seg, s_idx=s_idx, pk=pk:
                                   e.activation(
                                       dA[pk][:, seg * SEG + 2:seg * SEG + 2 + TC],
                                       sp_all[:, dt_i * TC:(dt_i + 1) * TC], AF.Exp,
                                       scale=Aneg[:, dt_i * 64 + s_idx:dt_i * 64 + s_idx + 1]),
                                   waits=wA if seg == 0 else [], inc=(seg == GS - 1))
                        dA_mark = S.mark("scalar")
                        wD = [bbc_mark, dA_mark]
                        S.emit("vector", lambda e, dt_i=dt_i, g=g: e.tensor_copy(
                            seg3(bcat, 1, 2),
                            carry[:, dt_i * 64 + g * GS:dt_i * 64 + (g + 1) * GS].unsqueeze(2)),
                            waits=wD)
                        S.emit("vector", lambda e, dt_i=dt_i, pg=pg: e.tensor_tensor(
                            seg3(bcat, 2, SEG),
                            q_all[:, dt_i * TC:(dt_i + 1) * TC].unsqueeze(1).broadcast_to([128, GS, TC]),
                            seg3(Bbc[pg], 2, SEG), ALU.mult), inc=True)
                        btt_marks[gi] = S.mark("vector")
                        S.emit("vector", lambda e, pk=pk: e.tensor_tensor_scan(
                            hcat[:, :], dA[pk][:, :], bcat[:, :], 0.0, ALU.mult, ALU.add),
                            inc=True)
                        scan_marks[k] = S.mark("vector")
                        wV = [cbc_mark]
                        if k >= 1:
                            wV.append(pe_marks[k - 1])
                        S.emit("vector", lambda e, pg=pg: e.tensor_tensor(
                            seg3(vcat, 2, SEG), seg3(hcat, 2, SEG),
                            seg3(Cbc[pg], 2, SEG), ALU.mult), waits=wV)
                        S.emit("vector", lambda e, dt_i=dt_i, g=g: e.tensor_copy(
                            carry[:, dt_i * 64 + g * GS:dt_i * 64 + (g + 1) * GS].unsqueeze(2),
                            seg3(hcat, SEG - 1, SEG)), inc=True)
                        v_marks[(gi, dt_i)] = S.mark("vector")
                        import os
                        _dbgg = int(os.environ.get("DBGG", "0"))
                        if tc == 0 and g == _dbgg and dt_i == 0 and i == 0 and "hc" in DBG2:
                            S.emit("sync", lambda e: e.dma_start(out=DBG2["hc"], in_=hcat[:, :]),
                                   dma="dbgh", waits=[v_marks[(gi, dt_i)]])
                            S.emit("sync", lambda e: e.dma_start(out=DBG2["vc"], in_=vcat[:, :]),
                                   dma="dbgh", waits=[v_marks[(gi, dt_i)]])
                            S.emit("sync", lambda e: e.dma_start(out=DBG2["bc2"], in_=bcat[:, :]),
                                   dma="dbgh", waits=[v_marks[(gi, dt_i)]])
                            S.emit("sync", lambda e, pk=pk: e.dma_start(out=DBG2["da"], in_=dA[pk][:, :]),
                                   dma="dbgh", waits=[v_marks[(gi, dt_i)]])
                            S.emit("vector", lambda e: e.engine_nop(),
                                   waits=[S.dmark("dbgh")], inc=True)
                        for seg in range(GS):
                            S.emit("tensor", lambda e, seg=seg, dt_i=dt_i, g=g, NGl=NGl:
                                   e.matmul(
                                       p_mm[dt_i][:, :], eye_b[:, :],
                                       vcat[:, seg * SEG + 2:seg * SEG + 2 + TC],
                                       start=(g == 0 and seg == 0),
                                       stop=(g == NGl - 1 and seg == GS - 1)),
                                   waits=[v_marks[(gi, dt_i)]] if seg == 0 else [],
                                   inc=(seg == GS - 1))
                        pe_marks[k] = S.mark("tensor")
                        if tc == 0 and g == int(_os2.environ.get("DBGPP", "0")) and dt_i == 0 and i == 0 and "pp" in DBG2:
                            S.emit("vector", lambda e: e.tensor_copy(
                                escr[:, :], p_mm[0][:, :]), waits=[pe_marks[k]], inc=True)
                            S.emit("sync", lambda e: e.dma_start(out=DBG2["pp"], in_=escr[:, :]),
                                   dma="dbgp", waits=[S.mark("vector")])
                            S.emit("tensor", lambda e: e.nop(), waits=[S.dmark("dbgp")])
                            S.emit("vector", lambda e: e.engine_nop(), waits=[S.dmark("dbgp")])
                        k_idx[0] += 1

                for dt_i in range(NT):
                    p = dt_i % 2
                    wE = [pe_marks[k_idx[0] - NT + dt_i]]
                    if tc * NT + dt_i >= 2:
                        wE.append(S.dmark(dma_y_name))
                    S.emit("vector", lambda e, dt_i=dt_i, p=p, ptc=ptc: e.scalar_tensor_tensor(
                        ystg[p][:, :], ucc[ptc][:, dt_i * TC:(dt_i + 1) * TC],
                        dskip[:, dt_i:dt_i + 1], p_mm[dt_i][:, :], ALU.mult, ALU.add),
                        waits=wE, inc=True)
                    evac_marks[(tc, dt_i)] = S.mark("vector")
                    S.emit("sync", lambda e, dt_i=dt_i, tc=tc, p=p: e.dma_start(
                        out=t_y[dt_i * 128:(dt_i + 1) * 128, tc * TC:(tc + 1) * TC],
                        in_=ystg[p][:, :]), dma=dma_y_name, waits=[evac_marks[(tc, dt_i)]])
                    if tc == 0 and dt_i == 0 and i == 0 and "ys" in DBG2:
                        S.emit("sync", lambda e, p=p: e.dma_start(
                            out=DBG2["ys"], in_=ystg[p][:, :]),
                            dma="dbgy", waits=[evac_marks[(tc, dt_i)]])
                        S.emit("sync", lambda e, ptc=ptc: e.dma_start(
                            out=DBG2["uc0"], in_=ucc[ptc][:, 0:TC]),
                            dma="dbgy", waits=[evac_marks[(tc, dt_i)]])
            return S.dmark(dma_y_name)

        def mamba_layer(i, rhs_in, rhs_w_of=None, mid_emit=None):
            tag = f"L{i}"
            for dst, src in [(convw, d_convw[i]), (convb, d_convb[i]), (dtb, d_dtb[i]),
                             (dskip, d_dskip[i])]:
                load_const(dst, src, f"cv{tag}",
                           waits=[S.mark("vector"), S.mark("scalar")] if i > 0 else ())
            S.emit("sync", lambda e: e.dma_start(out=escr[:, 0:NT * 64], in_=d_alog[i]),
                   dma=f"cv{tag}")
            S.emit("scalar", lambda e: e.activation(Aneg[:, :], escr[:, 0:NT * 64], AF.Exp),
                   waits=[S.dmark(f"cv{tag}")], inc=True)
            S.emit("vector", lambda e: e.tensor_scalar(
                Aneg[:, :], Aneg[:, :], -1.0, None, ALU.mult),
                waits=[S.mark("scalar")], inc=True)
            S.emit("vector", lambda e: e.memset(carry[:, :], 0.0), inc=True)
            aneg_mark = S.mark("scalar")

            wu_of = prep_lhsT(d_inw[i][0:DH, :], DH, f"{tag}u", eng="scalar")
            wt_ready = S.mark("scalar")
            ur_marks = {}

            def evac_u(m, n, bank, pe_mark):
                p = (n * 8 + m) % 2
                wv = [pe_mark]
                if n * 8 + m >= 2:
                    wv.append(S.dmark(f"ur{tag}"))
                S.emit("scalar", lambda e, bank=bank, p=p: e.activation(
                    stg[p][:, :], p_mm[bank][:, :], AF.Copy), waits=wv, inc=True)
                S.emit("sync", lambda e, m=m, n=n, p=p: e.dma_start(
                    out=t_uraw[m * 128:(m + 1) * 128, n * TC:(n + 1) * TC],
                    in_=stg[p][:, :]), dma=f"ur{tag}", waits=[S.mark("scalar")])
                ur_marks[(m, n)] = S.dmark(f"ur{tag}")
            mm_phase(rhs_in, 8, 8, wu_of, evac_u, f"{tag}u", extra_w=[wt_ready],
                     evac_eng="scalar", rhs_w=rhs_w_of,
                     first_w=[maxred_mark] if i == 0 else [S.mark("vector")])
            mu_done = S.mark("scalar")

            CC = 512
            ub = [sp_all.ap()[:, j * 520:j * 520 + CC + 4] for j in range(4)]
            ucb = [oq_alias[:, 4160 + p * CC:4160 + (p + 1) * CC] for p in range(2)]
            cacc = [escr, aux[:, 0:TC]]
            for j in range(4):
                S.emit("vector", lambda e, j=j: e.memset(ub[j][:, 0:3], 0.0), inc=True)
            silu_marks = {}
            uc_marks = {}
            nch = L // CC
            for ci in range(NT * nch):
                dt_i, ch = divmod(ci, nch)
                p = ci % 4
                pc = ci % 2
                wl = [ur_marks[(dt_i, ch)]]
                if ch > 0:
                    wl.append(ur_marks[(dt_i, ch - 1)])
                if ci >= 4:
                    wl.append(silu_marks[ci - 4])
                lo = ch * CC - 3
                if ch == 0:
                    S.emit("sync", lambda e, dt_i=dt_i, ch=ch, p=p: e.dma_start(
                        out=ub[p][:, 3:CC + 3],
                        in_=t_uraw[dt_i * 128:(dt_i + 1) * 128, ch * CC:(ch + 1) * CC]),
                        dma=f"ul{p}", waits=wl)
                    if ci > 0:
                        S.emit("vector", lambda e, p=p: e.memset(ub[p][:, 0:3], 0.0),
                               waits=[silu_marks[ci - 4]] if ci >= 4 else [])
                else:
                    S.emit("sync", lambda e, dt_i=dt_i, lo=lo, p=p: e.dma_start(
                        out=ub[p][:, 0:CC + 3],
                        in_=t_uraw[dt_i * 128:(dt_i + 1) * 128, lo:lo + CC + 3]),
                        dma=f"ul{p}", waits=wl)
                wts = [S.dmark(f"ul{p}")]
                if ci < 2:
                    wts.append(aneg_mark)
                else:
                    wts.append(silu_marks[ci - 2])
                S.emit("vector", lambda e, dt_i=dt_i, p=p, pc=pc: e.tensor_scalar(
                    cacc[pc][:, :], ub[p][:, 0:CC], convw[:, dt_i * 4:dt_i * 4 + 1], None,
                    ALU.mult), waits=wts, inc=True)
                for kk in range(1, 4):
                    S.emit("vector", lambda e, dt_i=dt_i, p=p, kk=kk, pc=pc: e.scalar_tensor_tensor(
                        cacc[pc][:, :], ub[p][:, kk:CC + kk],
                        convw[:, dt_i * 4 + kk:dt_i * 4 + kk + 1],
                        cacc[pc][:, :], ALU.mult, ALU.add), inc=(kk == 3))
                wsl = [S.mark("vector")]
                if ci >= 2:
                    wsl.append(uc_marks[ci - 2])
                S.emit("scalar", lambda e, dt_i=dt_i, pc=pc: e.activation(
                    ucb[pc][:, :], cacc[pc][:, :], AF.Silu, bias=convb[:, dt_i:dt_i + 1]),
                    waits=wsl, inc=True)
                silu_marks[ci] = S.mark("scalar")
                S.emit("gpsimd", lambda e, dt_i=dt_i, ch=ch, pc=pc: e.dma_start(
                    out=t_uc[dt_i * 128:(dt_i + 1) * 128, ch * CC:(ch + 1) * CC],
                    in_=ucb[pc][:, :]), dma=f"uc{tag}", waits=[silu_marks[ci]])
                uc_marks[ci] = S.dmark(f"uc{tag}")

            if mid_emit is not None:
                mid_emit(mu_done)

            wz_of = prep_lhsT(d_inw[i][DH:2 * DH, :], DH, f"{tag}z", eng="scalar")
            wt_ready = S.mark("scalar")

            def evac_z(m, n, bank, pe_mark):
                p = (n * 8 + m) % 2
                wv = [pe_mark]
                if n * 8 + m >= 2:
                    wv.append(S.dmark(f"zr{tag}"))
                S.emit("scalar", lambda e, bank=bank, p=p: e.activation(
                    stgb[p][:, :], p_mm[bank][:, :], AF.Copy), waits=wv, inc=True)
                S.emit("sync", lambda e, m=m, n=n, p=p: e.dma_start(
                    out=t_z[m * 128:(m + 1) * 128, n * TC:(n + 1) * TC],
                    in_=stgb[p][:, :]), dma=f"zr{tag}", waits=[S.mark("scalar")])
            mm_phase(rhs_in, 8, 8, wz_of, evac_z, f"{tag}z", extra_w=[wt_ready],
                     evac_eng="scalar", rhs_w=rhs_w_of)
            mz_done = S.mark("scalar")

            nmx = 2
            for m in range(nmx):
                rows = 128 if m == 0 else 64
                wv = [S.mark("tensor")] if m > 0 else [S.mark("vector"), S.mark("scalar")]
                S.emit("sync", lambda e, m=m, rows=rows: e.dma_start(
                    out=wld[0][0:rows, 0:DM], in_=d_xpw[i][m * 128:m * 128 + rows, :]),
                    dma=f"wx{tag}", waits=wv)
                S.emit("vector", lambda e, rows=rows: e.tensor_copy(
                    wldb[0:rows, 0:DM], wld[0][0:rows, 0:DM]),
                    waits=[S.dmark(f"wx{tag}")], inc=True)
                cast_mark = S.mark("vector")
                for k in range(8):
                    pt = p_tr[k % 2]
                    wtp = [cast_mark]
                    if m * 8 + k < 2:
                        wtp.append(mz_done)
                    if m * 8 + k >= 2:
                        wtp.append(tr_evac_marks[-2])
                    S.emit("tensor", lambda e, k=k, rows=rows, pt=pt: e.transpose(
                        pt[0:128, 0:rows], wldb[0:rows, k * 128:(k + 1) * 128],
                        eye_b[0:rows, 0:rows]), waits=wtp, inc=True)
                    col = (k * nmx + m) * 128
                    S.emit("vector", lambda e, col=col, rows=rows, pt=pt: e.tensor_copy(
                        wT[:, col:col + rows], pt[0:128, 0:rows]),
                        waits=[S.mark("tensor")], inc=True)
                    tr_evac_marks.append(S.mark("vector"))
            wt_ready = S.mark("vector")

            def evac_x(m, n, bank, pe_mark):
                rows = 128 if m == 0 else 64
                p = (n * 2 + m) % 2
                wv = [pe_mark]
                if n * 2 + m >= 2:
                    wv.append(S.dmark(f"xd{tag}"))
                S.emit("vector", lambda e, rows=rows, bank=bank, p=p: e.tensor_copy(
                    stg[p][0:rows, :], p_mm[bank][0:rows, :]), waits=wv, inc=True)
                S.emit("sync", lambda e, m=m, n=n, rows=rows, p=p: e.dma_start(
                    out=t_smp[m * 128:m * 128 + rows, n * TC:(n + 1) * TC],
                    in_=stg[p][0:rows, :]), dma=f"xd{tag}", waits=[S.mark("vector")])
            mm_phase(t_uc, 8, 2, lambda k, m: (k * nmx + m) * 128, evac_x, f"{tag}x",
                     extra_w=[wt_ready, S.dmark(f"uc{tag}")], first_w=[mz_done])
            slot_copy("gpsimd", "sci",
                      lambda p: t_ccsi[p * 192:(p + 1) * 192, :],
                      lambda p: t_smp[:, :], waits=[S.dmark(f"xd{tag}"), zf_done])
            S.emit("gpsimd", lambda e: e.collective_compute(
                "AllReduce", ALU.add, replica_groups=ALL8,
                ins=[t_ccsi.opt()], outs=[t_ccso.opt()]),
                waits=[S.dmark("sci")], dma="ccc", dmainc=1)
            slot_copy("scalar", "sco",
                      lambda p: t_smf[:, :],
                      lambda p: t_ccso[p * 192:(p + 1) * 192, :],
                      waits=[S.dmark("ccc")])
            ccc_mark = S.dmark("sco")
            for gi_, (dst, r0) in enumerate([(None, 0), (t_Bc, 64), (t_Cc, 128)]):
                for ch in range(4):
                    p = ch % 2
                    wv = [ccc_mark]
                    if gi_ * 4 + ch >= 2:
                        wv.append(S.mark("vector"))
                    S.emit("sync", lambda e, r0=r0, ch=ch, p=p: e.dma_start(
                        out=wld[p][0:64, 0:1024],
                        in_=t_smf[r0:r0 + 64, ch * 1024:(ch + 1) * 1024]),
                        dma=f"xu{tag}", waits=wv)
                    if dst is None:
                        S.emit("vector", lambda e, ch=ch, p=p: e.tensor_copy(
                            dtraw[:, ch * 1024:(ch + 1) * 1024], wld[p][0:64, 0:1024]),
                            waits=[S.dmark(f"xu{tag}")], inc=True)
                    else:
                        wbk = [S.dmark(f"xu{tag}")]
                        if gi_ * 4 + ch < 2:
                            wbk += [S.dmark("xqfm"), S.dmark("op")]
                        if gi_ * 4 + ch >= 2:
                            wbk.append(S.dmark(f"bc{tag}"))
                        S.emit("vector", lambda e, ch=ch, p=p: e.tensor_copy(
                            auxb[0:64, 0:1024], wld[p][0:64, 0:1024]), waits=wbk, inc=True)
                        S.emit("sync", lambda e, dst=dst, ch=ch: e.dma_start(
                            out=dst[0:64, ch * 1024:(ch + 1) * 1024], in_=auxb[0:64, 0:1024]),
                            dma=f"bc{tag}", waits=[S.mark("vector")])
            dtraw_mark = S.mark("vector")
            bc_done = S.dmark(f"bc{tag}")

            S.emit("sync", lambda e: e.dma_start(out=wld[1][:, 0:NT * 64], in_=d_dtw[i]),
                   dma=f"dw{tag}", waits=[S.mark("vector"), S.mark("scalar")])
            S.emit("vector", lambda e: e.tensor_copy(wldb[:, 0:NT * 64], wld[1][:, 0:NT * 64]),
                   waits=[S.dmark(f"dw{tag}")], inc=True)
            cast_mark = S.mark("vector")
            for k in range(NT):
                pt = p_tr[k % 2]
                wtp = [cast_mark]
                if k >= 2:
                    wtp.append(tr_evac_marks[-2])
                S.emit("tensor", lambda e, k=k, pt=pt: e.transpose(
                    pt[0:64, 0:128], wldb[:, k * 64:(k + 1) * 64], eye_b[:, :]),
                    waits=wtp, inc=True)
                S.emit("vector", lambda e, k=k, pt=pt: e.tensor_copy(
                    dtwT[0:64, k * 128:(k + 1) * 128], pt[0:64, 0:128]),
                    waits=[S.mark("tensor")], inc=True)
                tr_evac_marks.append(S.mark("vector"))
            dtwT_mark = S.mark("vector")

            wo_of = prep_lhsT(d_outw[i], DM, f"{tag}o", eng="scalar")
            wt_ready_o = S.mark("scalar")

            y_done = scan_phase(i, tag, dtraw_mark, dtwT_mark, bc_done)

            gt = gt_alias
            zl = stgb
            zs = escr
            yl = [sp_all.ap()[:, p * TC:(p + 1) * TC] for p in range(2)]
            g_marks = {}
            mm_done = {}
            bank_evac = {}
            for n in range(NTC):
                for k in range(NT):
                    p = k % 2
                    wv = [S.dmark(f"zr{tag}"), y_done]
                    if (n, k) != (0, 0):
                        if n * NT + k >= 2:
                            wv.append(g_marks[n * NT + k - 2])
                    S.emit("sync", lambda e, n=n, k=k, p=p: e.dma_start(
                        out=zl[p][:, :], in_=t_z[k * 128:(k + 1) * 128, n * TC:(n + 1) * TC]),
                        dma=f"zl{p}", waits=wv)
                    S.emit("sync", lambda e, n=n, k=k, p=p: e.dma_start(
                        out=yl[p][:, :], in_=t_y[k * 128:(k + 1) * 128, n * TC:(n + 1) * TC]),
                        dma=f"yl{p}")
                    S.emit("scalar", lambda e, p=p: e.activation(
                        zs[:, :], zl[p][:, :], AF.Silu),
                        waits=[S.dmark(f"zl{p}"), S.mark("vector")], inc=True)
                    wg_ = [S.dmark(f"yl{p}"), S.mark("scalar")]
                    if n > 0:
                        wg_.append(mm_done.get((n - 1, NT - 1), (None, 0)))
                    S.emit("vector", lambda e, k=k, p=p: e.tensor_tensor(
                        gt[:, k * TC:(k + 1) * TC], yl[p][:, :], zs[:, :], ALU.mult),
                        waits=[x for x in wg_ if x[0] is not None], inc=True)
                    g_marks[n * NT + k] = S.mark("vector")
                gm = S.mark("vector")
                for m in range(8):
                    bank = m
                    wm = [gm]
                    prev = bank_evac.get(bank)
                    if prev is not None:
                        wm.append(prev)
                    for k in range(NT):
                        S.emit("tensor", lambda e, m=m, k=k, bank=bank: e.matmul(
                            p_mm[bank][:, :], wT[:, wo_of(k, m):wo_of(k, m) + 128],
                            gt[:, k * TC:(k + 1) * TC],
                            start=(k == 0), stop=(k == NT - 1)),
                            waits=(wm + [wt_ready_o]) if k == 0 else [], inc=(k == NT - 1))
                    mm_done[(n, m)] = S.mark("tensor")
                    p = (n * 8 + m) % 2
                    wv = [mm_done[(n, m)]]
                    if n * 8 + m >= 2:
                        wv.append(S.dmark(f"op{tag}"))
                    if i == 0:
                        S.emit("vector", lambda e, bank=bank, p=p: e.tensor_copy(
                            auxb[:, p * TC:(p + 1) * TC], p_mm[bank][:, :]),
                            waits=wv, inc=True)
                        bank_evac[bank] = S.mark("vector")
                        S.emit("sync", lambda e, m=m, n=n, p=p: e.dma_start(
                            out=t_bg1[m * 128:(m + 1) * 128, n * TC:(n + 1) * TC],
                            in_=auxb[:, p * TC:(p + 1) * TC]),
                            dma=f"op{tag}", waits=[bank_evac[bank]])
                    else:
                        S.emit("vector", lambda e, bank=bank, p=p: e.tensor_copy(
                            stg[p][:, :], p_mm[bank][:, :]), waits=wv, inc=True)
                        bank_evac[bank] = S.mark("vector")
                        S.emit("sync", lambda e, m=m, n=n, p=p: e.dma_start(
                            out=t_m2p[m * 128:(m + 1) * 128, n * TC:(n + 1) * TC],
                            in_=stg[p][:, :]), dma=f"op{tag}", waits=[bank_evac[bank]])
                if i == 0 and n % 2 == 1:
                    q = n // 2
                    opq = S.dmark(f"op{tag}")
                    dyn_slot("scalar", f"bi{q}",
                             lambda bv, q=q: t_ccli_q[q][bass.ts(bv, DM), :],
                             lambda bv, q=q: t_bg1[:, q * LQ:(q + 1) * LQ],
                             waits=[opq, zf_done])
                    S.emit("gpsimd", lambda e, q=q: e.collective_compute(
                        "AllReduce", ALU.add, replica_groups=ALL8,
                        ins=[t_ccli_q[q].opt()], outs=[t_cclo_q[q].opt()]),
                        waits=[S.dmark(f"bi{q}")], dma=f"ca{q}", dmainc=1)
            if i == 0:
                bco_m = []
                for q in range(4):
                    dyn_slot("scalar" if q < 2 else "gpsimd", f"bo{q}",
                             lambda bv, q=q: t_l2in[:, q * LQ:(q + 1) * LQ],
                             lambda bv, q=q: t_cclo_q[q][bass.ts(bv, DM), :],
                             waits=[S.dmark(f"ca{q}")])
                    bco_m.append(S.dmark(f"bo{q}"))
                return bco_m
            return None

        def qdq3(dstb, src, inv_col, s_col, lo, hi, out_bf=False):
            S.emit("vector", lambda e: e.tensor_scalar(
                aux[:, 0:TC], src, inv_col, RND, ALU.mult, ALU.add), inc=True)
            S.emit("vector", lambda e: e.tensor_scalar(
                aux[:, 0:TC], aux[:, 0:TC], RND, hi, ALU.subtract, ALU.min))
            S.emit("vector", lambda e, dstb=dstb: e.tensor_scalar(
                dstb, aux[:, 0:TC], lo, s_col, ALU.max, ALU.mult), inc=True)

        def dequant_prep(w_dram, nrow_tiles, ncols, name, out_base=0):
            CW = 512
            ncc = ncols // CW
            nbc = CW // 32
            for m in range(nrow_tiles):
                for cch in range(ncc):
                    pw = (m * ncc + cch) % 2
                    wv = [S.mark("vector")] if m * ncc + cch == 0 else [S.mark("vector")]
                    S.emit("sync", lambda e, m=m, cch=cch, pw=pw: e.dma_start(
                        out=wld[pw][:, 0:CW],
                        in_=w_dram[m * 128:(m + 1) * 128, cch * CW:(cch + 1) * CW]),
                        dma="wp", waits=wv)
                    w3 = wld[pw][:, 0:CW].rearrange("p (nb b) -> p nb b", b=32)
                    S.emit("vector", lambda e, w3=w3: e.tensor_reduce(
                        aux[:, 0:nbc], w3, AX, ALU.max, apply_absolute_value=True),
                        waits=[S.dmark("wp")], inc=True)
                    S.gap("vector")
                    S.emit("vector", lambda e: e.tensor_scalar(
                        aux[:, 0:nbc], aux[:, 0:nbc], 1.0 / 7.0, 1e-12, ALU.mult, ALU.max))
                    S.gap("vector")
                    S.emit("vector", lambda e: e.reciprocal(
                        aux[:, nbc:2 * nbc], aux[:, 0:nbc]), inc=True)
                    S.gap("vector")
                    rq3 = aux[:, 512:512 + CW].rearrange("p (nb b) -> p nb b", b=32)
                    S.emit("vector", lambda e, w3=w3, rq3=rq3: e.tensor_tensor(
                        rq3, w3, aux[:, nbc:2 * nbc].unsqueeze(2).broadcast_to([128, nbc, 32]),
                        ALU.mult), waits=[S.mark("tensor")] if cch == 0 else [], inc=True)
                    S.emit("vector", lambda e, pw=pw: e.tensor_scalar(
                        wld[pw][:, 0:CW], aux[:, 512:512 + CW],
                        RND, RND, ALU.add, ALU.subtract), inc=True)
                    S.emit("vector", lambda e, pw=pw: e.tensor_scalar(
                        wld[pw][:, 0:CW], wld[pw][:, 0:CW], 7.0, -8.0, ALU.min, ALU.max))
                    wq3 = wldb[:, cch * CW:(cch + 1) * CW].rearrange("p (nb b) -> p nb b", b=32)
                    S.emit("vector", lambda e, pw=pw, wq3=wq3: e.tensor_tensor(
                        wq3, wld[pw][:, 0:CW].rearrange("p (nb b) -> p nb b", b=32),
                        aux[:, 0:nbc].unsqueeze(2).broadcast_to([128, nbc, 32]),
                        ALU.mult), inc=True)
                cast_mark = S.mark("vector")
                kt = ncols // 128
                for k in range(kt):
                    pt = p_tr[k % 2]
                    wtp = [cast_mark]
                    if m * kt + k >= 2:
                        wtp.append(tr_evac_marks[-2])
                    S.emit("tensor", lambda e, k=k, pt=pt: e.transpose(
                        pt[:, :], wldb[:, k * 128:(k + 1) * 128], eye_b[:, :]),
                        waits=wtp, inc=True)
                    col = out_base + (k * nrow_tiles + m) * 128
                    S.emit("vector", lambda e, col=col, pt=pt: e.tensor_copy(
                        wT[:, col:col + 128], pt[:, :]), waits=[S.mark("tensor")], inc=True)
                    tr_evac_marks.append(S.mark("vector"))
                prep_marks.append(S.mark("vector"))
            return lambda k, m: out_base + (k * nrow_tiles + m) * 128

        def collective_scalar_max(src_colidx, dst_colidx, ccname):
            S.emit("tensor", lambda e, src_colidx=src_colidx: e.transpose(
                p_mm[0][0:1, 0:128], acc1[:, src_colidx:src_colidx + 1], eye_f[:, :]),
                waits=[S.mark("vector")], inc=True)
            S.emit("vector", lambda e, dst_colidx=dst_colidx: e.reduce_max(
                sc11[0:1, dst_colidx:dst_colidx + 1], p_mm[0][0:1, 0:128], AX),
                waits=[S.mark("tensor")], inc=True)
            S.emit("sync", lambda e, dst_colidx=dst_colidx: e.dma_start(
                out=t_ccx1[0:1, 0:1], in_=sc11[0:1, dst_colidx:dst_colidx + 1]),
                dma="ccs", waits=[S.mark("vector")])
            S.emit("gpsimd", lambda e: e.collective_compute(
                "AllReduce", ALU.max, replica_groups=ALL8,
                ins=[t_ccx1.opt()], outs=[t_ccx2.opt()]),
                waits=[S.dmark("ccs")], dma="ccse", dmainc=1)
            S.emit("sync", lambda e, dst_colidx=dst_colidx: e.dma_start(
                out=sc11[0:1, dst_colidx:dst_colidx + 1], in_=t_ccx2[0:1, 0:1]),
                dma="ccs", waits=[S.dmark("ccse")])

        def ffn_phase():
            for which, w_dram, t_raw, acc_col in [("g", d_wg, t_grw, 1), ("u", d_wu, t_upw, 2)]:
                wf_of = dequant_prep(w_dram, NKI, DM, f"f{which}")
                wt_ready = S.mark("vector")
                S.emit("vector", lambda e: e.memset(stmaxf[:, :], 0.0), inc=True)

                def evac_f(m, n, bank, pe_mark, t_raw=t_raw, acc_col=acc_col, which=which):
                    p = (n * NKI + m) % 2
                    wv = [pe_mark]
                    if n * NKI + m >= 2:
                        wv.append(S.dmark("fr"))
                    S.emit("vector", lambda e, bank=bank, p=p: e.tensor_copy(
                        stg[p][:, :], p_mm[bank][:, :]), waits=wv, inc=True)
                    S.emit("vector", lambda e, p=p, m=m, n=n: e.reduce_max(
                        stmaxf[:, (m * NTC + n) % 128:(m * NTC + n) % 128 + 1],
                        stg[p][:, :], AX, apply_absolute_value=True), inc=True)
                    S.emit("sync", lambda e, m=m, n=n, p=p, t_raw=t_raw: e.dma_start(
                        out=t_raw[m * 128:(m + 1) * 128, n * TC:(n + 1) * TC],
                        in_=stg[p][:, :]), dma="fr", waits=[S.mark("vector")])
                mm_phase(t_xq, 8, NKI, wf_of, evac_f, f"f{which}",
                         extra_w=[wt_ready, S.dmark("xqfm")])
                S.emit("vector", lambda e, acc_col=acc_col: e.reduce_max(
                    acc1[:, acc_col:acc_col + 1], stmaxf[:, :], AX),
                    waits=[S.mark("vector")], inc=True)
                S.gap("vector")
                collective_scalar_max(acc_col, 4 + (0 if which == "g" else 1), f"m{which}")
            for src, dst in [(4, (2, 3)), (5, (4, 5))]:
                S.emit("vector", lambda e, src=src: e.tensor_scalar(
                    sc11[0:1, 6:7], sc11[0:1, src:src + 1], 1.0 / QMAX, 1e-12, ALU.mult, ALU.max),
                    waits=[S.dmark("ccs"), S.dmark("qb")], inc=True)
                S.gap("vector")
                S.emit("vector", lambda e: e.reciprocal(sc11[0:1, 7:8], sc11[0:1, 6:7]), inc=True)
                S.gap("vector")
                bcast_scalar(qsc[:, dst[1]:dst[1] + 1], sc11[0:1, 6:7], "qb",
                             waits=[S.mark("vector")])
                bcast_scalar(qsc[:, dst[0]:dst[0] + 1], sc11[0:1, 7:8], "qb")

            with ExitStack() as c6:
                fl = [dA[0].ap()[:, p * TC:(p + 1) * TC] for p in range(2)]
                gq = [escr, stg[1]]
                sg = [stg[0], auxb.ap().bitcast(F32)[:, 0:TC]]
                S.emit("vector", lambda e: e.memset(stmaxf[:, :], 0.0), inc=True)
                nchunks = NKI * NTC
                st_marks = {}
                sig_marks = {}

                def p2_sa(ci):
                    m, n = divmod(ci, NTC)
                    p = ci % 2
                    wv = [S.dmark("qb"), S.dmark("fr")]
                    if ci >= 2:
                        wv += [st_marks[ci - 2], S.dmark("g2w")]
                    S.emit("sync", lambda e, m=m, n=n, p=p: e.dma_start(
                        out=fl[p][:, :], in_=t_grw[m * 128:(m + 1) * 128, n * TC:(n + 1) * TC]),
                        dma=f"g2l{p}", waits=wv)
                    wts = [S.dmark(f"g2l{p}")]
                    if ci >= 2:
                        wts.append(st_marks[ci - 2])
                    S.emit("vector", lambda e, p=p: e.tensor_scalar(
                        aux[:, 0:TC], fl[p][:, :], qsc[:, 2:3], RND, ALU.mult, ALU.add),
                        waits=wts, inc=True)
                    S.emit("vector", lambda e: e.tensor_scalar(
                        aux[:, 0:TC], aux[:, 0:TC], RND, QMAX, ALU.subtract, ALU.min))
                    S.emit("vector", lambda e, p=p: e.tensor_scalar(
                        gq[p][:, :], aux[:, 0:TC], -QMAX - 1.0, qsc[:, 3:4], ALU.max, ALU.mult),
                        inc=True)
                    wsg = [S.mark("vector")]
                    if ci >= 2:
                        wsg.append(st_marks[ci - 2])
                    S.emit("scalar", lambda e, p=p: e.activation(
                        sg[p][:, :], gq[p][:, :], AF.Sigmoid), waits=wsg, inc=True)
                    sig_marks[ci] = S.mark("scalar")

                def p2_sb(ci):
                    m, n = divmod(ci, NTC)
                    p = ci % 2
                    S.emit("vector", lambda e, p=p: e.tensor_scalar(
                        aux[:, TC:2 * TC], sg[p][:, :], 65536.0, RND, ALU.mult, ALU.add),
                        waits=[sig_marks[ci]], inc=True)
                    S.emit("vector", lambda e: e.tensor_scalar(
                        aux[:, TC:2 * TC], aux[:, TC:2 * TC], RND, 65535.0, ALU.subtract, ALU.min))
                    S.emit("vector", lambda e, p=p: e.tensor_scalar(
                        sg[p][:, :], aux[:, TC:2 * TC], 0.0, 1.0 / 65536.0, ALU.max, ALU.mult))
                    S.emit("vector", lambda e, p=p: e.tensor_tensor(
                        fl[p][:, :], gq[p][:, :], sg[p][:, :], ALU.mult), inc=True)
                    st_marks[ci] = S.mark("vector")
                    S.emit("vector", lambda e, p=p, ci=ci: e.reduce_max(
                        stmaxf[:, ci % 128:ci % 128 + 1], fl[p][:, :], AX,
                        apply_absolute_value=True))
                    S.emit("gpsimd", lambda e, m=m, n=n, p=p: e.dma_start(
                        out=t_g2w[m * 128:(m + 1) * 128, n * TC:(n + 1) * TC],
                        in_=fl[p][:, :]), dma="g2w", waits=[S.mark("vector")])

                for ci in range(nchunks):
                    p2_sa(ci)
                    if ci >= 1:
                        p2_sb(ci - 1)
                p2_sb(nchunks - 1)
                S.emit("vector", lambda e: e.reduce_max(acc1[:, 3:4], stmaxf[:, :], AX),
                       waits=[S.mark("vector")], inc=True)
                S.gap("vector")
                collective_scalar_max(3, 6, "m2")
                S.emit("vector", lambda e: e.tensor_scalar(
                    sc11[0:1, 6:7], sc11[0:1, 6:7], 1.0 / QMAX, 1e-12, ALU.mult, ALU.max),
                    waits=[S.dmark("ccs"), S.dmark("qb")], inc=True)
                S.gap("vector")
                S.emit("vector", lambda e: e.reciprocal(sc11[0:1, 7:8], sc11[0:1, 6:7]), inc=True)
                S.gap("vector")
                bcast_scalar(qsc[:, 3:4], sc11[0:1, 6:7], "qb", waits=[S.mark("vector")])
                bcast_scalar(qsc[:, 2:3], sc11[0:1, 7:8], "qb")

                S.emit("vector", lambda e: e.memset(stmaxf[:, :], 0.0),
                       waits=[S.dmark("qb")], inc=True)
                st_marks = {}
                ow_marks = {}
                for ci in range(nchunks):
                    m, n = divmod(ci, NTC)
                    p = ci % 2
                    wv = [S.dmark("g2w"), S.dmark("fr")]
                    if ci >= 2:
                        wv += [st_marks[ci - 2], ow_marks[ci - 2]]
                    S.emit("sync", lambda e, m=m, n=n, p=p: e.dma_start(
                        out=fl[p][:, :], in_=t_g2w[m * 128:(m + 1) * 128, n * TC:(n + 1) * TC]),
                        dma=f"o3a{p}", waits=wv)
                    S.emit("sync", lambda e, m=m, n=n, p=p: e.dma_start(
                        out=gq[p][:, :],
                        in_=t_upw[m * 128:(m + 1) * 128, n * TC:(n + 1) * TC]),
                        dma=f"o3b{p}")
                    upbuf = lambda p: gq[p]
                    S.emit("vector", lambda e, p=p: e.tensor_scalar(
                        aux[:, 0:TC], fl[p][:, :], qsc[:, 2:3], RND, ALU.mult, ALU.add),
                        waits=[S.dmark(f"o3a{p}"), S.dmark(f"o3b{p}")], inc=True)
                    S.emit("vector", lambda e: e.tensor_scalar(
                        aux[:, 0:TC], aux[:, 0:TC], RND, QMAX, ALU.subtract, ALU.min))
                    S.emit("vector", lambda e, p=p: e.tensor_scalar(
                        fl[p][:, :], aux[:, 0:TC], -QMAX - 1.0, qsc[:, 3:4], ALU.max, ALU.mult))
                    S.emit("vector", lambda e, p=p: e.tensor_scalar(
                        aux[:, 0:TC], upbuf(p)[:, :], qsc[:, 4:5], RND, ALU.mult, ALU.add))
                    S.emit("vector", lambda e: e.tensor_scalar(
                        aux[:, 0:TC], aux[:, 0:TC], RND, QMAX, ALU.subtract, ALU.min))
                    S.emit("vector", lambda e, p=p: e.tensor_scalar(
                        upbuf(p)[:, :], aux[:, 0:TC], -QMAX - 1.0, qsc[:, 5:6], ALU.max, ALU.mult))
                    S.emit("vector", lambda e, p=p: e.tensor_tensor(
                        fl[p][:, :], fl[p][:, :], upbuf(p)[:, :], ALU.mult), inc=True)
                    st_marks[ci] = S.mark("vector")
                    S.emit("vector", lambda e, p=p, ci=ci: e.reduce_max(
                        stmaxf[:, ci % 128:ci % 128 + 1], fl[p][:, :], AX,
                        apply_absolute_value=True))
                    S.emit("gpsimd", lambda e, m=m, n=n, p=p: e.dma_start(
                        out=t_ow[m * 128:(m + 1) * 128, n * TC:(n + 1) * TC],
                        in_=fl[p][:, :]), dma="o3w", waits=[S.mark("vector")])
                    ow_marks[ci] = S.dmark("o3w")
                S.emit("vector", lambda e: e.reduce_max(acc1[:, 3:4], stmaxf[:, :], AX),
                       waits=[S.mark("vector")], inc=True)
                S.gap("vector")
                collective_scalar_max(3, 6, "m3")
                S.emit("vector", lambda e: e.tensor_scalar(
                    sc11[0:1, 6:7], sc11[0:1, 6:7], 1.0 / QMAX, 1e-12, ALU.mult, ALU.max),
                    waits=[S.dmark("ccs"), S.dmark("qb")], inc=True)
                S.gap("vector")
                S.emit("vector", lambda e: e.reciprocal(sc11[0:1, 7:8], sc11[0:1, 6:7]), inc=True)
                S.gap("vector")
                bcast_scalar(qsc[:, 7:8], sc11[0:1, 6:7], "qb", waits=[S.mark("vector")])
                bcast_scalar(qsc[:, 6:7], sc11[0:1, 7:8], "qb")

            wd_of = dequant_prep(d_wd, 8, IH, "fd")
            wt_ready = S.mark("vector")
            with ExitStack() as c7:
                oq = oq_alias
                ol = [dA[1].ap()[:, p * TC:(p + 1) * TC] for p in range(2)]
                m2l = [escr, ystg[0]]
                oq_marks = {}
                mm_done = {}
                bank_evac = {}
                for n in range(NTC):
                    for k in range(NKI):
                        p = k % 2
                        wv = [S.dmark("o3w"), S.dmark("qb")]
                        if n * NKI + k >= 2:
                            wv.append(oq_marks[n * NKI + k - 2])
                        if n > 0:
                            wv.append(mm_done.get((n - 1, 7), (None, 0)))
                        S.emit("sync", lambda e, n=n, k=k, p=p: e.dma_start(
                            out=ol[p][:, :], in_=t_ow[k * 128:(k + 1) * 128, n * TC:(n + 1) * TC]),
                            dma=f"odl{p}", waits=[x for x in wv if x[0] is not None])
                        S.emit("vector", lambda e, p=p: e.tensor_scalar(
                            aux[:, 0:TC], ol[p][:, :], qsc[:, 6:7], RND, ALU.mult, ALU.add),
                            waits=[S.dmark(f"odl{p}")], inc=True)
                        S.emit("vector", lambda e: e.tensor_scalar(
                            aux[:, 0:TC], aux[:, 0:TC], RND, QMAX, ALU.subtract, ALU.min))
                        S.emit("vector", lambda e, k=k: e.tensor_scalar(
                            oq[:, k * TC:(k + 1) * TC], aux[:, 0:TC], -QMAX - 1.0, qsc[:, 7:8],
                            ALU.max, ALU.mult), inc=True)
                        oq_marks[n * NKI + k] = S.mark("vector")
                    oqm = S.mark("vector")
                    for m in range(8):
                        bank = m
                        wm = [oqm, wt_ready]
                        prev = bank_evac.get(bank)
                        if prev is not None:
                            wm.append(prev)
                        for k in range(NKI):
                            S.emit("tensor", lambda e, m=m, k=k, bank=bank: e.matmul(
                                p_mm[bank][:, :], wT[:, wd_of(k, m):wd_of(k, m) + 128],
                                oq[:, k * TC:(k + 1) * TC],
                                start=(k == 0), stop=(k == NKI - 1)),
                                waits=wm if k == 0 else [], inc=(k == NKI - 1))
                        mm_done[(n, m)] = S.mark("tensor")
                        p = (n * 8 + m) % 2
                        S.emit("sync", lambda e, m=m, n=n, p=p: e.dma_start(
                            out=m2l[p][:, :], in_=t_m2p[m * 128:(m + 1) * 128, n * TC:(n + 1) * TC]),
                            dma=f"m2l{p}", waits=[S.dmark("op")])
                        wv = [mm_done[(n, m)], S.dmark(f"m2l{p}")]
                        if n * 8 + m >= 2:
                            wv.append(S.dmark("sumw"))
                        S.emit("vector", lambda e, bank=bank, p=p: e.tensor_tensor(
                            stg[p][:, :], p_mm[bank][:, :], m2l[p][:, :], ALU.add),
                            waits=wv, inc=True)
                        bank_evac[bank] = S.mark("vector")
                        S.emit("sync", lambda e, m=m, n=n, p=p: e.dma_start(
                            out=t_bgp[m * 128:(m + 1) * 128, n * TC:(n + 1) * TC],
                            in_=stg[p][:, :]), dma="sumw", waits=[bank_evac[bank]])
                    if n % 2 == 1:
                        q = n // 2
                        sumq = S.dmark("sumw")
                        dyn_slot("sync", f"fi{q}",
                                 lambda bv, q=q: t_ccbi_q[q][bass.ts(bv, DM), :],
                                 lambda bv, q=q: t_bgp[:, q * LQ:(q + 1) * LQ],
                                 waits=[sumq, zf_done])
                        S.emit("gpsimd", lambda e, q=q: e.collective_compute(
                            "AllReduce", ALU.add, replica_groups=ALL8,
                            ins=[t_ccbi_q[q].opt()], outs=[t_ccbo_q[q].opt()]),
                            waits=[S.dmark(f"fi{q}")], dma=f"cf{q}", dmainc=1)
            bfo_m = []
            for q in range(4):
                dyn_slot("gpsimd", f"fo{q}",
                         lambda bv, q=q: t_bgf[:, q * LQ:(q + 1) * LQ],
                         lambda bv, q=q: t_ccbo_q[q][bass.ts(bv, DM), :],
                         waits=[S.dmark(f"cf{q}")])
                bfo_m.append(S.dmark(f"fo{q}"))
            return bfo_m

        def final_phase(bfo_m):
            fstart = S.mark("vector")
            with ExitStack() as c8:
                sl = [aux[:, i * 128:(i + 1) * 128] for i in range(4)]
                xl = [rhsb[p].ap().bitcast(F32)[:, 0:DM] for p in range(2)]
                ob = [dA[p].ap()[:, 0:DM] for p in range(2)]
                ntok = L // 128
                add_marks = {}
                pe_tr = {}
                for tt in range(ntok):
                    p = tt % 2
                    half_m = bfo_m[tt // (ntok // 4)]
                    wv = [half_m]
                    if tt >= 2:
                        wv.append(add_marks[tt - 2])
                    S.emit("sync", lambda e, tt=tt, p=p: e.dma_start(
                        out=xl[p][:, :], in_=d_x[tt * 128:(tt + 1) * 128, :]), dma=f"xf{p}", waits=wv)
                    for c in range(8):
                        psl = sl[c % 4]
                        wv2 = [half_m]
                        if (tt * 8 + c) >= 4:
                            wv2.append(pe_tr[tt * 8 + c - 4])
                        S.emit("sync" if c % 4 < 2 else "gpsimd",
                               lambda e, tt=tt, c=c, psl=psl: e.dma_start(
                                   out=psl[:, :],
                                   in_=t_bgf[c * 128:(c + 1) * 128, tt * 128:(tt + 1) * 128]),
                               dma=f"sf{c % 4}", waits=wv2)
                        ptp = p_tr[c % 2]
                        wv3 = [S.dmark(f"sf{c % 4}"), fstart]
                        if (tt * 8 + c) >= 2:
                            wv3.append(add_marks.get(tt - 2, (None, 0)))
                        S.emit("tensor", lambda e, psl=psl, c=c: e.transpose(
                            p_mm[c % 4][:, 0:128], psl[:, :], eye_f[:, :]),
                            waits=[x for x in wv3 if x[0] is not None], inc=True)
                        pe_tr[tt * 8 + c] = S.mark("tensor")
                        S.emit("vector", lambda e, c=c, p=p: e.tensor_tensor(
                            ob[p][:, c * 128:(c + 1) * 128], p_mm[c % 4][0:128, 0:128],
                            xl[p][:, c * 128:(c + 1) * 128], ALU.add),
                            waits=[pe_tr[tt * 8 + c], S.dmark(f"xf{p}")], inc=True)
                    add_marks[tt] = S.mark("vector")
                    S.emit("sync", lambda e, tt=tt, p=p: e.dma_start(
                        out=d_out[tt * 128:(tt + 1) * 128, :], in_=ob[p][:, :]),
                        dma="outw", waits=[add_marks[tt]])
                S.emit("sync", lambda e: e.dma_start(out=sc11[0:1, 9:10], in_=t_ccx2[0:1, 0:1]),
                       dma="fin", waits=[S.dmark("outw")])

        if STAGE >= 1:
            pass
        if STAGE >= 2:
            l0_marks = mamba_layer(0, t_xn,
                                   rhs_w_of=lambda n: [xn_tile[4 * n + 3]],
                                   mid_emit=emit_quant)
        if STAGE >= 4:
            mamba_layer(1, t_l2in,
                        rhs_w_of=lambda n, mk=l0_marks: [mk[n // 2]])
        if STAGE >= 5:
            ff_marks = ffn_phase()
        if STAGE >= 6:
            final_phase(ff_marks)
        for tag, (dap, src, shp, dt) in dbg_outs.items():
            name = f"dbg{tag}"
            wv = [S.mark(e) for e in ENGINES] + [S.dmark(n) for n in list(S.dsems)]
            first = True
            for r in range(0, shp[0], 128):
                rows = min(128, shp[0] - r)
                S.emit("sync", lambda e, r=r, rows=rows, dap=dap, src=src: e.dma_start(
                    out=dap[r:r + rows, :], in_=src[r:r + rows, :]),
                    dma=name, waits=wv if first else [])
                first = False
            S.emit("sync", lambda e: e.dma_start(out=sc11[0:1, 10:11], in_=t_ccx2[0:1, 0:1]),
                   dma="fin2", waits=[S.dmark(name)])
        if STAGE < 6:
            wv = [S.mark(e) for e in ENGINES] + [S.dmark("xntok")]
            for tt in range(L // 128):
                S.emit("sync", lambda e, tt=tt: e.dma_start(
                    out=d_out[tt * 128:(tt + 1) * 128, :],
                    in_=t_xn_tok[tt * 128:(tt + 1) * 128, :]),
                    dma="outw", waits=wv if tt == 0 else [])
            S.emit("sync", lambda e: e.dma_start(out=sc11[0:1, 9:10], in_=t_ccx2[0:1, 0:1]),
                   dma="fin", waits=[S.dmark("outw")])

        blk = ctx.enter_context(nc.Block())
        S.finalize(blk)
    return nc, dbg_outs


def _slice_inputs(inputs, core):
    b, j = divmod(core, 2)
    f32 = np.float32

    def resh(v):
        v = np.ascontiguousarray(v)
        if v.ndim == 1:
            return np.ascontiguousarray(v.reshape(8, 128).T)
        k = v.shape[1]
        return np.ascontiguousarray(v.reshape(8, 128, k).transpose(1, 0, 2).reshape(128, 8 * k))

    m = {}
    x = np.asarray(inputs["L0_attn_x"], f32)
    m["x_tok"] = np.ascontiguousarray(x[b])
    for i in range(2):
        inw = np.asarray(inputs["L0_attn_in_proj_w"], f32)[i]
        m[f"inw{i}"] = np.ascontiguousarray(
            np.concatenate([inw[j * DH:(j + 1) * DH], inw[2 * DH + j * DH:2 * DH + (j + 1) * DH]], 0))
        m[f"convw{i}"] = resh(np.asarray(inputs["L0_attn_conv_w"], f32)[i][j * DH:(j + 1) * DH])
        m[f"convb{i}"] = resh(np.asarray(inputs["L0_attn_conv_b"], f32)[i][j * DH:(j + 1) * DH])
        m[f"xpw{i}"] = np.ascontiguousarray(
            np.asarray(inputs["L0_attn_x_proj_w"], f32)[i][:, j * DH:(j + 1) * DH])
        m[f"dtw{i}"] = resh(np.asarray(inputs["L0_attn_dt_proj_w"], f32)[i][j * DH:(j + 1) * DH])
        m[f"dtb{i}"] = resh(np.asarray(inputs["L0_attn_dt_proj_b"], f32)[i][j * DH:(j + 1) * DH])
        m[f"alog{i}"] = resh(np.asarray(inputs["L0_attn_A_log"], f32)[i][j * DH:(j + 1) * DH])
        m[f"dskip{i}"] = resh(np.asarray(inputs["L0_attn_D_skip"], f32)[i][j * DH:(j + 1) * DH])
        m[f"outw{i}"] = np.ascontiguousarray(
            np.asarray(inputs["L0_attn_out_proj_w"], f32)[i][:, j * DH:(j + 1) * DH])
    m["wg"] = np.ascontiguousarray(np.asarray(inputs["L0_ffn_w_gate"], f32)[j * IH:(j + 1) * IH])
    m["wu"] = np.ascontiguousarray(np.asarray(inputs["L0_ffn_w_up"], f32)[j * IH:(j + 1) * IH])
    m["wd"] = np.ascontiguousarray(np.asarray(inputs["L0_ffn_w_down"], f32)[:, j * IH:(j + 1) * IH])
    m["eye_b_in"] = np.eye(128, dtype=ml_dtypes.bfloat16)
    m["eye_f_in"] = np.eye(128, dtype=np.float32)
    return m


def kernel(**inputs):
    nc, _ = build_program(dbg=DBG)
    in_maps = [_slice_inputs(inputs, c) for c in range(8)]
    res = run_bass_kernel_spmd(nc, in_maps, core_ids=list(range(8)))
    out = np.stack([np.asarray(res.results[2 * b]["out"], np.float32) for b in range(4)], 0)
    return out
